# version 06
# baseline (speedup 1.0000x reference)
import numpy as np
import concourse.bass as bass
import concourse.mybir as mybir
from concourse.bass_utils import run_bass_kernel_spmd
from contextlib import ExitStack

F32 = mybir.dt.float32
BF16 = mybir.dt.bfloat16
AF = mybir.ActivationFunctionType
ALU = mybir.AluOpType
AX = mybir.AxisListType

S = 8192
D = 1024
T = 512
NT = S // T
C = 8
H = 16
NKB = S // 128
NSL = 9
RCONST = 1.5 * 2.0 ** 23
EXPB = -3.0

PHASES = 4
P3RET = True
P3SWC = True
P4A = 3
P4B = True
P4C = True
SKIP123 = False


class Prog:
    ENG = ("T", "V", "S", "Y", "G")
    NPOOL = 12

    def __init__(self, nc, es):
        self.nc = nc
        self.ops = {e: [] for e in self.ENG}
        self.semkeys = ["T", "V", "S"] + \
            [f"Y{i}" for i in range(self.NPOOL)] + \
            [f"G{i}" for i in range(self.NPOOL)]
        self.sem = {k: es.enter_context(nc.semaphore(f"sem{k}"))
                    for k in self.semkeys}
        self.cnt = {k: 0 for k in self.semkeys}
        self.dma_idx = {"Y": 0, "G": 0}
        self.res = {}
        self.waited = {e: {k: 0 for k in self.semkeys} for e in self.ENG}
        self.icnt = {e: 0 for e in self.ENG}
        self.drained = {e: 0 for e in self.ENG}

    def _add_waits(self, eng, reads, writes, need):
        def add(k2, c2):
            if k2 == eng:
                return
            need[k2] = max(need.get(k2, 0), c2)

        for r in reads:
            st = self.res.get(r)
            if st and st["w"]:
                add(*st["w"])
        for w in writes:
            st = self.res.get(w)
            if st:
                if st["w"]:
                    add(*st["w"])
                for (k2, c2) in st["r"]:
                    add(k2, c2)

    def op(self, eng, fn, reads=(), writes=(), dma=False, small=()):
        reads = list(reads)
        writes = list(writes)
        need = {}
        self._add_waits(eng, reads, writes, need)
        if dma:
            semkey = f"{eng}{self.dma_idx[eng] % self.NPOOL}"
            self.dma_idx[eng] += 1
            if self.cnt[semkey] > 0:
                need[semkey] = max(need.get(semkey, 0), self.cnt[semkey])
            inc = 16
        else:
            semkey = eng
            inc = 1
        waits = []
        for k2, c2 in sorted(need.items()):
            if self.waited[eng][k2] < c2:
                self.waited[eng][k2] = c2
                waits.append((self.sem[k2], c2))
        drain = False
        for r in small:
            st = self.res.get(r)
            if st and st["w"] and st["w"][0] == eng:
                wc = st["w"][1]
                if self.icnt[eng] - wc < 12 and wc > self.drained[eng]:
                    drain = True
        if drain:
            self.drained[eng] = self.icnt[eng]
        self.cnt[semkey] += inc
        newc = self.cnt[semkey]
        if not dma:
            self.icnt[eng] = newc
        sem = self.sem[semkey]

        def emit(sec, waits=waits, fn=fn, drain=drain, sem=sem, inc=inc):
            for s_, v_ in waits:
                sec.wait_ge(s_, v_)
            if drain:
                sec.drain()
            fn(sec).then_inc(sem, inc)

        self.ops[eng].append(emit)
        for w in writes:
            self.res[w] = {"w": (semkey, newc), "r": []}
        for r in reads:
            st = self.res.setdefault(r, {"w": None, "r": []})
            st["r"].append((semkey, newc))
        return newc

    def finish(self):
        for k in self.semkeys:
            v = self.cnt[k]
            if v:
                def emit(sec, s_=self.sem[k], v_=v):
                    sec.wait_ge(s_, v_)
                self.ops["Y"].append(emit)


RACECHECK = False


def build():
    nc = bass.Bass("TRN2", target_bir_lowering=False,
                   detect_race_conditions=RACECHECK)

    def din(name, shape, dt=F32):
        return nc.dram_tensor(name, shape, dt, kind="ExternalInput")

    xfm = din("xfm", [D, S])
    l0w = {k: din(f"l0w{k}", [D, D]) for k in "qkvo"}
    l0b = {k: din(f"l0b{k}", [D]) for k in "qkvo"}
    g1r = din("g1r", [4, 1024]); g2r = din("g2r", [4, 256])
    g1i = din("g1i", [4, 1024]); g2i = din("g2i", [4, 256])
    l0fb = din("l0fb", [D])
    l1w = {k: din(f"l1w{k}", [D, D]) for k in "qvgo"}
    l1lng = din("l1lng", [D]); l1lnb = din("l1lnb", [D])
    l1decay = din("l1decay", [16])
    l1rhsT = din("l1rhsT", [D, D])
    identb = din("identb", [128, 128], BF16)
    identf = din("identf", [128, 128])
    oh16 = din("oh16", [128, 16 * 16], BF16)
    ones16 = din("ones16", [16, 128], BF16)
    ones16f = din("ones16f", [16, 128])
    masks = din("masks", [128, 3 * 384], BF16)
    dtabs = din("dtabs", [128, 384])

    ofm = nc.dram_tensor("ofm", [S, D], F32, kind="ExternalOutput")

    def scratch(name, k):
        kind = "ExternalOutput" if PHASES == k else "Internal"
        return nc.dram_tensor(name, [D, S], F32, kind=kind)

    x1d = scratch("x1d", 1)
    x2d = scratch("x2d", 2)
    x3d = scratch("x3d", 3)
    yscr = nc.dram_tensor("yscr", [D, S], BF16)
    gscr = nc.dram_tensor("gscr", [D, S], BF16)
    pscr = nc.dram_tensor("pscr", [2 * 1024 * 256], F32)

    es = ExitStack()
    sbt = lambda n, sh, dt: es.enter_context(nc.sbuf_tensor(n, sh, dt))
    with es:
        WA = sbt("WA", [128, 4 * 8192], BF16)
        WB = sbt("WB", [128, 8192], BF16)
        KR = sbt("KR", [128, NSL * 1024], BF16)
        VT = sbt("VT", [128, NSL * (H * 65)], BF16)
        XF = sbt("XF", [128, 2048], F32)
        XBF = sbt("XBF", [128, 4096], BF16)
        XS = sbt("XS", [128, 4096], BF16)
        QR0 = sbt("QR0", [128, 4096], BF16)
        AO = sbt("AO", [128, 4096], BF16)
        AOtm = sbt("AOtm", [128, 1024], BF16)
        UF = sbt("UF", [128, 1024], F32)
        OX = sbt("OX", [128, 1024], F32)
        ST0 = sbt("ST0", [128, 1024], F32)
        ST1 = sbt("ST1", [128, 1024], F32)
        LMAT = sbt("LMAT", [128, H * 128], BF16)
        PWR = sbt("PWR", [128, 1024], F32)
        WMT = sbt("WMT", [128, 384], BF16)
        WMT2 = sbt("WMT2", [128, 384], BF16)
        MSK = sbt("MSK", [128, 3 * 384], BF16)
        IDB = sbt("IDB", [128, 128], BF16)
        IDF = sbt("IDF", [128, 128], F32)
        OH = sbt("OH", [128, 256], BF16)
        ON16 = sbt("ON16", [16, 128], BF16)
        ON16F = sbt("ON16F", [16, 128], F32)
        ONESC = sbt("ONESC", [128, 1], BF16)
        NCS = sbt("NCS", [1, 3 * 1024], BF16)
        ROWS = sbt("ROWS", [16, T], BF16)
        RXT = sbt("RXT", [1, 2 * T], BF16)
        RXF = sbt("RXF", [1, 128], F32)
        CST = sbt("CST", [128, 4], F32)
        ROWSF = sbt("ROWSF", [16, 2 * T], F32)
        COLS = sbt("COLS", [128, 128], F32)
        BIA = sbt("BIA", [128, 64], F32)
        RST = sbt("RST", [128, 16], F32)
        RSI = sbt("RSI", [128, 16], F32)
        SMC = sbt("SMC", [128, 8], F32)
        SMC2 = sbt("SMC2", [128, 8], F32)
        LNC = sbt("LNC", [128, 16], F32)
        PRC = sbt("PRC", [128, 8], F32)
        DEC = sbt("DEC", [16, 2], F32)
        LNR = sbt("LNR", [1, 16], F32)
        RSB = sbt("RSB", [128, 1024], F32)

        PQ = es.enter_context(nc.psum_tensor("PQ", [128, 1024], F32))
        SC = es.enter_context(nc.psum_tensor("SCP", [128, 1536], F32))
        AVQ = es.enter_context(nc.psum_tensor("AVQ", [128, 1024], F32))
        PT = es.enter_context(nc.psum_tensor("PTP", [128, 1024], BF16))

        p = Prog(nc, es)
        ident = IDB[:, :]

        def dma(q, dst, src, reads, writes, slow=False):
            p.op(q, lambda s, d=dst, r=src, sl=slow: s.dma_start(
                d, r, allow_slow_non_contiguous=sl),
                 reads=reads, writes=writes, dma=True)

        def mm(out, lh, rh, start, stop, reads, writes):
            p.op("T", lambda s, o=out, l=lh, r=rh, a=start, b=stop:
                 s.matmul(o, l, r, start=a, stop=b, skip_group_check=True),
                 reads=reads, writes=writes)

        def tpose(out, in_, reads, writes, f32=False):
            pc = in_.ap[0][1]
            idap = (IDF if f32 else IDB)[0:pc, 0:pc]
            p.op("T", lambda s, o=out, i=in_, d=idap: s.transpose(o, i, d),
                 reads=list(reads) + ["IDB"], writes=writes)

        vop = lambda fn, reads=(), writes=(), small=(): p.op("V", fn, reads, writes, small=small)
        sop = lambda fn, reads=(), writes=(), small=(): p.op("S", fn, reads, writes, small=small)

        PA = PQ[:, 0:512]
        PB = PQ[:, 512:1024]

        dma("Y", IDB[:, :], identb.ap(), [], ["IDB"])
        dma("Y", IDF[:, :], identf.ap(), [], ["IDB"])
        dma("Y", MSK[:, :], masks.ap(), [], ["MSK"])
        dma("Y", OH[:, :], oh16.ap(), [], ["OH"])
        dma("Y", ON16[:, :], ones16.ap(), [], ["ON16"])
        dma("Y", ON16F[:, :], ones16f.ap(), [], ["ON16F"])
        dma("Y", ST0[:, 0:384], dtabs.ap(), [], ["DTB"])
        for j, t in enumerate([l0b["q"], l0b["k"], l0b["v"], l0b["o"], l0fb,
                               l1lng, l1lnb]):
            dma("Y", BIA[:, 8 * j:8 * j + 8],
                bass.AP(t, 0, [[1, 128], [128, 8]]), ["BIA"], ["BIA"],
                slow=True)
        vop(lambda s: s.tensor_scalar(BIA[:, 0:8], BIA[:, 0:8], 0.125, None,
                                      ALU.mult), ["BIA"], ["BIA"])
        vop(lambda s: s.memset(ONESC[:, :], 1.0), [], ["ONESC"])
        vop(lambda s: s.memset(CST[:, 0:1], EXPB), ["CST"], ["CST"])
        vop(lambda s: s.memset(CST[:, 1:2], 128.5), ["CST"], ["CST"])
        vop(lambda s: s.memset(KR[:, :], 0.0), [], ["KR"])
        vop(lambda s: s.memset(
            bass.AP(VT, 0, [[NSL * H * 65, 128], [65, NSL * H], [1, 64]]),
            0.0), [], ["VT"])
        vop(lambda s: s.memset(
            bass.AP(VT, 64, [[NSL * H * 65, 128], [65, NSL * H]]), 1.0),
            ["VT"], ["VT"])

        dma("Y", DEC[:, 0:1], bass.AP(l1decay, 0, [[1, 16], [16, 1]]),
            ["DEC"], ["DEC"], slow=True)
        vop(lambda s: s.tensor_scalar(DEC[:, 0:1], DEC[:, 0:1], 1e-30, None,
                                      ALU.max), ["DEC"], ["DEC"])
        sop(lambda s: s.activation(DEC[:, 1:2], DEC[:, 0:1], AF.Ln),
            ["DEC"], ["DECL"])
        tpose(PQ[0:1, 0:16], DEC[:, 1:2], ["DECL"], ["pa"], f32=True)
        vop(lambda s: s.tensor_copy(LNR[:, :], PQ[0:1, 0:16]), ["pa"], ["LNR"])
        for h in range(H):
            mm(PQ[:, 0:1], ON16F[0:1, :], LNR[:, h:h + 1], True, True,
               ["ON16F", "LNR"], ["pa"])
            vop(lambda s, h=h: s.tensor_copy(LNC[:, h:h + 1], PQ[:, 0:1]),
                ["pa", "LNC"], ["LNC"])
        for h in range(H):
            vop(lambda s, h=h: s.scalar_tensor_tensor(
                OX[:, 0:128], ST0[:, 0:128], LNC[:, h:h + 1],
                ST0[:, 128:256], ALU.mult, ALU.add),
                ["DTB", "LNC", "OX0"], ["OX0"], small=["LNC"])
            sop(lambda s, h=h: s.activation(LMAT[:, 128 * h:128 * h + 128],
                                            OX[:, 0:128], AF.Exp),
                ["OX0"], ["LMAT"])
        for c4 in range(8):
            vop(lambda s, c=c4: s.tensor_copy(PRC[0:64, c:c + 1],
                                              LNC[0:64, 2 * c:2 * c + 1]),
                ["LNC", "PRC"], ["PRC"])
            vop(lambda s, c=c4: s.tensor_copy(PRC[64:128, c:c + 1],
                                              LNC[64:128, 2 * c + 1:2 * c + 2]),
                ["LNC", "PRC"], ["PRC"])
            sop(lambda s, c=c4: s.activation(PWR[:, 128 * c:128 * c + 128],
                                             ST0[:, 256:384], AF.Exp,
                                             scale=PRC[:, c:c + 1]),
                ["DTB", "PRC"], ["PWR"], small=["PRC"])
        vop(lambda s: s.memset(ST0[:, :], 0.0), [], ["ST0", "DTB"])
        vop(lambda s: s.memset(ST1[:, :], 0.0), [], ["ST1"])

        for part, (gg1, gg2) in enumerate([(g1r, g2r), (g1i, g2i)]):
            dma("Y", UF[0:4, 0:1024], gg1.ap(), ["UF0"], ["UF0"])
            dma("Y", OX[0:4, 0:256], gg2.ap(), ["OX0"], ["OX0"])
            for mb in range(8):
                mm(PQ[:, 0:256], UF[0:4, 128 * mb:128 * mb + 128],
                   OX[0:4, 0:256], True, True, ["UF0", "OX0"], ["pa"])
                vop(lambda s: s.tensor_copy(
                    OX[:, 512:768],
                    bass.AP(PQ, 0, [[1024, 128], [1, 16], [16, 16]])),
                    ["pa", "OX1"], ["OX1"])
                dma("Y", bass.AP(pscr, (part * 1024 + 128 * mb) * 256,
                                 [[256, 128], [1, 256]]), OX[:, 512:768],
                    ["OX1"], [f"pscr{part}_{mb}"])
        pdeps = [f"pscr{i}_{m}" for i in range(2) for m in range(8)]

        def wf_reorder(dst_col_base):
            for q4 in range(4):
                for dl in range(8):
                    src_r = bass.AP(pscr, (8 * q4 + dl) * 256,
                                    [[16, 16], [8192, 32], [1, 16]])
                    src_i = bass.AP(pscr, 1024 * 256 + (8 * q4 + dl) * 256,
                                    [[16, 16], [8192, 32], [1, 16]])
                    rows = (16 * dl, 16 * dl + 16)
                    dma("G", WB[rows[0]:rows[1], 1024 * q4:1024 * q4 + 512],
                        src_r, pdeps + ["WB"], ["WB"], slow=True)
                    dma("G", WB[rows[0]:rows[1],
                               1024 * q4 + 512:1024 * q4 + 1024],
                        src_i, pdeps + ["WB"], ["WB"], slow=True)
                    dma("G", WB[rows[0]:rows[1],
                               1024 * (4 + q4) + 512:1024 * (4 + q4) + 1024],
                        src_r, pdeps + ["WB"], ["WB"], slow=True)
                    dma("G", WB[rows[0]:rows[1],
                               1024 * (4 + q4):1024 * (4 + q4) + 512],
                        src_i, pdeps + ["WB"], ["WB"], slow=True)
            for q4 in range(4):
                vop(lambda s, q=q4: s.tensor_scalar(
                    WB[:, 1024 * (4 + q):1024 * (4 + q) + 512],
                    WB[:, 1024 * (4 + q):1024 * (4 + q) + 512], -1.0, None,
                    ALU.mult), ["WB"], ["WB"])
        wf_reorder(0)

        def load_weights(tensors):
            for sl, t in enumerate(tensors):
                dma("G", WA[:, 8192 * sl:8192 * sl + 8192],
                    bass.AP(t, 0, [[1024, 128], [128 * 1024, 8], [1, 1024]]),
                    ["WA"], ["WA"])

        def build_negcs(slots):
            bk = {id(WA): "WA", id(WB): "WB"}
            for j, (buf, base) in enumerate(slots):
                for half in range(2):
                    for dc in range(C):
                        mm(PQ[0:1, 0:512], ONESC[:, :],
                           buf[:, base + 1024 * dc + 512 * half:
                               base + 1024 * dc + 512 * half + 512],
                           dc == 0, dc == C - 1, ["ONESC", bk[id(buf)]], ["pa"])
                    vop(lambda s, j=j, half=half: s.tensor_copy(
                        NCS[0:1, 1024 * j + 512 * half:
                            1024 * j + 512 * half + 512],
                        PQ[0:1, 0:512]), ["pa", "NCS"], ["NCS"])

        def sweep_stats(xsrc, tag=None):
            for i in range(NT):
                rd = ["XBF"] + ([f"{tag}@{i}"] if tag else [])
                dma("G", XBF[:, :],
                    bass.AP(xsrc, i * T, [[S, 128], [128 * S, 8], [1, T]]),
                    rd, ["XBF"])
                vop(lambda s: s.tensor_tensor(XS[:, :], XBF[:, :], XBF[:, :],
                                              ALU.mult), ["XBF", "XS"], ["XS"])
                for dc in range(C):
                    mm(PQ[0:16, 0:T], OH[:, 16 * i:16 * i + 16],
                       XBF[:, T * dc:T * dc + T], i == 0 and dc == 0,
                       i == NT - 1 and dc == C - 1, ["XBF", "OH"], ["pa"])
                for dc in range(C):
                    mm(PQ[0:16, 512:1024], OH[:, 16 * i:16 * i + 16],
                       XS[:, T * dc:T * dc + T], i == 0 and dc == 0,
                       i == NT - 1 and dc == C - 1, ["XS", "OH"], ["pb"])
            rowmath("pa", PQ[0:16, 0:T], "pb", PQ[0:16, 512:1024])

        def rowmath(ka, s_reg, kb, ss_reg):
            c1 = 1.0 / D
            vop(lambda s: s.tensor_scalar(ROWSF[:, 0:T], s_reg, -c1, None,
                                          ALU.mult), [ka, "ROWSF"], ["ROWSF"])
            vop(lambda s: s.scalar_tensor_tensor(
                OX[0:16, 0:512], ROWSF[:, 0:T], -1.0, ROWSF[:, 0:T],
                ALU.mult, ALU.mult), ["ROWSF", "OX0"], ["OX0"])
            vop(lambda s: s.tensor_scalar(OX[0:16, 0:512], OX[0:16, 0:512],
                                          1e-5, None, ALU.add), ["OX0"],
                ["OX0"])
            vop(lambda s: s.scalar_tensor_tensor(
                OX[0:16, 512:1024], ss_reg, c1, OX[0:16, 0:512],
                ALU.mult, ALU.add), [kb, "OX0", "OX1"], ["OX1"])
            vop(lambda s: s.reciprocal(OX[0:16, 0:512], OX[0:16, 512:1024]),
                ["OX1", "OX0"], ["OX0"])
            sop(lambda s: s.activation(ROWSF[:, T:2 * T], OX[0:16, 0:512],
                                       AF.Sqrt), ["OX0", "ROWSF"], ["ROWSF"])
            vop(lambda s: s.tensor_tensor(ROWS[:, 0:T], ROWSF[:, 0:T],
                                          ROWSF[:, T:2 * T], ALU.mult),
                ["ROWSF", "ROWS"], ["ROWS"])


        def extract_row(i, pack_off, dst_off):
            mm(SC[0:1, 0:T], IDB[0:16, i:i + 1],
               ROWS[0:16, pack_off:pack_off + T], True, True,
               ["IDB", "ROWS", "sc0"], ["sc0"])
            vop(lambda s, d=dst_off: s.tensor_copy(
                RXT[0:1, d:d + T], SC[0:1, 0:T]), ["sc0", "RXT"], ["RXT"])

        def extract_row_f(i, packf_off, dst_off):
            mm(SC[0:1, 0:T], IDF[0:16, i:i + 1],
               ROWSF[0:16, packf_off:packf_off + T], True, True,
               ["IDB", "ROWSF", "sc0"], ["sc0"])
            vop(lambda s, d=dst_off: s.tensor_copy(
                RXT[0:1, d:d + T], SC[0:1, 0:T]), ["sc0", "RXT"], ["RXT"])

        def bcast_row(dst_reg, dst_key, src_off):
            mm(dst_reg, ON16[0:1, :], RXT[0:1, src_off:src_off + T],
               True, True, ["ON16", "RXT", dst_key], [dst_key])

        def make_xs(i, xsrc, tag=None):
            rd = ["XBF"] + ([f"{tag}@{i}"] if tag else [])
            dma("G", XBF[:, :],
                bass.AP(xsrc, i * T, [[S, 128], [128 * S, 8], [1, T]]),
                rd, ["XBF"])
            extract_row_f(i, T, T)
            extract_row(i, 0, 0)
            bcast_row(SC[:, 0:T], "sc0", T)
            for dc in range(C):
                vop(lambda s, dc=dc: s.tensor_tensor(
                    XS[:, T * dc:T * dc + T], XBF[:, T * dc:T * dc + T],
                    SC[:, 0:T], ALU.mult), ["XBF", "sc0", "XS"], ["XS"])

        def proj_ln(i, wbase, ncs_j, evac):
            for oc in range(C):
                breg, bkey = (PA, "pa") if oc % 2 == 0 else (PB, "pb")
                for dc in range(C):
                    mm(breg, WA[:, wbase + 1024 * dc + 128 * oc:
                               wbase + 1024 * dc + 128 * oc + 128],
                       XS[:, T * dc:T * dc + T],
                       dc == 0, False, ["WA", "XS"], [bkey])
                mm(breg, NCS[0:1, 1024 * ncs_j + 128 * oc:
                             1024 * ncs_j + 128 * oc + 128],
                   RXT[0:1, 0:T], False, True, ["NCS", "RXT"], [bkey])
                evac(oc, breg, bkey)

        def proj_plain(wbase, rhs_buf, rhs_key, evac):
            for oc in range(C):
                breg, bkey = (PA, "pa") if oc % 2 == 0 else (PB, "pb")
                for dc in range(C):
                    mm(breg, WA[:, wbase + 1024 * dc + 128 * oc:
                               wbase + 1024 * dc + 128 * oc + 128],
                       rhs_buf[:, T * dc:T * dc + T],
                       dc == 0, dc == C - 1, ["WA", rhs_key], [bkey])
                evac(oc, breg, bkey)

        def evac_resid(xsrc_res, dst_dram, bias_slot, dst_tag, src_tag=None):
            def f_for(i):
                def f(oc, breg, bkey):
                    sl = oc % 2
                    xf = XF[:, 1024 * sl:1024 * sl + 512]
                    xkey = f"XFH{sl}"
                    rdx = [xkey] + ([f"{src_tag}@{i}"] if src_tag else [])
                    dma("Y", xf,
                        bass.AP(xsrc_res, oc * 128 * S + i * T,
                                [[S, 128], [1, T]]), rdx, [xkey])
                    ox = OX[:, 512 * sl:512 * sl + 512]
                    okey = f"OX{sl}"
                    if bias_slot is not None:
                        vop(lambda s, b=breg, x=xf, o=ox, oc=oc:
                            s.scalar_tensor_tensor(
                                o, b, BIA[:, 8 * bias_slot + oc:
                                          8 * bias_slot + oc + 1], x,
                                ALU.add, ALU.add),
                            [bkey, xkey, "BIA", okey], [okey])
                    else:
                        vop(lambda s, b=breg, x=xf, o=ox:
                            s.tensor_tensor(o, b, x, ALU.add),
                            [bkey, xkey, okey], [okey])
                    dma("Y", bass.AP(dst_dram, oc * 128 * S + i * T,
                                     [[S, 128], [1, T]]), ox, [okey],
                        [okey, f"{dst_tag}@{i}"])
                return f
            return f_for

        if not SKIP123:
            load_weights([l0w["q"], l0w["k"], l0w["v"], l0w["o"]])
            build_negcs([(WA, 0), (WA, 8192), (WA, 16384)])
            sweep_stats(xfm)

        def evac_q(oc, breg, bkey):
            sop(lambda s, oc=oc, b=breg: s.activation(
                QR0[:, 512 * oc:512 * oc + 512], b, AF.Identity,
                bias=BIA[:, oc:oc + 1], scale=0.125),
                [bkey, "BIA", "QR0"], ["QR0"])

        def evac_k(i):
            def f(oc, breg, bkey):
                slots = [(4 * i + m) % NSL for m in range(4)]
                runs = [[0]]
                for m in range(1, 4):
                    if slots[m] == slots[m - 1] + 1:
                        runs[-1].append(m)
                    else:
                        runs.append([m])
                for run in runs:
                    m0, ln = run[0], len(run)
                    dst = bass.AP(KR, (slots[m0] * 8 + oc) * 128,
                                  [[NSL * 1024, 128], [1024, ln], [1, 128]])
                    src = bass.AP(PQ, (0 if bkey == "pa" else 512) + m0 * 128,
                                  [[1024, 128], [128, ln], [1, 128]])
                    sop(lambda s, d=dst, r=src, oc=oc: s.activation(
                        d, r, AF.Identity, bias=BIA[:, 8 + oc:9 + oc]),
                        [bkey, "BIA", "KR"], ["KR"])
            return f

        def evac_v(oc, breg, bkey):
            sop(lambda s, oc=oc, b=breg: s.activation(
                XBF[:, 512 * oc:512 * oc + 512], b, AF.Identity,
                bias=BIA[:, 16 + oc:17 + oc]), [bkey, "BIA", "XBF"], ["XBF"])

        def vt_build(i, src_buf, src_key, nsl=NSL):
            for m in range(4):
                sl = (4 * i + m) % nsl
                for dc in range(C):
                    tpose(PT[:, 128 * dc:128 * dc + 128],
                          src_buf[:, 512 * dc + 128 * m:
                                  512 * dc + 128 * m + 128],
                          [src_key], ["pt"] if dc == C - 1 else ["pt_"])
                dst = bass.AP(VT, sl * (H * 65),
                              [[NSL * H * 65, 128], [65, H], [1, 64]])
                vop(lambda s, d=dst: s.tensor_copy(
                    d, bass.AP(PT, 0, [[1024, 128], [64, 16], [1, 64]])),
                    ["pt", "pt_", "VT"], ["VT"])

        def attention(j):
            for ql in range(4):
                qb = 4 * j + ql
                variant = 0 if qb == 0 else (2 if qb == NKB - 1 else 1)
                if qb == 0:
                    vop(lambda s: s.memset(SC[:, 0:128], 0.0),
                        ["sc0"], ["sc0"])
                if qb == NKB - 1:
                    vop(lambda s: s.memset(SC[:, 1024:1152], 0.0),
                        ["sc2"], ["sc2"])
                for h in range(H):
                    po = (h % 2) * 64
                    hc = h // 2
                    avreg, avkey = ((AVQ[:, 0:512], "av0") if h % 2 == 0
                                    else (AVQ[:, 512:1024], "av1"))
                    for jj, kb in enumerate((qb - 1, qb, qb + 1)):
                        if 0 <= kb < NKB:
                            sl = kb % NSL
                            lh = bass.AP(KR, (sl * 8 + hc) * 128
                                         + po * (NSL * 1024),
                                         [[NSL * 1024, 64], [1, 128]])
                            rh = bass.AP(QR0, hc * 512 + ql * 128 + po * 4096,
                                         [[4096, 64], [1, 128]])
                            mm(SC[:, 512 * jj:512 * jj + 128], lh, rh,
                               True, True, ["KR", "QR0"], [f"sc{jj}"])
                    sop(lambda s: s.activation(
                        WMT[:, :],
                        bass.AP(SC, 0, [[1536, 128], [512, 3], [1, 128]]),
                        AF.Exp, bias=CST[:, 0:1]),
                        ["sc0", "sc1", "sc2", "WMT", "CST"], ["WMT"])
                    vop(lambda s, v=variant: s.tensor_tensor(
                        WMT2[:, :], WMT[:, :], MSK[:, 384 * v:384 * v + 384],
                        ALU.mult), ["WMT", "MSK", "WMT2"], ["WMT2"])
                    for jj, kb in enumerate((qb - 1, qb, qb + 1)):
                        sl = kb % NSL
                        rh = bass.AP(VT, sl * (H * 65) + h * 65,
                                     [[NSL * H * 65, 128], [1, 65]])
                        mm(avreg[:, 0:65], WMT2[:, 128 * jj:128 * jj + 128],
                           rh, jj == 0, jj == 2, ["WMT2", "VT"], [avkey])
                    vop(lambda s, h=h, a=avreg: s.tensor_copy(
                        RST[:, h:h + 1], a[:, 64:65]), [avkey, "RST"],
                        ["RST"])
                    vop(lambda s, h=h, a=avreg: s.tensor_copy(
                        AOtm[:, 64 * h:64 * h + 64], a[:, 0:64]),
                        [avkey, "AOtm"], ["AOtm"])
                vop(lambda s: s.reciprocal(RSI[:, :], RST[:, :]),
                    ["RST", "RSI"], ["RSI"], small=["RST"])
                for h in range(H):
                    vop(lambda s, h=h: s.tensor_scalar(
                        AOtm[:, 64 * h:64 * h + 64],
                        AOtm[:, 64 * h:64 * h + 64], RSI[:, h:h + 1], None,
                        ALU.mult), ["AOtm", "RSI"], ["AOtm"], small=["RSI"])
                for dc in range(C):
                    tpose(PT[:, 128 * dc:128 * dc + 128],
                          AOtm[:, 128 * dc:128 * dc + 128], ["AOtm"],
                          ["pt"] if dc == C - 1 else ["pt_"])
                vop(lambda s, ql=ql: s.tensor_copy(
                    bass.AP(AO, ql * 128, [[4096, 128], [512, 8], [1, 128]]),
                    PT[:, :]), ["pt", "pt_", "AO"], ["AO"])

        evac_x1 = evac_resid(xfm, x1d, 3, "x1")
        for i in range(0 if SKIP123 else NT + 1):
            if i < NT:
                make_xs(i, xfm)
                proj_ln(i, 8192, 1, evac_k(i))
                proj_ln(i, 16384, 2, evac_v)
                vt_build(i, XBF, "XBF")
            if i >= 1:
                attention(i - 1)
                proj_plain(24576, AO, "AO", evac_x1(i - 1))
            if i < NT:
                proj_ln(i, 0, 0, evac_q)

        if PHASES >= 2 and not SKIP123:
            load_weights([l1w["q"], l1w["v"], l1w["g"], l1w["o"]])
            build_negcs([(WB, 0)])
            sweep_stats(x1d, "x1")
            evac_x2 = evac_resid(x1d, x2d, 4, "x2", "x1")

            def proj_wf(i, evac):
                for oc in range(C):
                    breg, bkey = (PA, "pa") if oc % 2 == 0 else (PB, "pb")
                    for dc in range(C):
                        mm(breg, WB[:, 1024 * dc + 128 * oc:
                                   1024 * dc + 128 * oc + 128],
                           XS[:, T * dc:T * dc + T],
                           dc == 0, False, ["WB", "XS"], [bkey])
                    mm(breg, NCS[0:1, 128 * oc:128 * oc + 128],
                       RXT[0:1, 0:T], False, True, ["NCS", "RXT"], [bkey])
                    evac(oc, breg, bkey)

            for i in range(NT):
                make_xs(i, x1d, "x1")
                proj_wf(i, evac_x2(i))

        if PHASES >= 3 and not SKIP123:
            build_negcs([(WA, 0), (WA, 8192), (WA, 16384)])
            sweep_stats(x2d, "x2")

            def evac_qr0(oc, breg, bkey):
                sop(lambda s, oc=oc, b=breg: s.activation(
                    QR0[:, 512 * oc:512 * oc + 512], b, AF.Identity),
                    [bkey, "QR0"], ["QR0"])

            def evac_gate(i):
                def f(oc, breg, bkey):
                    sop(lambda s, oc=oc, b=breg: s.activation(
                        AO[:, 512 * oc:512 * oc + 512], b, AF.Silu),
                        [bkey, "AO"], ["AO"])
                    dma("G", bass.AP(gscr, oc * 128 * S + i * T,
                                     [[S, 128], [1, T]]),
                        AO[:, 512 * oc:512 * oc + 512], ["AO"],
                        ["AO", f"g@{i}"])
                return f

            def evac_v3(oc, breg, bkey):
                sop(lambda s, oc=oc, b=breg: s.activation(
                    XBF[:, 512 * oc:512 * oc + 512], b, AF.Identity),
                    [bkey, "XBF"], ["XBF"])

            for i in range(NT):
                make_xs(i, x2d, "x2")
                proj_ln(i, 0, 0, evac_qr0)
                proj_ln(i, 16384, 2, evac_gate(i))
                proj_ln(i, 8192, 1, evac_v3)
                for tb in range(4 if P3RET else 0):
                    gtb = 4 * i + tb
                    STc = ST0 if gtb % 2 == 0 else ST1
                    STp = ST1 if gtb % 2 == 0 else ST0
                    ckey = f"ST{gtb % 2}"
                    pkey = f"ST{(gtb + 1) % 2}"
                    slv = tb % 2
                    for dc in range(C):
                        tpose(PT[:, 128 * dc:128 * dc + 128],
                              XBF[:, 512 * dc + 128 * tb:
                                  512 * dc + 128 * tb + 128], ["XBF"],
                              ["pt"] if dc == C - 1 else ["pt_"])
                    vop(lambda s, slv=slv: s.tensor_copy(
                        KR[:, slv * 1024:slv * 1024 + 1024], PT[:, :]),
                        ["pt", "pt_", "KR"], ["KR"])
                    for c4 in range(8):
                        mm(SC[:, 0:128],
                           KR[:, slv * 1024 + 128 * c4:
                              slv * 1024 + 128 * c4 + 128],
                           LMAT[:, 128 * (2 * c4):128 * (2 * c4) + 128],
                           True, True, ["KR", "LMAT"], ["sc0"])
                        mm(SC[:, 512:640],
                           KR[:, slv * 1024 + 128 * c4:
                              slv * 1024 + 128 * c4 + 128],
                           LMAT[:, 128 * (2 * c4 + 1):
                                128 * (2 * c4 + 1) + 128],
                           True, True, ["KR", "LMAT"], ["sc1"])
                        vop(lambda s, c=c4, sp=STp, sc_=STc:
                            s.scalar_tensor_tensor(
                                sc_[0:64, 128 * c:128 * c + 128],
                                PWR[0:64, 128 * c:128 * c + 128],
                                sp[0:64, 128 * c + 127:128 * c + 128],
                                SC[0:64, 0:128], ALU.mult, ALU.add),
                            ["PWR", pkey, "sc0", ckey], [ckey])
                        vop(lambda s, c=c4, sp=STp, sc_=STc:
                            s.scalar_tensor_tensor(
                                sc_[64:128, 128 * c:128 * c + 128],
                                PWR[64:128, 128 * c:128 * c + 128],
                                sp[64:128, 128 * c + 127:128 * c + 128],
                                SC[64:128, 512:640], ALU.mult, ALU.add),
                            ["PWR", pkey, "sc1", ckey], [ckey])
                        vop(lambda s, c=c4, sc_=STc, tb=tb:
                            s.tensor_tensor(
                                AO[:, 512 * c + 128 * tb:
                                   512 * c + 128 * tb + 128],
                                sc_[:, 128 * c:128 * c + 128],
                                QR0[:, 512 * c + 128 * tb:
                                    512 * c + 128 * tb + 128], ALU.mult),
                            [ckey, "QR0", "AO"], ["AO"])
                vop(lambda s: s.tensor_tensor(XS[:, :], AO[:, :], AO[:, :],
                                              ALU.mult), ["AO", "XS"], ["XS"])
                for dc in range(C):
                    mm(AVQ[0:16, 0:512], OH[:, 16 * i:16 * i + 16],
                       AO[:, T * dc:T * dc + T], i == 0 and dc == 0,
                       i == NT - 1 and dc == C - 1, ["AO", "OH"], ["av0"])
                for dc in range(C):
                    mm(AVQ[0:16, 512:1024], OH[:, 16 * i:16 * i + 16],
                       XS[:, T * dc:T * dc + T], i == 0 and dc == 0,
                       i == NT - 1 and dc == C - 1, ["XS", "OH"], ["av1"])
                dma("G", bass.AP(yscr, i * T,
                                 [[S, 128], [128 * S, 8], [1, T]]),
                    AO[:, :], ["AO"], [f"y@{i}"])

            rowmath("av0", AVQ[0:16, 0:512], "av1", AVQ[0:16, 512:1024])
            evac_x3 = evac_resid(x2d, x3d, None, "x3", "x2")
            for i in range(NT if P3SWC else 0):
                dma("G", AO[:, :],
                    bass.AP(yscr, i * T, [[S, 128], [128 * S, 8], [1, T]]),
                    ["AO", f"y@{i}"], ["AO"])
                dma("G", QR0[:, :],
                    bass.AP(gscr, i * T, [[S, 128], [128 * S, 8], [1, T]]),
                    ["QR0", f"g@{i}"], ["QR0"])
                extract_row_f(i, 0, 0)
                extract_row_f(i, T, T)
                bcast_row(SC[:, 0:T], "sc0", 0)
                bcast_row(SC[:, 512:512 + T], "sc1", T)
                for dc in range(C):
                    vop(lambda s, dc=dc: s.tensor_tensor(
                        XBF[:, T * dc:T * dc + T], AO[:, T * dc:T * dc + T],
                        SC[:, 0:T], ALU.add), ["AO", "sc0", "XBF"], ["XBF"])
                    vop(lambda s, dc=dc: s.scalar_tensor_tensor(
                        XS[:, T * dc:T * dc + T], XBF[:, T * dc:T * dc + T],
                        BIA[:, 40 + dc:41 + dc], SC[:, 512:512 + T],
                        ALU.mult, ALU.mult), ["XBF", "BIA", "sc1", "XS"],
                        ["XS"])
                    vop(lambda s, dc=dc: s.scalar_tensor_tensor(
                        XBF[:, T * dc:T * dc + T], XS[:, T * dc:T * dc + T],
                        BIA[:, 48 + dc:49 + dc],
                        QR0[:, T * dc:T * dc + T], ALU.add, ALU.mult),
                        ["XS", "BIA", "QR0", "XBF"], ["XBF"])
                proj_plain(24576, XBF, "XBF", evac_x3(i))

        if PHASES >= 4:
            for oc in range(8 if P4A >= 1 else 0):
                dma("Y", UF[:, 0:1024],
                    bass.AP(l1rhsT, oc * 128 * 1024, [[1024, 128], [1, 1024]]),
                    ["UF0"], ["UF0"])
                vop(lambda s: s.reduce_max(SMC[:, 0:1], UF[:, 0:1024],
                                           axis=AX.X,
                                           apply_absolute_value=True),
                    ["UF0", "SMC"], ["SMC"])
                vop(lambda s, oc=oc: s.tensor_scalar(
                    SMC2[:, oc:oc + 1], SMC[:, 0:1], 1e-12, 1.0 / 127.0,
                    ALU.max, ALU.mult), ["SMC", "SMC2"], ["SMC2"],
                    small=["SMC"])
                vop(lambda s, oc=oc: s.reciprocal(SMC[:, 1:2],
                                                  SMC2[:, oc:oc + 1]),
                    ["SMC2", "SMC"], ["SMC"], small=["SMC2"])
                vop(lambda s: s.tensor_scalar(
                    OX[:, 0:1024], UF[:, 0:1024], SMC[:, 1:2], 128.5,
                    ALU.mult, ALU.add), ["UF0", "SMC", "OX0", "OX1"],
                    ["OX0", "OX1"], small=["SMC"])
                for lb in range(8):
                    b = float(128 >> lb)
                    src, sk = (OX[:, 0:1024], ["OX0", "OX1"]) if lb == 0 \
                        else (UF[:, 0:1024], ["UF0"])
                    vop(lambda s, b=b, src=src: s.tensor_scalar(
                        ST1[:, :], src, b, -b, ALU.is_ge, ALU.mult),
                        sk + ["ST1"], ["ST1"])
                    vop(lambda s, src=src: s.tensor_tensor(
                        UF[:, 0:1024], src, ST1[:, :], ALU.add),
                        sk + ["ST1", "UF0"], ["UF0"])
                dst = XS if oc < 4 else XBF
                dk = "XS" if oc < 4 else "XBF"
                cb = (oc % 4) * 1024
                vop(lambda s, d=dst, cb=cb: s.scalar_tensor_tensor(
                    d[:, cb:cb + 1024], OX[:, 0:1024], 128.0, UF[:, 0:1024],
                    ALU.subtract, ALU.subtract),
                    ["OX0", "OX1", "UF0", dk], [dk])
            if P4A >= 2:
                tpose(PQ[0:8, 0:128], SMC2[:, 0:8], ["SMC2"], ["pa"],
                      f32=True)
                vop(lambda s: s.tensor_copy(ROWSF[0:8, 0:128],
                                            PQ[0:8, 0:128]),
                    ["pa", "ROWSF"], ["ROWSF"])
            for oc in range(8 if P4A >= 2 else 0):
                mm(PQ[0:1, 512:640], IDF[0:8, oc:oc + 1],
                   ROWSF[0:8, 0:128], True, True, ["IDB", "ROWSF"], ["pb"])
                vop(lambda s: s.tensor_copy(RXF[0:1, 0:128],
                                            PQ[0:1, 512:640]),
                    ["pb", "RXF"], ["RXF"])
                mm(PQ[:, 512:640], ON16F[0:1, :], RXF[0:1, 0:128],
                   True, True, ["ON16F", "RXF"], ["pb"])
                vop(lambda s, oc=oc: s.tensor_copy(
                    RSB[:, 128 * oc:128 * oc + 128], PQ[:, 512:640]),
                    ["pb", "RSB"], ["RSB"])
            for oc in range(8 if P4A >= 3 else 0):
                src_t = XS if oc < 4 else XBF
                sk = "XS" if oc < 4 else "XBF"
                cb = (oc % 4) * 1024
                for kc in range(8):
                    tpose(PT[:, 128 * kc:128 * kc + 128],
                          src_t[:, cb + 128 * kc:cb + 128 * kc + 128],
                          [sk], ["pt"] if kc == C - 1 else ["pt_"])
                vop(lambda s, oc=oc: s.tensor_copy(
                    bass.AP(WB, 128 * oc, [[8192, 128], [1024, 8], [1, 128]]),
                    PT[:, :]), ["pt", "pt_", "WB"], ["WB"])

            if P4B:
                sweep_stats(x3d, "x3")
            for pk, (roff, cbase) in enumerate([(0, 0), (T, 64)]
                                               if P4B else []):
                for q4 in range(4):
                    tpose(PQ[:, 0:16],
                          ROWSF[0:16, roff + 128 * q4:roff + 128 * q4 + 128],
                          ["ROWSF"], ["pa"], f32=True)
                    vop(lambda s, q4=q4, cb=cbase: s.tensor_copy(
                        bass.AP(COLS, cb + q4, [[128, 128], [4, 16]]),
                        PQ[:, 0:16]), ["pa", "COLS"], ["COLS"])

            for tb in range(NKB if P4C else 0):
                sl = tb % 2
                xfr = XF[:, 1024 * sl:1024 * sl + 1024]
                xfk = f"XFH{sl}"
                dma("Y", xfr,
                    bass.AP(x3d, tb * 128,
                            [[S, 128], [128 * S, 8], [1, 128]]),
                    [xfk, f"x3@{tb // 4}"], [xfk])
                for dc in range(C):
                    tpose(PQ[:, 128 * dc:128 * dc + 128],
                          xfr[:, 128 * dc:128 * dc + 128], [xfk],
                          ["pa"] if dc < 4 else ["pb"], f32=True)
                sop(lambda s, tb=tb: s.activation(
                    UF[:, 0:1024], PQ[:, :], AF.Identity,
                    bias=COLS[:, tb:tb + 1]),
                    ["pa", "pb", "COLS", "UF0"], ["UF0"])
                vop(lambda s: s.reduce_max(SMC[:, 0:1], UF[:, 0:1024],
                                           axis=AX.X,
                                           apply_absolute_value=True),
                    ["UF0", "SMC"], ["SMC"])
                vop(lambda s, tb=tb: s.tensor_scalar(
                    SMC[:, 1:2], SMC[:, 0:1], COLS[:, 64 + tb:65 + tb],
                    None, ALU.mult), ["SMC", "COLS"], ["SMC"], small=["SMC"])
                vop(lambda s: s.tensor_scalar(
                    SMC[:, 2:3], SMC[:, 1:2], 1e-12, 1.0 / 127.0, ALU.max,
                    ALU.mult), ["SMC", "LSC"], ["LSC"], small=["SMC"])
                vop(lambda s: s.reciprocal(SMC[:, 3:4], SMC[:, 0:1]),
                    ["SMC", "RCQ"], ["RCQ"], small=["SMC"])
                vop(lambda s: s.tensor_scalar(SMC[:, 4:5], SMC[:, 3:4],
                                              127.0, None, ALU.mult),
                    ["RCQ", "RCQ2"], ["RCQ2"], small=["RCQ"])
                sop(lambda s: s.activation(
                    OX[:, 0:1024], UF[:, 0:1024], AF.Identity,
                    bias=CST[:, 1:2], scale=SMC[:, 4:5]),
                    ["UF0", "RCQ2", "OX0", "OX1", "CST"], ["OX0", "OX1"],
                    small=["RCQ2"])
                for lb in range(8):
                    b = float(128 >> lb)
                    src, sk = (OX[:, 0:1024], ["OX0", "OX1"]) if lb == 0 \
                        else (ST1[:, :], ["ST1"])
                    vop(lambda s, b=b, src=src: s.tensor_scalar(
                        ST0[:, :], src, b, -b, ALU.is_ge, ALU.mult),
                        sk + ["ST0"], ["ST0"])
                    vop(lambda s, src=src: s.tensor_tensor(
                        ST1[:, :], src, ST0[:, :], ALU.add),
                        sk + ["ST0", "ST1"], ["ST1"])
                vop(lambda s: s.scalar_tensor_tensor(
                    XBF[:, 0:1024], OX[:, 0:1024], 128.0, ST1[:, :],
                    ALU.subtract, ALU.subtract),
                    ["OX0", "OX1", "ST1", "XBF"], ["XBF"])
                for kc in range(C):
                    tpose(PT[:, 128 * kc:128 * kc + 128],
                          XBF[:, 128 * kc:128 * kc + 128], ["XBF"],
                          ["pt"] if kc == C - 1 else ["pt_"])
                vop(lambda s: s.tensor_copy(XS[:, 0:1024], PT[:, :]),
                    ["pt", "pt_", "XS"], ["XS"])
                for half in range(2):
                    sck = f"sc{half}"
                    for kc in range(C):
                        mm(SC[:, 512 * half:512 * half + 512],
                           XS[:, 128 * kc:128 * kc + 128],
                           WB[:, 1024 * kc + 512 * half:
                              1024 * kc + 512 * half + 512],
                           kc == 0, kc == C - 1, ["XS", "WB"], [sck])
                    okey = f"OX{half}"
                    vop(lambda s, half=half: s.scalar_tensor_tensor(
                        OX[:, 512 * half:512 * half + 512],
                        RSB[:, 512 * half:512 * half + 512], SMC[:, 2:3],
                        SC[:, 512 * half:512 * half + 512], ALU.mult,
                        ALU.mult), ["RSB", "LSC", sck, okey], [okey],
                        small=["LSC"])
                    vop(lambda s, half=half, tb=tb:
                        s.scalar_tensor_tensor(
                            OX[:, 512 * half:512 * half + 512],
                            UF[:, 512 * half:512 * half + 512],
                            COLS[:, tb:tb + 1],
                            OX[:, 512 * half:512 * half + 512],
                            ALU.subtract, ALU.add),
                        ["UF0", "COLS", okey], [okey])
                    dma("Y", bass.AP(ofm, tb * 128 * D + 512 * half,
                                     [[1024, 128], [1, 512]]),
                        OX[:, 512 * half:512 * half + 512], [okey], [okey])

        p.finish()

        with nc.Block() as block:
            @block.tensor
            def _(sec):
                for f in p.ops["T"]:
                    f(sec)

            @block.vector
            def _(sec):
                for f in p.ops["V"]:
                    f(sec)

            @block.scalar
            def _(sec):
                for f in p.ops["S"]:
                    f(sec)

            @block.sync
            def _(sec):
                for f in p.ops["Y"]:
                    f(sec)

            @block.gpsimd
            def _(sec):
                for f in p.ops["G"]:
                    f(sec)

    return nc


_CONSTS = None


def host_consts():
    global _CONSTS
    if _CONSTS is not None:
        return _CONSTS
    import ml_dtypes
    bf = ml_dtypes.bfloat16
    ident = np.eye(128, dtype=bf)
    oh = np.zeros((128, 256), dtype=bf)
    for i in range(16):
        oh[16 * 0:, 16 * i + i] = 0
    for i in range(16):
        oh[:, 16 * i + i] = 1
    ones16 = np.ones((16, 128), dtype=bf)
    ones16f = np.ones((16, 128), dtype=np.float32)
    k = np.arange(128)[:, None]
    q = np.arange(128)[None, :]
    mL = (k >= q).astype(bf)
    mR = (k <= q).astype(bf)
    mid = np.ones((128, 128), dtype=bf)
    zero = np.zeros((128, 128), dtype=bf)
    masks = np.concatenate([zero, mid, mR, mL, mid, mR, mL, mid, zero],
                           axis=1)
    s_ = np.arange(128)[:, None]
    t_ = np.arange(128)[None, :]
    diota = np.broadcast_to((t_ - s_).astype(np.float32), (128, 128))
    dnegb = np.where(t_ >= s_, 0.0, -1e9).astype(np.float32)
    dp1 = np.broadcast_to((t_ + 1).astype(np.float32), (128, 128))
    dtabs = np.ascontiguousarray(
        np.concatenate([diota, dnegb, dp1], axis=1).astype(np.float32))
    _CONSTS = {
        "identb": ident, "identf": np.eye(128, dtype=np.float32),
        "oh16": oh, "ones16": ones16, "ones16f": ones16f,
        "masks": np.ascontiguousarray(masks), "dtabs": dtabs,
    }
    return _CONSTS


def host_inputs(inputs, core):
    x = np.asarray(inputs["L0_attn_x"][core])
    cc = np.ascontiguousarray
    d = {
        "xfm": cc(x.T),
        "l0wq": cc(np.asarray(inputs["L0_attn_Wq"]).T),
        "l0wk": cc(np.asarray(inputs["L0_attn_Wk"]).T),
        "l0wv": cc(np.asarray(inputs["L0_attn_Wv"]).T),
        "l0wo": cc(np.asarray(inputs["L0_attn_Wo"]).T),
        "l0bq": np.asarray(inputs["L0_attn_bq"]),
        "l0bk": np.asarray(inputs["L0_attn_bk"]),
        "l0bv": np.asarray(inputs["L0_attn_bv"]),
        "l0bo": np.asarray(inputs["L0_attn_bo"]),
        "g1r": cc(np.asarray(inputs["L0_ffn_G1_R"])[0].transpose(1, 0, 2)
                  .reshape(4, 1024)),
        "g2r": cc(np.asarray(inputs["L0_ffn_G2_R"])[:, :, 0, :]
                  .reshape(4, 256)),
        "g1i": cc(np.asarray(inputs["L0_ffn_G1_I"])[0].transpose(1, 0, 2)
                  .reshape(4, 1024)),
        "g2i": cc(np.asarray(inputs["L0_ffn_G2_I"])[:, :, 0, :]
                  .reshape(4, 256)),
        "l0fb": cc(np.concatenate([np.asarray(inputs["L0_ffn_bias_real"]),
                                   np.asarray(inputs["L0_ffn_bias_imag"])])),
        "l1wq": cc(np.asarray(inputs["L1_attn_Wq"]).T),
        "l1wv": cc(np.asarray(inputs["L1_attn_Wv"]).T),
        "l1wg": cc(np.asarray(inputs["L1_attn_Wg"]).T),
        "l1wo": cc(np.asarray(inputs["L1_attn_Wo"]).T),
        "l1lng": np.asarray(inputs["L1_attn_ln_g"]),
        "l1lnb": np.asarray(inputs["L1_attn_ln_b"]),
        "l1decay": cc(np.asarray(inputs["L1_attn_decay"])[:, 1, 0]),
        "l1rhsT": cc(np.asarray(inputs["L1_ffn_rhs"]).T),
    }
    d.update(host_consts())
    return d


def kernel(**inputs):
    nc = build()
    in_maps = [host_inputs(inputs, core) for core in range(8)]
    res = run_bass_kernel_spmd(nc, in_maps, core_ids=list(range(8)))
    outs = []
    for core in range(8):
        r = res.results[core]
        if PHASES == 4:
            outs.append(np.asarray(r["ofm"]))
        else:
            key = {1: "x1d", 2: "x2d", 3: "x3d"}[PHASES]
            outs.append(np.asarray(r[key]).T)
    return np.stack(outs, axis=0)


# version 08
# speedup vs baseline: 1.0840x; 1.0840x over previous
import numpy as np
import concourse.bass as bass
import concourse.mybir as mybir
from concourse.bass_utils import run_bass_kernel_spmd
from contextlib import ExitStack

F32 = mybir.dt.float32
BF16 = mybir.dt.bfloat16
AF = mybir.ActivationFunctionType
ALU = mybir.AluOpType
AX = mybir.AxisListType

S = 8192
D = 1024
T = 512
NT = S // T
C = 8
H = 16
NKB = S // 128
NSL = 9
RCONST = 1.5 * 2.0 ** 23
EXPB = -3.0

PHASES = 4
P3RET = True
P3SWC = True
P4A = 3
P4B = True
P4C = True
SKIP123 = False


class Prog:
    ENG = ("T", "V", "S", "Y", "G")
    NPOOL = 12

    def __init__(self, nc, es):
        self.nc = nc
        self.ops = {e: [] for e in self.ENG}
        self.semkeys = ["T", "V", "S"] + \
            [f"Y{i}" for i in range(self.NPOOL)] + \
            [f"G{i}" for i in range(self.NPOOL)]
        self.sem = {k: es.enter_context(nc.semaphore(f"sem{k}"))
                    for k in self.semkeys}
        self.cnt = {k: 0 for k in self.semkeys}
        self.dma_idx = {"Y": 0, "G": 0}
        self.res = {}
        self.waited = {e: {k: 0 for k in self.semkeys} for e in self.ENG}
        self.icnt = {e: 0 for e in self.ENG}
        self.drained = {e: 0 for e in self.ENG}

    def _add_waits(self, eng, reads, writes, need):
        def add(k2, c2):
            if k2 == eng:
                return
            need[k2] = max(need.get(k2, 0), c2)

        for r in reads:
            st = self.res.get(r)
            if st and st["w"]:
                add(*st["w"])
        for w in writes:
            st = self.res.get(w)
            if st:
                if st["w"]:
                    add(*st["w"])
                for (k2, c2) in st["r"]:
                    add(k2, c2)

    def op(self, eng, fn, reads=(), writes=(), dma=False, small=()):
        reads = list(reads)
        writes = list(writes)
        need = {}
        self._add_waits(eng, reads, writes, need)
        if dma:
            semkey = f"{eng}{self.dma_idx[eng] % self.NPOOL}"
            self.dma_idx[eng] += 1
            if self.cnt[semkey] > 0:
                need[semkey] = max(need.get(semkey, 0), self.cnt[semkey])
            inc = 16
        else:
            semkey = eng
            inc = 1
        waits = []
        for k2, c2 in sorted(need.items()):
            if self.waited[eng][k2] < c2:
                self.waited[eng][k2] = c2
                waits.append((self.sem[k2], c2))
        drain = False
        for r in small:
            st = self.res.get(r)
            if st and st["w"] and st["w"][0] == eng:
                wc = st["w"][1]
                if self.icnt[eng] - wc < 12 and wc > self.drained[eng]:
                    drain = True
        if drain:
            self.drained[eng] = self.icnt[eng]
        self.cnt[semkey] += inc
        newc = self.cnt[semkey]
        if not dma:
            self.icnt[eng] = newc
        sem = self.sem[semkey]

        def emit(sec, waits=waits, fn=fn, drain=drain, sem=sem, inc=inc):
            for s_, v_ in waits:
                sec.wait_ge(s_, v_)
            if drain:
                sec.drain()
            fn(sec).then_inc(sem, inc)

        self.ops[eng].append(emit)
        for w in writes:
            self.res[w] = {"w": (semkey, newc), "r": []}
        for r in reads:
            st = self.res.setdefault(r, {"w": None, "r": []})
            st["r"].append((semkey, newc))
        return newc

    def finish(self):
        for k in self.semkeys:
            v = self.cnt[k]
            if v:
                def emit(sec, s_=self.sem[k], v_=v):
                    sec.wait_ge(s_, v_)
                self.ops["Y"].append(emit)


RACECHECK = False


def build():
    nc = bass.Bass("TRN2", target_bir_lowering=False,
                   detect_race_conditions=RACECHECK)

    def din(name, shape, dt=F32):
        return nc.dram_tensor(name, shape, dt, kind="ExternalInput")

    xfm = din("xfm", [D, S])
    l0w = {k: din(f"l0w{k}", [D, D]) for k in "qkvo"}
    l0b = {k: din(f"l0b{k}", [D]) for k in "qkvo"}
    g1r = din("g1r", [4, 1024]); g2r = din("g2r", [4, 256])
    g1i = din("g1i", [4, 1024]); g2i = din("g2i", [4, 256])
    l0fb = din("l0fb", [D])
    l1w = {k: din(f"l1w{k}", [D, D]) for k in "qvgo"}
    l1lng = din("l1lng", [D]); l1lnb = din("l1lnb", [D])
    l1decay = din("l1decay", [16])
    l1rhsT = din("l1rhsT", [D, D])
    identb = din("identb", [128, 128], BF16)
    identf = din("identf", [128, 128])
    oh16 = din("oh16", [128, 16 * 16], BF16)
    ones16 = din("ones16", [16, 128], BF16)
    ones16f = din("ones16f", [16, 128])
    masks = din("masks", [128, 3 * 384], BF16)
    dtabs = din("dtabs", [128, 384])

    ofm = nc.dram_tensor("ofm", [S, D], F32, kind="ExternalOutput")

    def scratch(name, k):
        kind = "ExternalOutput" if PHASES == k else "Internal"
        return nc.dram_tensor(name, [D, S], F32, kind=kind)

    x1d = scratch("x1d", 1)
    x2d = scratch("x2d", 2)
    x3d = scratch("x3d", 3)
    yscr = nc.dram_tensor("yscr", [D, S], BF16)
    gscr = nc.dram_tensor("gscr", [D, S], BF16)
    pscr = nc.dram_tensor("pscr", [2 * 1024 * 256], BF16)

    es = ExitStack()
    sbt = lambda n, sh, dt: es.enter_context(nc.sbuf_tensor(n, sh, dt))
    with es:
        WA = sbt("WA", [128, 4 * 8192], BF16)
        WB = sbt("WB", [128, 8192], BF16)
        KR = sbt("KR", [128, NSL * 1024], BF16)
        VT = sbt("VT", [128, NSL * (H * 65)], BF16)
        XF = sbt("XF", [128, 2048], F32)
        XBF = sbt("XBF", [128, 4096], BF16)
        XS = sbt("XS", [128, 4096], BF16)
        QR0 = sbt("QR0", [128, 4096], BF16)
        AO = sbt("AO", [128, 4096], BF16)
        AOtm = sbt("AOtm", [128, 1024], BF16)
        UF = sbt("UF", [128, 1024], F32)
        OX = sbt("OX", [128, 1024], F32)
        ST0 = sbt("ST0", [128, 1024], F32)
        ST1 = sbt("ST1", [128, 1024], F32)
        LMAT = sbt("LMAT", [128, H * 128], BF16)
        PWR = sbt("PWR", [128, 1024], F32)
        WMT = sbt("WMT", [128, 384], BF16)
        WMT2 = sbt("WMT2", [128, 384], BF16)
        MSK = sbt("MSK", [128, 3 * 384], BF16)
        IDB = sbt("IDB", [128, 128], BF16)
        IDF = sbt("IDF", [128, 128], F32)
        OH = sbt("OH", [128, 256], BF16)
        ON16 = sbt("ON16", [16, 128], BF16)
        ON16F = sbt("ON16F", [16, 128], F32)
        ONESC = sbt("ONESC", [128, 1], BF16)
        NCS = sbt("NCS", [1, 3 * 1024], BF16)
        ROWS = sbt("ROWS", [16, T], BF16)
        RXT = sbt("RXT", [1, 2 * T], BF16)
        RXF = sbt("RXF", [1, 128], F32)
        CST = sbt("CST", [128, 4], F32)
        ROWSF = sbt("ROWSF", [16, 2 * T], F32)
        COLS = sbt("COLS", [128, 128], F32)
        BIA = sbt("BIA", [128, 64], F32)
        RST = sbt("RST", [128, 16], F32)
        RSI = sbt("RSI", [128, 16], F32)
        SMC = sbt("SMC", [128, 8], F32)
        SMC2 = sbt("SMC2", [128, 8], F32)
        LNC = sbt("LNC", [128, 16], F32)
        PRC = sbt("PRC", [128, 8], F32)
        DEC = sbt("DEC", [16, 2], F32)
        LNR = sbt("LNR", [1, 16], F32)
        RSB = sbt("RSB", [128, 1024], F32)

        PQ = es.enter_context(nc.psum_tensor("PQ", [128, 1024], F32))
        SC = es.enter_context(nc.psum_tensor("SCP", [128, 1536], F32))
        AVQ = es.enter_context(nc.psum_tensor("AVQ", [128, 1024], F32))
        PT = es.enter_context(nc.psum_tensor("PTP", [128, 1024], BF16))

        p = Prog(nc, es)
        ident = IDB[:, :]

        def dma(q, dst, src, reads, writes, slow=False):
            p.op(q, lambda s, d=dst, r=src, sl=slow: s.dma_start(
                d, r, allow_slow_non_contiguous=sl),
                 reads=reads, writes=writes, dma=True)

        def mm(out, lh, rh, start, stop, reads, writes):
            p.op("T", lambda s, o=out, l=lh, r=rh, a=start, b=stop:
                 s.matmul(o, l, r, start=a, stop=b, skip_group_check=True),
                 reads=reads, writes=writes)

        def tpose(out, in_, reads, writes, f32=False):
            pc = in_.ap[0][1]
            idap = (IDF if f32 else IDB)[0:pc, 0:pc]
            p.op("T", lambda s, o=out, i=in_, d=idap: s.transpose(o, i, d),
                 reads=list(reads) + ["IDB"], writes=writes)

        vop = lambda fn, reads=(), writes=(), small=(): p.op("V", fn, reads, writes, small=small)
        sop = lambda fn, reads=(), writes=(), small=(): p.op("S", fn, reads, writes, small=small)

        PA = PQ[:, 0:512]
        PB = PQ[:, 512:1024]

        dma("Y", IDB[:, :], identb.ap(), [], ["IDB"])
        dma("Y", IDF[:, :], identf.ap(), [], ["IDB"])
        dma("Y", MSK[:, :], masks.ap(), [], ["MSK"])
        dma("Y", OH[:, :], oh16.ap(), [], ["OH"])
        dma("Y", ON16[:, :], ones16.ap(), [], ["ON16"])
        dma("Y", ON16F[:, :], ones16f.ap(), [], ["ON16F"])
        dma("Y", ST0[:, 0:384], dtabs.ap(), [], ["DTB"])
        for j, t in enumerate([l0b["q"], l0b["k"], l0b["v"], l0b["o"], l0fb,
                               l1lng, l1lnb]):
            dma("Y", BIA[:, 8 * j:8 * j + 8],
                bass.AP(t, 0, [[1, 128], [128, 8]]), ["BIA"], ["BIA"],
                slow=True)
        vop(lambda s: s.tensor_scalar(BIA[:, 0:8], BIA[:, 0:8], 0.125, None,
                                      ALU.mult), ["BIA"], ["BIA"])
        vop(lambda s: s.memset(ONESC[:, :], 1.0), [], ["ONESC"])
        vop(lambda s: s.memset(CST[:, 0:1], EXPB), ["CST"], ["CST"])
        vop(lambda s: s.memset(CST[:, 1:2], 128.5), ["CST"], ["CST"])
        vop(lambda s: s.memset(KR[:, :], 0.0), [], ["KR"])
        vop(lambda s: s.memset(
            bass.AP(VT, 0, [[NSL * H * 65, 128], [65, NSL * H], [1, 64]]),
            0.0), [], ["VT"])
        vop(lambda s: s.memset(
            bass.AP(VT, 64, [[NSL * H * 65, 128], [65, NSL * H]]), 1.0),
            ["VT"], ["VT"])

        dma("Y", DEC[:, 0:1], bass.AP(l1decay, 0, [[1, 16], [16, 1]]),
            ["DEC"], ["DEC"], slow=True)
        vop(lambda s: s.tensor_scalar(DEC[:, 0:1], DEC[:, 0:1], 1e-30, None,
                                      ALU.max), ["DEC"], ["DEC"])
        sop(lambda s: s.activation(DEC[:, 1:2], DEC[:, 0:1], AF.Ln),
            ["DEC"], ["DECL"])
        tpose(PQ[0:1, 0:16], DEC[:, 1:2], ["DECL"], ["pa"], f32=True)
        vop(lambda s: s.tensor_copy(LNR[:, :], PQ[0:1, 0:16]), ["pa"], ["LNR"])
        for h in range(H):
            mm(PQ[:, 0:1], ON16F[0:1, :], LNR[:, h:h + 1], True, True,
               ["ON16F", "LNR"], ["pa"])
            vop(lambda s, h=h: s.tensor_copy(LNC[:, h:h + 1], PQ[:, 0:1]),
                ["pa", "LNC"], ["LNC"])
        for h in range(H):
            vop(lambda s, h=h: s.scalar_tensor_tensor(
                OX[:, 0:128], ST0[:, 0:128], LNC[:, h:h + 1],
                ST0[:, 128:256], ALU.mult, ALU.add),
                ["DTB", "LNC", "OX0"], ["OX0"], small=["LNC"])
            sop(lambda s, h=h: s.activation(LMAT[:, 128 * h:128 * h + 128],
                                            OX[:, 0:128], AF.Exp),
                ["OX0"], ["LMAT"])
        for c4 in range(8):
            vop(lambda s, c=c4: s.tensor_copy(PRC[0:64, c:c + 1],
                                              LNC[0:64, 2 * c:2 * c + 1]),
                ["LNC", "PRC"], ["PRC"])
            vop(lambda s, c=c4: s.tensor_copy(PRC[64:128, c:c + 1],
                                              LNC[64:128, 2 * c + 1:2 * c + 2]),
                ["LNC", "PRC"], ["PRC"])
            sop(lambda s, c=c4: s.activation(PWR[:, 128 * c:128 * c + 128],
                                             ST0[:, 256:384], AF.Exp,
                                             scale=PRC[:, c:c + 1]),
                ["DTB", "PRC"], ["PWR"], small=["PRC"])
        vop(lambda s: s.memset(ST0[:, :], 0.0), [], ["ST0", "DTB"])
        vop(lambda s: s.memset(ST1[:, :], 0.0), [], ["ST1"])

        for part, (gg1, gg2) in enumerate([(g1r, g2r), (g1i, g2i)]):
            dma("Y", UF[0:4, 0:1024], gg1.ap(), ["UF0"], ["UF0"])
            dma("Y", OX[0:4, 0:256], gg2.ap(), ["OX0"], ["OX0"])
            for mb in range(8):
                mm(PQ[:, 0:256], UF[0:4, 128 * mb:128 * mb + 128],
                   OX[0:4, 0:256], True, True, ["UF0", "OX0"], ["pa"])
                vop(lambda s: s.tensor_copy(
                    XBF[:, 0:256],
                    bass.AP(PQ, 0, [[1024, 128], [1, 16], [16, 16]])),
                    ["pa", "KRS"], ["KRS"])
                dma("Y", bass.AP(pscr, (part * 1024 + 128 * mb) * 256,
                                 [[256, 128], [1, 256]]), XBF[:, 0:256],
                    ["KRS", "XBF"], [f"pscr{part}_{mb}"])
        pdeps = [f"pscr{i}_{m}" for i in range(2) for m in range(8)]

        def wf_reorder(dst_col_base):
            for q4 in range(4):
                for dl in range(8):
                    src_r = bass.AP(pscr, (8 * q4 + dl) * 256,
                                    [[16, 16], [8192, 32], [1, 16]])
                    src_i = bass.AP(pscr, 1024 * 256 + (8 * q4 + dl) * 256,
                                    [[16, 16], [8192, 32], [1, 16]])
                    rows = (16 * dl, 16 * dl + 16)
                    dma("Y", WB[rows[0]:rows[1], 1024 * q4:1024 * q4 + 512],
                        src_r, pdeps + ["WB"], ["WB"], slow=True)
                    dma("Y", WB[rows[0]:rows[1],
                               1024 * q4 + 512:1024 * q4 + 1024],
                        src_i, pdeps + ["WB"], ["WB"], slow=True)
                    dma("Y", WB[rows[0]:rows[1],
                               1024 * (4 + q4) + 512:1024 * (4 + q4) + 1024],
                        src_r, pdeps + ["WB"], ["WB"], slow=True)
                    dma("Y", WB[rows[0]:rows[1],
                               1024 * (4 + q4):1024 * (4 + q4) + 512],
                        src_i, pdeps + ["WB"], ["WB"], slow=True)
            for q4 in range(4):
                vop(lambda s, q=q4: s.tensor_scalar(
                    WB[:, 1024 * (4 + q):1024 * (4 + q) + 512],
                    WB[:, 1024 * (4 + q):1024 * (4 + q) + 512], -1.0, None,
                    ALU.mult), ["WB"], ["WB"])
        wf_reorder(0)

        def load_weights(tensors):
            for sl, t in enumerate(tensors):
                dma("G", WA[:, 8192 * sl:8192 * sl + 8192],
                    bass.AP(t, 0, [[1024, 128], [128 * 1024, 8], [1, 1024]]),
                    ["WA"], ["WA"])

        def build_negcs(slots):
            bk = {id(WA): "WA", id(WB): "WB"}
            for j, (buf, base) in enumerate(slots):
                for half in range(2):
                    for dc in range(C):
                        mm(PQ[0:1, 0:512], ONESC[:, :],
                           buf[:, base + 1024 * dc + 512 * half:
                               base + 1024 * dc + 512 * half + 512],
                           dc == 0, dc == C - 1, ["ONESC", bk[id(buf)]], ["pa"])
                    vop(lambda s, j=j, half=half: s.tensor_copy(
                        NCS[0:1, 1024 * j + 512 * half:
                            1024 * j + 512 * half + 512],
                        PQ[0:1, 0:512]), ["pa", "NCS"], ["NCS"])

        def sweep_stats(xsrc, tag=None):
            for i in range(NT):
                rd = ["XBF"] + ([f"{tag}@{i}"] if tag else [])
                dma("G", XBF[:, :],
                    bass.AP(xsrc, i * T, [[S, 128], [128 * S, 8], [1, T]]),
                    rd, ["XBF"])
                vop(lambda s: s.tensor_tensor(XS[:, :], XBF[:, :], XBF[:, :],
                                              ALU.mult), ["XBF", "XS"], ["XS"])
                for dc in range(C):
                    mm(PQ[0:16, 0:T], OH[:, 16 * i:16 * i + 16],
                       XBF[:, T * dc:T * dc + T], i == 0 and dc == 0,
                       i == NT - 1 and dc == C - 1, ["XBF", "OH"], ["pa"])
                for dc in range(C):
                    mm(PQ[0:16, 512:1024], OH[:, 16 * i:16 * i + 16],
                       XS[:, T * dc:T * dc + T], i == 0 and dc == 0,
                       i == NT - 1 and dc == C - 1, ["XS", "OH"], ["pb"])
            rowmath("pa", PQ[0:16, 0:T], "pb", PQ[0:16, 512:1024])

        def rowmath(ka, s_reg, kb, ss_reg):
            c1 = 1.0 / D
            vop(lambda s: s.tensor_scalar(ROWSF[:, 0:T], s_reg, -c1, None,
                                          ALU.mult), [ka, "ROWSF"], ["ROWSF"])
            vop(lambda s: s.scalar_tensor_tensor(
                OX[0:16, 0:512], ROWSF[:, 0:T], -1.0, ROWSF[:, 0:T],
                ALU.mult, ALU.mult), ["ROWSF", "OX0"], ["OX0"])
            vop(lambda s: s.tensor_scalar(OX[0:16, 0:512], OX[0:16, 0:512],
                                          1e-5, None, ALU.add), ["OX0"],
                ["OX0"])
            vop(lambda s: s.scalar_tensor_tensor(
                OX[0:16, 512:1024], ss_reg, c1, OX[0:16, 0:512],
                ALU.mult, ALU.add), [kb, "OX0", "OX1"], ["OX1"])
            vop(lambda s: s.reciprocal(OX[0:16, 0:512], OX[0:16, 512:1024]),
                ["OX1", "OX0"], ["OX0"])
            sop(lambda s: s.activation(ROWSF[:, T:2 * T], OX[0:16, 0:512],
                                       AF.Sqrt), ["OX0", "ROWSF"], ["ROWSF"])
            vop(lambda s: s.tensor_tensor(ROWS[:, 0:T], ROWSF[:, 0:T],
                                          ROWSF[:, T:2 * T], ALU.mult),
                ["ROWSF", "ROWS"], ["ROWS"])


        def extract_row(i, pack_off, dst_off):
            mm(SC[0:1, 0:T], IDB[0:16, i:i + 1],
               ROWS[0:16, pack_off:pack_off + T], True, True,
               ["IDB", "ROWS", "sc0"], ["sc0"])
            vop(lambda s, d=dst_off: s.tensor_copy(
                RXT[0:1, d:d + T], SC[0:1, 0:T]), ["sc0", "RXT"], ["RXT"])

        def extract_row_f(i, packf_off, dst_off):
            mm(SC[0:1, 0:T], IDF[0:16, i:i + 1],
               ROWSF[0:16, packf_off:packf_off + T], True, True,
               ["IDB", "ROWSF", "sc0"], ["sc0"])
            vop(lambda s, d=dst_off: s.tensor_copy(
                RXT[0:1, d:d + T], SC[0:1, 0:T]), ["sc0", "RXT"], ["RXT"])

        def bcast_row(dst_reg, dst_key, src_off):
            mm(dst_reg, ON16[0:1, :], RXT[0:1, src_off:src_off + T],
               True, True, ["ON16", "RXT", dst_key], [dst_key])

        def make_xs(i, xsrc, tag=None):
            rd = ["XBF"] + ([f"{tag}@{i}"] if tag else [])
            dma("G", XBF[:, :],
                bass.AP(xsrc, i * T, [[S, 128], [128 * S, 8], [1, T]]),
                rd, ["XBF"])
            extract_row_f(i, T, T)
            extract_row(i, 0, 0)
            bcast_row(SC[:, 0:T], "sc0", T)
            for dc in range(C):
                vop(lambda s, dc=dc: s.tensor_tensor(
                    XS[:, T * dc:T * dc + T], XBF[:, T * dc:T * dc + T],
                    SC[:, 0:T], ALU.mult), ["XBF", "sc0", "XS"], ["XS"])

        def proj_ln(i, wbase, ncs_j, evac):
            for oc in range(C):
                breg, bkey = (PA, "pa") if oc % 2 == 0 else (PB, "pb")
                for dc in range(C):
                    mm(breg, WA[:, wbase + 1024 * dc + 128 * oc:
                               wbase + 1024 * dc + 128 * oc + 128],
                       XS[:, T * dc:T * dc + T],
                       dc == 0, False, ["WA", "XS"], [bkey])
                mm(breg, NCS[0:1, 1024 * ncs_j + 128 * oc:
                             1024 * ncs_j + 128 * oc + 128],
                   RXT[0:1, 0:T], False, True, ["NCS", "RXT"], [bkey])
                evac(oc, breg, bkey)

        def proj_plain(wbase, rhs_buf, rhs_key, evac):
            for oc in range(C):
                breg, bkey = (PA, "pa") if oc % 2 == 0 else (PB, "pb")
                for dc in range(C):
                    mm(breg, WA[:, wbase + 1024 * dc + 128 * oc:
                               wbase + 1024 * dc + 128 * oc + 128],
                       rhs_buf[:, T * dc:T * dc + T],
                       dc == 0, dc == C - 1, ["WA", rhs_key], [bkey])
                evac(oc, breg, bkey)

        def evac_resid(xsrc_res, dst_dram, bias_slot, dst_tag, src_tag=None):
            def f_for(i):
                def f(oc, breg, bkey):
                    sl = oc % 2
                    xf = XF[:, 1024 * sl:1024 * sl + 512]
                    xkey = f"XFH{sl}"
                    rdx = [xkey] + ([f"{src_tag}@{i}"] if src_tag else [])
                    dma("Y", xf,
                        bass.AP(xsrc_res, oc * 128 * S + i * T,
                                [[S, 128], [1, T]]), rdx, [xkey])
                    ox = OX[:, 512 * sl:512 * sl + 512]
                    okey = f"OX{sl}"
                    if bias_slot is not None:
                        vop(lambda s, b=breg, x=xf, o=ox, oc=oc:
                            s.scalar_tensor_tensor(
                                o, b, BIA[:, 8 * bias_slot + oc:
                                          8 * bias_slot + oc + 1], x,
                                ALU.add, ALU.add),
                            [bkey, xkey, "BIA", okey], [okey])
                    else:
                        vop(lambda s, b=breg, x=xf, o=ox:
                            s.tensor_tensor(o, b, x, ALU.add),
                            [bkey, xkey, okey], [okey])
                    dma("Y", bass.AP(dst_dram, oc * 128 * S + i * T,
                                     [[S, 128], [1, T]]), ox, [okey],
                        [okey, f"{dst_tag}@{i}"])
                return f
            return f_for

        if not SKIP123:
            load_weights([l0w["q"], l0w["k"], l0w["v"], l0w["o"]])
            build_negcs([(WA, 0), (WA, 8192), (WA, 16384)])
            sweep_stats(xfm)

        def evac_q(oc, breg, bkey):
            sop(lambda s, oc=oc, b=breg: s.activation(
                QR0[:, 512 * oc:512 * oc + 512], b, AF.Identity,
                bias=BIA[:, oc:oc + 1], scale=0.125),
                [bkey, "BIA", "QR0"], ["QR0"])

        def evac_k(i):
            def f(oc, breg, bkey):
                slots = [(4 * i + m) % NSL for m in range(4)]
                runs = [[0]]
                for m in range(1, 4):
                    if slots[m] == slots[m - 1] + 1:
                        runs[-1].append(m)
                    else:
                        runs.append([m])
                for run in runs:
                    m0, ln = run[0], len(run)
                    dst = bass.AP(KR, (slots[m0] * 8 + oc) * 128,
                                  [[NSL * 1024, 128], [1024, ln], [1, 128]])
                    src = bass.AP(PQ, (0 if bkey == "pa" else 512) + m0 * 128,
                                  [[1024, 128], [128, ln], [1, 128]])
                    sop(lambda s, d=dst, r=src, oc=oc: s.activation(
                        d, r, AF.Identity, bias=BIA[:, 8 + oc:9 + oc]),
                        [bkey, "BIA", "KR"], ["KR"])
            return f

        def evac_v(oc, breg, bkey):
            sop(lambda s, oc=oc, b=breg: s.activation(
                XBF[:, 512 * oc:512 * oc + 512], b, AF.Identity,
                bias=BIA[:, 16 + oc:17 + oc]), [bkey, "BIA", "XBF"], ["XBF"])

        def vt_build(i, src_buf, src_key, nsl=NSL):
            for m in range(4):
                sl = (4 * i + m) % nsl
                for dc in range(C):
                    tpose(PT[:, 128 * dc:128 * dc + 128],
                          src_buf[:, 512 * dc + 128 * m:
                                  512 * dc + 128 * m + 128],
                          [src_key], ["pt"] if dc == C - 1 else ["pt_"])
                dst = bass.AP(VT, sl * (H * 65),
                              [[NSL * H * 65, 128], [65, H], [1, 64]])
                vop(lambda s, d=dst: s.tensor_copy(
                    d, bass.AP(PT, 0, [[1024, 128], [64, 16], [1, 64]])),
                    ["pt", "pt_", "VT"], ["VT"])

        def attention(j):
            for ql in range(4):
                qb = 4 * j + ql
                variant = 0 if qb == 0 else (2 if qb == NKB - 1 else 1)
                if qb == 0:
                    for par in range(2):
                        vop(lambda s, par=par: s.memset(
                            SC[:, 512 * par:512 * par + 128], 0.0),
                            [f"sc{par}"], [f"sc{par}"])
                if qb == NKB - 1:
                    for par in range(2):
                        vop(lambda s, par=par: s.memset(
                            SC[:, 512 * par + 256:512 * par + 384], 0.0),
                            [f"sc{par}"], [f"sc{par}"])

                def scores(h):
                    par = h % 2
                    po = (h % 2) * 64
                    hc = h // 2
                    for jj, kb in enumerate((qb - 1, qb, qb + 1)):
                        if 0 <= kb < NKB:
                            sl = kb % NSL
                            lh = bass.AP(KR, (sl * 8 + hc) * 128
                                         + po * (NSL * 1024),
                                         [[NSL * 1024, 64], [1, 128]])
                            rh = bass.AP(QR0, hc * 512 + ql * 128 + po * 4096,
                                         [[4096, 64], [1, 128]])
                            mm(SC[:, 512 * par + 128 * jj:
                                  512 * par + 128 * jj + 128], lh, rh,
                               True, True, ["KR", "QR0"], [f"sc{par}"])

                def attend(h):
                    par = h % 2
                    avreg, avkey = ((AVQ[:, 0:512], "av0") if h % 2 == 0
                                    else (AVQ[:, 512:1024], "av1"))
                    sop(lambda s, par=par: s.activation(
                        WMT[:, :], SC[:, 512 * par:512 * par + 384],
                        AF.Exp, bias=CST[:, 0:1]),
                        [f"sc{par}", "WMT", "CST"], ["WMT"])
                    vop(lambda s, v=variant: s.tensor_tensor(
                        WMT2[:, :], WMT[:, :], MSK[:, 384 * v:384 * v + 384],
                        ALU.mult), ["WMT", "MSK", "WMT2"], ["WMT2"])
                    for jj, kb in enumerate((qb - 1, qb, qb + 1)):
                        sl = kb % NSL
                        rh = bass.AP(VT, sl * (H * 65) + h * 65,
                                     [[NSL * H * 65, 128], [1, 65]])
                        mm(avreg[:, 0:65], WMT2[:, 128 * jj:128 * jj + 128],
                           rh, jj == 0, jj == 2, ["WMT2", "VT"], [avkey])
                    vop(lambda s, h=h, a=avreg: s.tensor_copy(
                        RST[:, h:h + 1], a[:, 64:65]), [avkey, "RST"],
                        ["RST"])
                    vop(lambda s, h=h, a=avreg: s.tensor_copy(
                        AOtm[:, 64 * h:64 * h + 64], a[:, 0:64]),
                        [avkey, "AOtm"], ["AOtm"])

                for h in range(H + 1):
                    if h < H:
                        scores(h)
                    if h >= 1:
                        attend(h - 1)
                vop(lambda s: s.reciprocal(RSI[:, :], RST[:, :]),
                    ["RST", "RSI"], ["RSI"], small=["RST"])
                for h in range(H):
                    vop(lambda s, h=h: s.tensor_scalar(
                        AOtm[:, 64 * h:64 * h + 64],
                        AOtm[:, 64 * h:64 * h + 64], RSI[:, h:h + 1], None,
                        ALU.mult), ["AOtm", "RSI"], ["AOtm"], small=["RSI"])
                for dc in range(C):
                    tpose(PT[:, 128 * dc:128 * dc + 128],
                          AOtm[:, 128 * dc:128 * dc + 128], ["AOtm"],
                          ["pt"] if dc == C - 1 else ["pt_"])
                vop(lambda s, ql=ql: s.tensor_copy(
                    bass.AP(AO, ql * 128, [[4096, 128], [512, 8], [1, 128]]),
                    PT[:, :]), ["pt", "pt_", "AO"], ["AO"])

        evac_x1 = evac_resid(xfm, x1d, 3, "x1")
        for i in range(0 if SKIP123 else NT + 1):
            if i < NT:
                make_xs(i, xfm)
                proj_ln(i, 8192, 1, evac_k(i))
                proj_ln(i, 16384, 2, evac_v)
                vt_build(i, XBF, "XBF")
            if i >= 1:
                attention(i - 1)
                proj_plain(24576, AO, "AO", evac_x1(i - 1))
            if i < NT:
                proj_ln(i, 0, 0, evac_q)

        if PHASES >= 2 and not SKIP123:
            load_weights([l1w["q"], l1w["v"], l1w["g"], l1w["o"]])
            build_negcs([(WB, 0)])
            sweep_stats(x1d, "x1")
            evac_x2 = evac_resid(x1d, x2d, 4, "x2", "x1")

            def proj_wf(i, evac):
                for oc in range(C):
                    breg, bkey = (PA, "pa") if oc % 2 == 0 else (PB, "pb")
                    for dc in range(C):
                        mm(breg, WB[:, 1024 * dc + 128 * oc:
                                   1024 * dc + 128 * oc + 128],
                           XS[:, T * dc:T * dc + T],
                           dc == 0, False, ["WB", "XS"], [bkey])
                    mm(breg, NCS[0:1, 128 * oc:128 * oc + 128],
                       RXT[0:1, 0:T], False, True, ["NCS", "RXT"], [bkey])
                    evac(oc, breg, bkey)

            for i in range(NT):
                make_xs(i, x1d, "x1")
                proj_wf(i, evac_x2(i))

        if PHASES >= 3 and not SKIP123:
            build_negcs([(WA, 0), (WA, 8192), (WA, 16384)])
            sweep_stats(x2d, "x2")

            def evac_qr0(oc, breg, bkey):
                sop(lambda s, oc=oc, b=breg: s.activation(
                    QR0[:, 512 * oc:512 * oc + 512], b, AF.Identity),
                    [bkey, "QR0"], ["QR0"])

            def evac_gate(i):
                def f(oc, breg, bkey):
                    sop(lambda s, oc=oc, b=breg: s.activation(
                        AO[:, 512 * oc:512 * oc + 512], b, AF.Silu),
                        [bkey, "AO"], ["AO"])
                    dma("G", bass.AP(gscr, oc * 128 * S + i * T,
                                     [[S, 128], [1, T]]),
                        AO[:, 512 * oc:512 * oc + 512], ["AO"],
                        ["AO", f"g@{i}"])
                return f

            def evac_v3(oc, breg, bkey):
                sop(lambda s, oc=oc, b=breg: s.activation(
                    XBF[:, 512 * oc:512 * oc + 512], b, AF.Identity),
                    [bkey, "XBF"], ["XBF"])

            for i in range(NT):
                make_xs(i, x2d, "x2")
                proj_ln(i, 0, 0, evac_qr0)
                proj_ln(i, 16384, 2, evac_gate(i))
                proj_ln(i, 8192, 1, evac_v3)
                for tb in range(4 if P3RET else 0):
                    gtb = 4 * i + tb
                    STc = ST0 if gtb % 2 == 0 else ST1
                    STp = ST1 if gtb % 2 == 0 else ST0
                    ckey = f"ST{gtb % 2}"
                    pkey = f"ST{(gtb + 1) % 2}"
                    slv = tb % 2
                    for dc in range(C):
                        tpose(PT[:, 128 * dc:128 * dc + 128],
                              XBF[:, 512 * dc + 128 * tb:
                                  512 * dc + 128 * tb + 128], ["XBF"],
                              ["pt"] if dc == C - 1 else ["pt_"])
                    vop(lambda s, slv=slv: s.tensor_copy(
                        KR[:, slv * 1024:slv * 1024 + 1024], PT[:, :]),
                        ["pt", "pt_", "KR"], ["KR"])
                    for c4 in range(8):
                        mm(SC[:, 0:128],
                           KR[:, slv * 1024 + 128 * c4:
                              slv * 1024 + 128 * c4 + 128],
                           LMAT[:, 128 * (2 * c4):128 * (2 * c4) + 128],
                           True, True, ["KR", "LMAT"], ["sc0"])
                        mm(SC[:, 512:640],
                           KR[:, slv * 1024 + 128 * c4:
                              slv * 1024 + 128 * c4 + 128],
                           LMAT[:, 128 * (2 * c4 + 1):
                                128 * (2 * c4 + 1) + 128],
                           True, True, ["KR", "LMAT"], ["sc1"])
                        vop(lambda s, c=c4, sp=STp, sc_=STc:
                            s.scalar_tensor_tensor(
                                sc_[0:64, 128 * c:128 * c + 128],
                                PWR[0:64, 128 * c:128 * c + 128],
                                sp[0:64, 128 * c + 127:128 * c + 128],
                                SC[0:64, 0:128], ALU.mult, ALU.add),
                            ["PWR", pkey, "sc0", ckey], [ckey])
                        vop(lambda s, c=c4, sp=STp, sc_=STc:
                            s.scalar_tensor_tensor(
                                sc_[64:128, 128 * c:128 * c + 128],
                                PWR[64:128, 128 * c:128 * c + 128],
                                sp[64:128, 128 * c + 127:128 * c + 128],
                                SC[64:128, 512:640], ALU.mult, ALU.add),
                            ["PWR", pkey, "sc1", ckey], [ckey])
                        vop(lambda s, c=c4, sc_=STc, tb=tb:
                            s.tensor_tensor(
                                AO[:, 512 * c + 128 * tb:
                                   512 * c + 128 * tb + 128],
                                sc_[:, 128 * c:128 * c + 128],
                                QR0[:, 512 * c + 128 * tb:
                                    512 * c + 128 * tb + 128], ALU.mult),
                            [ckey, "QR0", "AO"], ["AO"])
                vop(lambda s: s.tensor_tensor(XS[:, :], AO[:, :], AO[:, :],
                                              ALU.mult), ["AO", "XS"], ["XS"])
                for dc in range(C):
                    mm(AVQ[0:16, 0:512], OH[:, 16 * i:16 * i + 16],
                       AO[:, T * dc:T * dc + T], i == 0 and dc == 0,
                       i == NT - 1 and dc == C - 1, ["AO", "OH"], ["av0"])
                for dc in range(C):
                    mm(AVQ[0:16, 512:1024], OH[:, 16 * i:16 * i + 16],
                       XS[:, T * dc:T * dc + T], i == 0 and dc == 0,
                       i == NT - 1 and dc == C - 1, ["XS", "OH"], ["av1"])
                dma("G", bass.AP(yscr, i * T,
                                 [[S, 128], [128 * S, 8], [1, T]]),
                    AO[:, :], ["AO"], [f"y@{i}"])

            rowmath("av0", AVQ[0:16, 0:512], "av1", AVQ[0:16, 512:1024])
            evac_x3 = evac_resid(x2d, x3d, None, "x3", "x2")
            for i in range(NT if P3SWC else 0):
                dma("G", AO[:, :],
                    bass.AP(yscr, i * T, [[S, 128], [128 * S, 8], [1, T]]),
                    ["AO", f"y@{i}"], ["AO"])
                dma("G", QR0[:, :],
                    bass.AP(gscr, i * T, [[S, 128], [128 * S, 8], [1, T]]),
                    ["QR0", f"g@{i}"], ["QR0"])
                extract_row_f(i, 0, 0)
                extract_row_f(i, T, T)
                bcast_row(SC[:, 0:T], "sc0", 0)
                bcast_row(SC[:, 512:512 + T], "sc1", T)
                for dc in range(C):
                    vop(lambda s, dc=dc: s.tensor_tensor(
                        XBF[:, T * dc:T * dc + T], AO[:, T * dc:T * dc + T],
                        SC[:, 0:T], ALU.add), ["AO", "sc0", "XBF"], ["XBF"])
                    vop(lambda s, dc=dc: s.scalar_tensor_tensor(
                        XS[:, T * dc:T * dc + T], XBF[:, T * dc:T * dc + T],
                        BIA[:, 40 + dc:41 + dc], SC[:, 512:512 + T],
                        ALU.mult, ALU.mult), ["XBF", "BIA", "sc1", "XS"],
                        ["XS"])
                    vop(lambda s, dc=dc: s.scalar_tensor_tensor(
                        XBF[:, T * dc:T * dc + T], XS[:, T * dc:T * dc + T],
                        BIA[:, 48 + dc:49 + dc],
                        QR0[:, T * dc:T * dc + T], ALU.add, ALU.mult),
                        ["XS", "BIA", "QR0", "XBF"], ["XBF"])
                proj_plain(24576, XBF, "XBF", evac_x3(i))

        if PHASES >= 4:
            for oc in range(8 if P4A >= 1 else 0):
                dma("Y", UF[:, 0:1024],
                    bass.AP(l1rhsT, oc * 128 * 1024, [[1024, 128], [1, 1024]]),
                    ["UF0"], ["UF0"])
                vop(lambda s: s.reduce_max(SMC[:, 0:1], UF[:, 0:1024],
                                           axis=AX.X,
                                           apply_absolute_value=True),
                    ["UF0", "SMC"], ["SMC"])
                vop(lambda s, oc=oc: s.tensor_scalar(
                    SMC2[:, oc:oc + 1], SMC[:, 0:1], 1e-12, 1.0 / 127.0,
                    ALU.max, ALU.mult), ["SMC", "SMC2"], ["SMC2"],
                    small=["SMC"])
                vop(lambda s, oc=oc: s.reciprocal(SMC[:, 1:2],
                                                  SMC2[:, oc:oc + 1]),
                    ["SMC2", "SMC"], ["SMC"], small=["SMC2"])
                vop(lambda s: s.tensor_scalar(
                    OX[:, 0:1024], UF[:, 0:1024], SMC[:, 1:2], 128.5,
                    ALU.mult, ALU.add), ["UF0", "SMC", "OX0", "OX1"],
                    ["OX0", "OX1"], small=["SMC"])
                for lb in range(8):
                    b = float(128 >> lb)
                    src, sk = (OX[:, 0:1024], ["OX0", "OX1"]) if lb == 0 \
                        else (UF[:, 0:1024], ["UF0"])
                    vop(lambda s, b=b, src=src: s.tensor_scalar(
                        ST1[:, :], src, b, -b, ALU.is_ge, ALU.mult),
                        sk + ["ST1"], ["ST1"])
                    vop(lambda s, src=src: s.tensor_tensor(
                        UF[:, 0:1024], src, ST1[:, :], ALU.add),
                        sk + ["ST1", "UF0"], ["UF0"])
                dst = XS if oc < 4 else XBF
                dk = "XS" if oc < 4 else "XBF"
                cb = (oc % 4) * 1024
                vop(lambda s, d=dst, cb=cb: s.scalar_tensor_tensor(
                    d[:, cb:cb + 1024], OX[:, 0:1024], 128.0, UF[:, 0:1024],
                    ALU.subtract, ALU.subtract),
                    ["OX0", "OX1", "UF0", dk], [dk])
            if P4A >= 2:
                tpose(PQ[0:8, 0:128], SMC2[:, 0:8], ["SMC2"], ["pa"],
                      f32=True)
                vop(lambda s: s.tensor_copy(ROWSF[0:8, 0:128],
                                            PQ[0:8, 0:128]),
                    ["pa", "ROWSF"], ["ROWSF"])
            for oc in range(8 if P4A >= 2 else 0):
                mm(PQ[0:1, 512:640], IDF[0:8, oc:oc + 1],
                   ROWSF[0:8, 0:128], True, True, ["IDB", "ROWSF"], ["pb"])
                vop(lambda s: s.tensor_copy(RXF[0:1, 0:128],
                                            PQ[0:1, 512:640]),
                    ["pb", "RXF"], ["RXF"])
                mm(PQ[:, 512:640], ON16F[0:1, :], RXF[0:1, 0:128],
                   True, True, ["ON16F", "RXF"], ["pb"])
                vop(lambda s, oc=oc: s.tensor_copy(
                    RSB[:, 128 * oc:128 * oc + 128], PQ[:, 512:640]),
                    ["pb", "RSB"], ["RSB"])
            for oc in range(8 if P4A >= 3 else 0):
                src_t = XS if oc < 4 else XBF
                sk = "XS" if oc < 4 else "XBF"
                cb = (oc % 4) * 1024
                for kc in range(8):
                    tpose(PT[:, 128 * kc:128 * kc + 128],
                          src_t[:, cb + 128 * kc:cb + 128 * kc + 128],
                          [sk], ["pt"] if kc == C - 1 else ["pt_"])
                vop(lambda s, oc=oc: s.tensor_copy(
                    bass.AP(WB, 128 * oc, [[8192, 128], [1024, 8], [1, 128]]),
                    PT[:, :]), ["pt", "pt_", "WB"], ["WB"])

            if P4B:
                sweep_stats(x3d, "x3")
            for pk, (roff, cbase) in enumerate([(0, 0), (T, 64)]
                                               if P4B else []):
                for q4 in range(4):
                    tpose(PQ[:, 0:16],
                          ROWSF[0:16, roff + 128 * q4:roff + 128 * q4 + 128],
                          ["ROWSF"], ["pa"], f32=True)
                    vop(lambda s, q4=q4, cb=cbase: s.tensor_copy(
                        bass.AP(COLS, cb + q4, [[128, 128], [4, 16]]),
                        PQ[:, 0:16]), ["pa", "COLS"], ["COLS"])

            for tb in range(NKB if P4C else 0):
                sl = tb % 2
                xfr = XF[:, 1024 * sl:1024 * sl + 1024]
                xfk = f"XFH{sl}"
                dma("Y", xfr,
                    bass.AP(x3d, tb * 128,
                            [[S, 128], [128 * S, 8], [1, 128]]),
                    [xfk, f"x3@{tb // 4}"], [xfk])
                for dc in range(C):
                    tpose(PQ[:, 128 * dc:128 * dc + 128],
                          xfr[:, 128 * dc:128 * dc + 128], [xfk],
                          ["pa"] if dc < 4 else ["pb"], f32=True)
                sop(lambda s, tb=tb: s.activation(
                    UF[:, 0:1024], PQ[:, :], AF.Identity,
                    bias=COLS[:, tb:tb + 1]),
                    ["pa", "pb", "COLS", "UF0"], ["UF0"])
                vop(lambda s: s.reduce_max(SMC[:, 0:1], UF[:, 0:1024],
                                           axis=AX.X,
                                           apply_absolute_value=True),
                    ["UF0", "SMC"], ["SMC"])
                vop(lambda s, tb=tb: s.tensor_scalar(
                    SMC[:, 1:2], SMC[:, 0:1], COLS[:, 64 + tb:65 + tb],
                    None, ALU.mult), ["SMC", "COLS"], ["SMC"], small=["SMC"])
                vop(lambda s: s.tensor_scalar(
                    SMC[:, 2:3], SMC[:, 1:2], 1e-12, 1.0 / 127.0, ALU.max,
                    ALU.mult), ["SMC", "LSC"], ["LSC"], small=["SMC"])
                vop(lambda s: s.reciprocal(SMC[:, 3:4], SMC[:, 0:1]),
                    ["SMC", "RCQ"], ["RCQ"], small=["SMC"])
                vop(lambda s: s.tensor_scalar(SMC[:, 4:5], SMC[:, 3:4],
                                              127.0, None, ALU.mult),
                    ["RCQ", "RCQ2"], ["RCQ2"], small=["RCQ"])
                sop(lambda s: s.activation(
                    OX[:, 0:1024], UF[:, 0:1024], AF.Identity,
                    bias=CST[:, 1:2], scale=SMC[:, 4:5]),
                    ["UF0", "RCQ2", "OX0", "OX1", "CST"], ["OX0", "OX1"],
                    small=["RCQ2"])
                for lb in range(8):
                    b = float(128 >> lb)
                    src, sk = (OX[:, 0:1024], ["OX0", "OX1"]) if lb == 0 \
                        else (ST1[:, :], ["ST1"])
                    vop(lambda s, b=b, src=src: s.tensor_scalar(
                        ST0[:, :], src, b, -b, ALU.is_ge, ALU.mult),
                        sk + ["ST0"], ["ST0"])
                    vop(lambda s, src=src: s.tensor_tensor(
                        ST1[:, :], src, ST0[:, :], ALU.add),
                        sk + ["ST0", "ST1"], ["ST1"])
                vop(lambda s: s.scalar_tensor_tensor(
                    XBF[:, 0:1024], OX[:, 0:1024], 128.0, ST1[:, :],
                    ALU.subtract, ALU.subtract),
                    ["OX0", "OX1", "ST1", "XBF"], ["XBF"])
                for kc in range(C):
                    tpose(PT[:, 128 * kc:128 * kc + 128],
                          XBF[:, 128 * kc:128 * kc + 128], ["XBF"],
                          ["pt"] if kc == C - 1 else ["pt_"])
                vop(lambda s: s.tensor_copy(XS[:, 0:1024], PT[:, :]),
                    ["pt", "pt_", "XS"], ["XS"])
                for half in range(2):
                    sck = f"sc{half}"
                    for kc in range(C):
                        mm(SC[:, 512 * half:512 * half + 512],
                           XS[:, 128 * kc:128 * kc + 128],
                           WB[:, 1024 * kc + 512 * half:
                              1024 * kc + 512 * half + 512],
                           kc == 0, kc == C - 1, ["XS", "WB"], [sck])
                    okey = f"OX{half}"
                    vop(lambda s, half=half: s.scalar_tensor_tensor(
                        OX[:, 512 * half:512 * half + 512],
                        RSB[:, 512 * half:512 * half + 512], SMC[:, 2:3],
                        SC[:, 512 * half:512 * half + 512], ALU.mult,
                        ALU.mult), ["RSB", "LSC", sck, okey], [okey],
                        small=["LSC"])
                    vop(lambda s, half=half, tb=tb:
                        s.scalar_tensor_tensor(
                            OX[:, 512 * half:512 * half + 512],
                            UF[:, 512 * half:512 * half + 512],
                            COLS[:, tb:tb + 1],
                            OX[:, 512 * half:512 * half + 512],
                            ALU.subtract, ALU.add),
                        ["UF0", "COLS", okey], [okey])
                    dma("Y", bass.AP(ofm, tb * 128 * D + 512 * half,
                                     [[1024, 128], [1, 512]]),
                        OX[:, 512 * half:512 * half + 512], [okey], [okey])

        p.finish()

        with nc.Block() as block:
            @block.tensor
            def _(sec):
                for f in p.ops["T"]:
                    f(sec)

            @block.vector
            def _(sec):
                for f in p.ops["V"]:
                    f(sec)

            @block.scalar
            def _(sec):
                for f in p.ops["S"]:
                    f(sec)

            @block.sync
            def _(sec):
                for f in p.ops["Y"]:
                    f(sec)

            @block.gpsimd
            def _(sec):
                for f in p.ops["G"]:
                    f(sec)

    return nc


_CONSTS = None


def host_consts():
    global _CONSTS
    if _CONSTS is not None:
        return _CONSTS
    import ml_dtypes
    bf = ml_dtypes.bfloat16
    ident = np.eye(128, dtype=bf)
    oh = np.zeros((128, 256), dtype=bf)
    for i in range(16):
        oh[16 * 0:, 16 * i + i] = 0
    for i in range(16):
        oh[:, 16 * i + i] = 1
    ones16 = np.ones((16, 128), dtype=bf)
    ones16f = np.ones((16, 128), dtype=np.float32)
    k = np.arange(128)[:, None]
    q = np.arange(128)[None, :]
    mL = (k >= q).astype(bf)
    mR = (k <= q).astype(bf)
    mid = np.ones((128, 128), dtype=bf)
    zero = np.zeros((128, 128), dtype=bf)
    masks = np.concatenate([zero, mid, mR, mL, mid, mR, mL, mid, zero],
                           axis=1)
    s_ = np.arange(128)[:, None]
    t_ = np.arange(128)[None, :]
    diota = np.broadcast_to((t_ - s_).astype(np.float32), (128, 128))
    dnegb = np.where(t_ >= s_, 0.0, -1e9).astype(np.float32)
    dp1 = np.broadcast_to((t_ + 1).astype(np.float32), (128, 128))
    dtabs = np.ascontiguousarray(
        np.concatenate([diota, dnegb, dp1], axis=1).astype(np.float32))
    _CONSTS = {
        "identb": ident, "identf": np.eye(128, dtype=np.float32),
        "oh16": oh, "ones16": ones16, "ones16f": ones16f,
        "masks": np.ascontiguousarray(masks), "dtabs": dtabs,
    }
    return _CONSTS


def host_inputs(inputs, core):
    x = np.asarray(inputs["L0_attn_x"][core])
    cc = np.ascontiguousarray
    d = {
        "xfm": cc(x.T),
        "l0wq": cc(np.asarray(inputs["L0_attn_Wq"]).T),
        "l0wk": cc(np.asarray(inputs["L0_attn_Wk"]).T),
        "l0wv": cc(np.asarray(inputs["L0_attn_Wv"]).T),
        "l0wo": cc(np.asarray(inputs["L0_attn_Wo"]).T),
        "l0bq": np.asarray(inputs["L0_attn_bq"]),
        "l0bk": np.asarray(inputs["L0_attn_bk"]),
        "l0bv": np.asarray(inputs["L0_attn_bv"]),
        "l0bo": np.asarray(inputs["L0_attn_bo"]),
        "g1r": cc(np.asarray(inputs["L0_ffn_G1_R"])[0].transpose(1, 0, 2)
                  .reshape(4, 1024)),
        "g2r": cc(np.asarray(inputs["L0_ffn_G2_R"])[:, :, 0, :]
                  .reshape(4, 256)),
        "g1i": cc(np.asarray(inputs["L0_ffn_G1_I"])[0].transpose(1, 0, 2)
                  .reshape(4, 1024)),
        "g2i": cc(np.asarray(inputs["L0_ffn_G2_I"])[:, :, 0, :]
                  .reshape(4, 256)),
        "l0fb": cc(np.concatenate([np.asarray(inputs["L0_ffn_bias_real"]),
                                   np.asarray(inputs["L0_ffn_bias_imag"])])),
        "l1wq": cc(np.asarray(inputs["L1_attn_Wq"]).T),
        "l1wv": cc(np.asarray(inputs["L1_attn_Wv"]).T),
        "l1wg": cc(np.asarray(inputs["L1_attn_Wg"]).T),
        "l1wo": cc(np.asarray(inputs["L1_attn_Wo"]).T),
        "l1lng": np.asarray(inputs["L1_attn_ln_g"]),
        "l1lnb": np.asarray(inputs["L1_attn_ln_b"]),
        "l1decay": cc(np.asarray(inputs["L1_attn_decay"])[:, 1, 0]),
        "l1rhsT": cc(np.asarray(inputs["L1_ffn_rhs"]).T),
    }
    d.update(host_consts())
    return d


def kernel(**inputs):
    nc = build()
    in_maps = [host_inputs(inputs, core) for core in range(8)]
    res = run_bass_kernel_spmd(nc, in_maps, core_ids=list(range(8)))
    outs = []
    for core in range(8):
        r = res.results[core]
        if PHASES == 4:
            outs.append(np.asarray(r["ofm"]))
        else:
            key = {1: "x1d", 2: "x2d", 3: "x3d"}[PHASES]
            outs.append(np.asarray(r[key]).T)
    return np.stack(outs, axis=0)


# version 09 (final)
# speedup vs baseline: 1.1243x; 1.0371x over previous
import numpy as np
import concourse.bass as bass
import concourse.mybir as mybir
from concourse.bass_utils import run_bass_kernel_spmd
from contextlib import ExitStack

F32 = mybir.dt.float32
BF16 = mybir.dt.bfloat16
AF = mybir.ActivationFunctionType
ALU = mybir.AluOpType
AX = mybir.AxisListType

S = 8192
D = 1024
T = 512
NT = S // T
C = 8
H = 16
NKB = S // 128
NSL = 9
RCONST = 1.5 * 2.0 ** 23
EXPB = -3.0

PHASES = 4
P3RET = True
P3SWC = True
P4A = 3
P4B = True
P4C = True
SKIP123 = False


class Prog:
    ENG = ("T", "V", "S", "Y", "G")
    NPOOL = 12

    def __init__(self, nc, es):
        self.nc = nc
        self.ops = {e: [] for e in self.ENG}
        self.semkeys = ["T", "V", "S"] + \
            [f"Y{i}" for i in range(self.NPOOL)] + \
            [f"G{i}" for i in range(self.NPOOL)]
        self.sem = {k: es.enter_context(nc.semaphore(f"sem{k}"))
                    for k in self.semkeys}
        self.cnt = {k: 0 for k in self.semkeys}
        self.dma_idx = {"Y": 0, "G": 0}
        self.res = {}
        self.waited = {e: {k: 0 for k in self.semkeys} for e in self.ENG}
        self.icnt = {e: 0 for e in self.ENG}
        self.drained = {e: 0 for e in self.ENG}

    def _add_waits(self, eng, reads, writes, need):
        def add(k2, c2):
            if k2 == eng:
                return
            need[k2] = max(need.get(k2, 0), c2)

        for r in reads:
            st = self.res.get(r)
            if st and st["w"]:
                add(*st["w"])
        for w in writes:
            st = self.res.get(w)
            if st:
                if st["w"]:
                    add(*st["w"])
                for (k2, c2) in st["r"]:
                    add(k2, c2)

    def op(self, eng, fn, reads=(), writes=(), dma=False, small=()):
        reads = list(reads)
        writes = list(writes)
        need = {}
        self._add_waits(eng, reads, writes, need)
        if dma:
            semkey = f"{eng}{self.dma_idx[eng] % self.NPOOL}"
            self.dma_idx[eng] += 1
            if self.cnt[semkey] > 0:
                need[semkey] = max(need.get(semkey, 0), self.cnt[semkey])
            inc = 16
        else:
            semkey = eng
            inc = 1
        waits = []
        for k2, c2 in sorted(need.items()):
            if self.waited[eng][k2] < c2:
                self.waited[eng][k2] = c2
                waits.append((self.sem[k2], c2))
        drain = False
        for r in small:
            st = self.res.get(r)
            if st and st["w"] and st["w"][0] == eng:
                wc = st["w"][1]
                if self.icnt[eng] - wc < 12 and wc > self.drained[eng]:
                    drain = True
        if drain:
            self.drained[eng] = self.icnt[eng]
        self.cnt[semkey] += inc
        newc = self.cnt[semkey]
        if not dma:
            self.icnt[eng] = newc
        sem = self.sem[semkey]

        def emit(sec, waits=waits, fn=fn, drain=drain, sem=sem, inc=inc):
            for s_, v_ in waits:
                sec.wait_ge(s_, v_)
            if drain:
                sec.drain()
            fn(sec).then_inc(sem, inc)

        self.ops[eng].append(emit)
        for w in writes:
            self.res[w] = {"w": (semkey, newc), "r": []}
        for r in reads:
            st = self.res.setdefault(r, {"w": None, "r": []})
            st["r"].append((semkey, newc))
        return newc

    def finish(self):
        for k in self.semkeys:
            v = self.cnt[k]
            if v:
                def emit(sec, s_=self.sem[k], v_=v):
                    sec.wait_ge(s_, v_)
                self.ops["Y"].append(emit)


RACECHECK = False


def build():
    nc = bass.Bass("TRN2", target_bir_lowering=False,
                   detect_race_conditions=RACECHECK)

    def din(name, shape, dt=F32):
        return nc.dram_tensor(name, shape, dt, kind="ExternalInput")

    xfm = din("xfm", [D, S])
    l0w = {k: din(f"l0w{k}", [D, D]) for k in "qkvo"}
    l0b = {k: din(f"l0b{k}", [D]) for k in "qkvo"}
    g1r = din("g1r", [4, 1024]); g2r = din("g2r", [4, 256])
    g1i = din("g1i", [4, 1024]); g2i = din("g2i", [4, 256])
    l0fb = din("l0fb", [D])
    l1w = {k: din(f"l1w{k}", [D, D]) for k in "qvgo"}
    l1lng = din("l1lng", [D]); l1lnb = din("l1lnb", [D])
    l1decay = din("l1decay", [16])
    l1rhsT = din("l1rhsT", [D, D])
    identb = din("identb", [128, 128], BF16)
    identf = din("identf", [128, 128])
    oh16 = din("oh16", [128, 16 * 16], BF16)
    ones16 = din("ones16", [16, 128], BF16)
    ones16f = din("ones16f", [16, 128])
    masks = din("masks", [128, 3 * 384], BF16)
    dtabs = din("dtabs", [128, 384])

    ofm = nc.dram_tensor("ofm", [S, D], F32, kind="ExternalOutput")

    def scratch(name, k):
        kind = "ExternalOutput" if PHASES == k else "Internal"
        return nc.dram_tensor(name, [D, S], F32, kind=kind)

    x1d = scratch("x1d", 1)
    x2d = scratch("x2d", 2)
    x3d = scratch("x3d", 3)
    yscr = nc.dram_tensor("yscr", [D, S], BF16)
    gscr = nc.dram_tensor("gscr", [D, S], BF16)
    pscr = nc.dram_tensor("pscr", [2 * 1024 * 256], BF16)

    es = ExitStack()
    sbt = lambda n, sh, dt: es.enter_context(nc.sbuf_tensor(n, sh, dt))
    with es:
        WA = sbt("WA", [128, 4 * 8192], BF16)
        WB = sbt("WB", [128, 8192], BF16)
        KR = sbt("KR", [128, NSL * 1024], BF16)
        VT = sbt("VT", [128, NSL * (H * 65)], BF16)
        XF = sbt("XF", [128, 2048], F32)
        XBF = sbt("XBF", [128, 4096], BF16)
        XS = sbt("XS", [128, 4096], BF16)
        QR0 = sbt("QR0", [128, 4096], BF16)
        AO = sbt("AO", [128, 4096], BF16)
        AOtm = sbt("AOtm", [128, 1024], BF16)
        UF = sbt("UF", [128, 1024], F32)
        OX = sbt("OX", [128, 1024], F32)
        ST0 = sbt("ST0", [128, 1024], F32)
        ST1 = sbt("ST1", [128, 1024], F32)
        LMAT = sbt("LMAT", [128, H * 128], BF16)
        PWR = sbt("PWR", [128, 1024], F32)
        WMT = sbt("WMT", [128, 384], BF16)
        WMT2 = sbt("WMT2", [128, 384], BF16)
        MSK = sbt("MSK", [128, 3 * 384], BF16)
        IDB = sbt("IDB", [128, 128], BF16)
        IDF = sbt("IDF", [128, 128], F32)
        OH = sbt("OH", [128, 256], BF16)
        ON16 = sbt("ON16", [16, 128], BF16)
        ON16F = sbt("ON16F", [16, 128], F32)
        ONESC = sbt("ONESC", [128, 1], BF16)
        NCS = sbt("NCS", [1, 3 * 1024], BF16)
        ROWS = sbt("ROWS", [16, T], BF16)
        RXT = sbt("RXT", [1, 2 * T], BF16)
        RXF = sbt("RXF", [1, 128], F32)
        CST = sbt("CST", [128, 4], F32)
        ROWSF = sbt("ROWSF", [16, 2 * T], F32)
        COLS = sbt("COLS", [128, 128], F32)
        BIA = sbt("BIA", [128, 64], F32)
        RST = sbt("RST", [128, 16], F32)
        RSI = sbt("RSI", [128, 16], F32)
        SMC = sbt("SMC", [128, 8], F32)
        SMC2 = sbt("SMC2", [128, 8], F32)
        LNC = sbt("LNC", [128, 16], F32)
        PRC = sbt("PRC", [128, 8], F32)
        DEC = sbt("DEC", [16, 2], F32)
        LNR = sbt("LNR", [1, 16], F32)
        RSB = sbt("RSB", [128, 1024], F32)

        PQ = es.enter_context(nc.psum_tensor("PQ", [128, 1024], F32))
        SC = es.enter_context(nc.psum_tensor("SCP", [128, 1536], F32))
        AVQ = es.enter_context(nc.psum_tensor("AVQ", [128, 1024], F32))
        PT = es.enter_context(nc.psum_tensor("PTP", [128, 1024], BF16))

        p = Prog(nc, es)
        ident = IDB[:, :]

        def dma(q, dst, src, reads, writes, slow=False):
            p.op(q, lambda s, d=dst, r=src, sl=slow: s.dma_start(
                d, r, allow_slow_non_contiguous=sl),
                 reads=reads, writes=writes, dma=True)

        def mm(out, lh, rh, start, stop, reads, writes):
            p.op("T", lambda s, o=out, l=lh, r=rh, a=start, b=stop:
                 s.matmul(o, l, r, start=a, stop=b, skip_group_check=True),
                 reads=reads, writes=writes)

        def tpose(out, in_, reads, writes, f32=False):
            pc = in_.ap[0][1]
            idap = (IDF if f32 else IDB)[0:pc, 0:pc]
            p.op("T", lambda s, o=out, i=in_, d=idap: s.transpose(o, i, d),
                 reads=list(reads) + ["IDB"], writes=writes)

        vop = lambda fn, reads=(), writes=(), small=(): p.op("V", fn, reads, writes, small=small)
        sop = lambda fn, reads=(), writes=(), small=(): p.op("S", fn, reads, writes, small=small)

        PA = PQ[:, 0:512]
        PB = PQ[:, 512:1024]

        dma("Y", IDB[:, :], identb.ap(), [], ["IDB"])
        dma("Y", IDF[:, :], identf.ap(), [], ["IDB"])
        dma("Y", MSK[:, :], masks.ap(), [], ["MSK"])
        dma("Y", OH[:, :], oh16.ap(), [], ["OH"])
        dma("Y", ON16[:, :], ones16.ap(), [], ["ON16"])
        dma("Y", ON16F[:, :], ones16f.ap(), [], ["ON16F"])
        dma("Y", ST0[:, 0:384], dtabs.ap(), [], ["DTB"])
        for j, t in enumerate([l0b["q"], l0b["k"], l0b["v"], l0b["o"], l0fb,
                               l1lng, l1lnb]):
            dma("Y", BIA[:, 8 * j:8 * j + 8],
                bass.AP(t, 0, [[1, 128], [128, 8]]), ["BIA"], ["BIA"],
                slow=True)
        vop(lambda s: s.tensor_scalar(BIA[:, 0:8], BIA[:, 0:8], 0.125, None,
                                      ALU.mult), ["BIA"], ["BIA"])
        vop(lambda s: s.memset(ONESC[:, :], 1.0), [], ["ONESC"])
        vop(lambda s: s.memset(CST[:, 0:1], EXPB), ["CST"], ["CST"])
        vop(lambda s: s.memset(CST[:, 1:2], 128.5), ["CST"], ["CST"])
        vop(lambda s: s.memset(KR[:, :], 0.0), [], ["KR"])
        vop(lambda s: s.memset(
            bass.AP(VT, 0, [[NSL * H * 65, 128], [65, NSL * H], [1, 64]]),
            0.0), [], ["VT"])
        vop(lambda s: s.memset(
            bass.AP(VT, 64, [[NSL * H * 65, 128], [65, NSL * H]]), 1.0),
            ["VT"], ["VT"])

        dma("Y", DEC[:, 0:1], bass.AP(l1decay, 0, [[1, 16], [16, 1]]),
            ["DEC"], ["DEC"], slow=True)
        vop(lambda s: s.tensor_scalar(DEC[:, 0:1], DEC[:, 0:1], 1e-30, None,
                                      ALU.max), ["DEC"], ["DEC"])
        sop(lambda s: s.activation(DEC[:, 1:2], DEC[:, 0:1], AF.Ln),
            ["DEC"], ["DECL"])
        tpose(PQ[0:1, 0:16], DEC[:, 1:2], ["DECL"], ["pa"], f32=True)
        vop(lambda s: s.tensor_copy(LNR[:, :], PQ[0:1, 0:16]), ["pa"], ["LNR"])
        for h in range(H):
            mm(PQ[:, 0:1], ON16F[0:1, :], LNR[:, h:h + 1], True, True,
               ["ON16F", "LNR"], ["pa"])
            vop(lambda s, h=h: s.tensor_copy(LNC[:, h:h + 1], PQ[:, 0:1]),
                ["pa", "LNC"], ["LNC"])
        for h in range(H):
            vop(lambda s, h=h: s.scalar_tensor_tensor(
                OX[:, 0:128], ST0[:, 0:128], LNC[:, h:h + 1],
                ST0[:, 128:256], ALU.mult, ALU.add),
                ["DTB", "LNC", "OX0"], ["OX0"], small=["LNC"])
            sop(lambda s, h=h: s.activation(LMAT[:, 128 * h:128 * h + 128],
                                            OX[:, 0:128], AF.Exp),
                ["OX0"], ["LMAT"])
        for c4 in range(8):
            vop(lambda s, c=c4: s.tensor_copy(PRC[0:64, c:c + 1],
                                              LNC[0:64, 2 * c:2 * c + 1]),
                ["LNC", "PRC"], ["PRC"])
            vop(lambda s, c=c4: s.tensor_copy(PRC[64:128, c:c + 1],
                                              LNC[64:128, 2 * c + 1:2 * c + 2]),
                ["LNC", "PRC"], ["PRC"])
            sop(lambda s, c=c4: s.activation(PWR[:, 128 * c:128 * c + 128],
                                             ST0[:, 256:384], AF.Exp,
                                             scale=PRC[:, c:c + 1]),
                ["DTB", "PRC"], ["PWR"], small=["PRC"])
        vop(lambda s: s.memset(ST0[:, :], 0.0), [], ["ST0", "DTB"])
        vop(lambda s: s.memset(ST1[:, :], 0.0), [], ["ST1"])

        for part, (gg1, gg2) in enumerate([(g1r, g2r), (g1i, g2i)]):
            dma("Y", UF[0:4, 0:1024], gg1.ap(), ["UF0"], ["UF0"])
            dma("Y", OX[0:4, 0:256], gg2.ap(), ["OX0"], ["OX0"])
            for mb in range(8):
                mm(PQ[:, 0:256], UF[0:4, 128 * mb:128 * mb + 128],
                   OX[0:4, 0:256], True, True, ["UF0", "OX0"], ["pa"])
                vop(lambda s: s.tensor_copy(
                    XBF[:, 0:256],
                    bass.AP(PQ, 0, [[1024, 128], [1, 16], [16, 16]])),
                    ["pa", "KRS"], ["KRS"])
                dma("Y", bass.AP(pscr, (part * 1024 + 128 * mb) * 256,
                                 [[256, 128], [1, 256]]), XBF[:, 0:256],
                    ["KRS", "XBF"], [f"pscr{part}_{mb}"])
        pdeps = [f"pscr{i}_{m}" for i in range(2) for m in range(8)]

        def wf_reorder(dst_col_base):
            for q4 in range(4):
                for dl in range(8):
                    src_r = bass.AP(pscr, (8 * q4 + dl) * 256,
                                    [[16, 16], [8192, 32], [1, 16]])
                    src_i = bass.AP(pscr, 1024 * 256 + (8 * q4 + dl) * 256,
                                    [[16, 16], [8192, 32], [1, 16]])
                    rows = (16 * dl, 16 * dl + 16)
                    dma("Y", WB[rows[0]:rows[1], 1024 * q4:1024 * q4 + 512],
                        src_r, pdeps + ["WB"], ["WB"], slow=True)
                    dma("Y", WB[rows[0]:rows[1],
                               1024 * q4 + 512:1024 * q4 + 1024],
                        src_i, pdeps + ["WB"], ["WB"], slow=True)
                    dma("Y", WB[rows[0]:rows[1],
                               1024 * (4 + q4) + 512:1024 * (4 + q4) + 1024],
                        src_r, pdeps + ["WB"], ["WB"], slow=True)
                    dma("Y", WB[rows[0]:rows[1],
                               1024 * (4 + q4):1024 * (4 + q4) + 512],
                        src_i, pdeps + ["WB"], ["WB"], slow=True)
            for q4 in range(4):
                vop(lambda s, q=q4: s.tensor_scalar(
                    WB[:, 1024 * (4 + q):1024 * (4 + q) + 512],
                    WB[:, 1024 * (4 + q):1024 * (4 + q) + 512], -1.0, None,
                    ALU.mult), ["WB"], ["WB"])
        wf_reorder(0)

        def load_weights(tensors):
            for sl, t in enumerate(tensors):
                dma("G", WA[:, 8192 * sl:8192 * sl + 8192],
                    bass.AP(t, 0, [[1024, 128], [128 * 1024, 8], [1, 1024]]),
                    ["WA"], ["WA"])

        def build_negcs(slots):
            bk = {id(WA): "WA", id(WB): "WB"}
            for j, (buf, base) in enumerate(slots):
                for half in range(2):
                    for dc in range(C):
                        mm(PQ[0:1, 0:512], ONESC[:, :],
                           buf[:, base + 1024 * dc + 512 * half:
                               base + 1024 * dc + 512 * half + 512],
                           dc == 0, dc == C - 1, ["ONESC", bk[id(buf)]], ["pa"])
                    vop(lambda s, j=j, half=half: s.tensor_copy(
                        NCS[0:1, 1024 * j + 512 * half:
                            1024 * j + 512 * half + 512],
                        PQ[0:1, 0:512]), ["pa", "NCS"], ["NCS"])

        def sweep_stats(xsrc, tag=None):
            for i in range(NT):
                rd = ["XBF"] + ([f"{tag}@{i}"] if tag else [])
                dma("G", XBF[:, :],
                    bass.AP(xsrc, i * T, [[S, 128], [128 * S, 8], [1, T]]),
                    rd, ["XBF"])
                vop(lambda s: s.tensor_tensor(XS[:, :], XBF[:, :], XBF[:, :],
                                              ALU.mult), ["XBF", "XS"], ["XS"])
                for dc in range(C):
                    mm(PQ[0:16, 0:T], OH[:, 16 * i:16 * i + 16],
                       XBF[:, T * dc:T * dc + T], i == 0 and dc == 0,
                       i == NT - 1 and dc == C - 1, ["XBF", "OH"], ["pa"])
                for dc in range(C):
                    mm(PQ[0:16, 512:1024], OH[:, 16 * i:16 * i + 16],
                       XS[:, T * dc:T * dc + T], i == 0 and dc == 0,
                       i == NT - 1 and dc == C - 1, ["XS", "OH"], ["pb"])
            rowmath("pa", PQ[0:16, 0:T], "pb", PQ[0:16, 512:1024])

        def rowmath(ka, s_reg, kb, ss_reg):
            c1 = 1.0 / D
            vop(lambda s: s.tensor_scalar(ROWSF[:, 0:T], s_reg, -c1, None,
                                          ALU.mult), [ka, "ROWSF"], ["ROWSF"])
            vop(lambda s: s.scalar_tensor_tensor(
                OX[0:16, 0:512], ROWSF[:, 0:T], -1.0, ROWSF[:, 0:T],
                ALU.mult, ALU.mult), ["ROWSF", "OX0"], ["OX0"])
            vop(lambda s: s.tensor_scalar(OX[0:16, 0:512], OX[0:16, 0:512],
                                          1e-5, None, ALU.add), ["OX0"],
                ["OX0"])
            vop(lambda s: s.scalar_tensor_tensor(
                OX[0:16, 512:1024], ss_reg, c1, OX[0:16, 0:512],
                ALU.mult, ALU.add), [kb, "OX0", "OX1"], ["OX1"])
            vop(lambda s: s.reciprocal(OX[0:16, 0:512], OX[0:16, 512:1024]),
                ["OX1", "OX0"], ["OX0"])
            sop(lambda s: s.activation(ROWSF[:, T:2 * T], OX[0:16, 0:512],
                                       AF.Sqrt), ["OX0", "ROWSF"], ["ROWSF"])
            vop(lambda s: s.tensor_tensor(ROWS[:, 0:T], ROWSF[:, 0:T],
                                          ROWSF[:, T:2 * T], ALU.mult),
                ["ROWSF", "ROWS"], ["ROWS"])


        def extract_row(i, pack_off, dst_off):
            mm(SC[0:1, 0:T], IDB[0:16, i:i + 1],
               ROWS[0:16, pack_off:pack_off + T], True, True,
               ["IDB", "ROWS", "sc0"], ["sc0"])
            vop(lambda s, d=dst_off: s.tensor_copy(
                RXT[0:1, d:d + T], SC[0:1, 0:T]), ["sc0", "RXT"], ["RXT"])

        def extract_row_f(i, packf_off, dst_off):
            mm(SC[0:1, 0:T], IDF[0:16, i:i + 1],
               ROWSF[0:16, packf_off:packf_off + T], True, True,
               ["IDB", "ROWSF", "sc0"], ["sc0"])
            vop(lambda s, d=dst_off: s.tensor_copy(
                RXT[0:1, d:d + T], SC[0:1, 0:T]), ["sc0", "RXT"], ["RXT"])

        def bcast_row(dst_reg, dst_key, src_off):
            mm(dst_reg, ON16[0:1, :], RXT[0:1, src_off:src_off + T],
               True, True, ["ON16", "RXT", dst_key], [dst_key])

        def make_xs(i, xsrc, tag=None):
            rd = ["XBF"] + ([f"{tag}@{i}"] if tag else [])
            dma("G", XBF[:, :],
                bass.AP(xsrc, i * T, [[S, 128], [128 * S, 8], [1, T]]),
                rd, ["XBF"])
            extract_row_f(i, T, T)
            extract_row(i, 0, 0)
            bcast_row(SC[:, 0:T], "sc0", T)
            for dc in range(C):
                vop(lambda s, dc=dc: s.tensor_tensor(
                    XS[:, T * dc:T * dc + T], XBF[:, T * dc:T * dc + T],
                    SC[:, 0:T], ALU.mult), ["XBF", "sc0", "XS"], ["XS"])

        def proj_ln(i, wbase, ncs_j, evac):
            for oc in range(C):
                breg, bkey = (PA, "pa") if oc % 2 == 0 else (PB, "pb")
                for dc in range(C):
                    mm(breg, WA[:, wbase + 1024 * dc + 128 * oc:
                               wbase + 1024 * dc + 128 * oc + 128],
                       XS[:, T * dc:T * dc + T],
                       dc == 0, False, ["WA", "XS"], [bkey])
                mm(breg, NCS[0:1, 1024 * ncs_j + 128 * oc:
                             1024 * ncs_j + 128 * oc + 128],
                   RXT[0:1, 0:T], False, True, ["NCS", "RXT"], [bkey])
                evac(oc, breg, bkey)

        def proj_plain(wbase, rhs_buf, rhs_key, evac):
            for oc in range(C):
                breg, bkey = (PA, "pa") if oc % 2 == 0 else (PB, "pb")
                for dc in range(C):
                    mm(breg, WA[:, wbase + 1024 * dc + 128 * oc:
                               wbase + 1024 * dc + 128 * oc + 128],
                       rhs_buf[:, T * dc:T * dc + T],
                       dc == 0, dc == C - 1, ["WA", rhs_key], [bkey])
                evac(oc, breg, bkey)

        def evac_resid(xsrc_res, dst_dram, bias_slot, dst_tag, src_tag=None):
            def f_for(i):
                def f(oc, breg, bkey):
                    sl = oc % 2
                    xf = XF[:, 1024 * sl:1024 * sl + 512]
                    xkey = f"XFH{sl}"
                    rdx = [xkey] + ([f"{src_tag}@{i}"] if src_tag else [])
                    dma("Y", xf,
                        bass.AP(xsrc_res, oc * 128 * S + i * T,
                                [[S, 128], [1, T]]), rdx, [xkey])
                    ox = OX[:, 512 * sl:512 * sl + 512]
                    okey = f"OX{sl}"
                    if bias_slot is not None:
                        vop(lambda s, b=breg, x=xf, o=ox, oc=oc:
                            s.scalar_tensor_tensor(
                                o, b, BIA[:, 8 * bias_slot + oc:
                                          8 * bias_slot + oc + 1], x,
                                ALU.add, ALU.add),
                            [bkey, xkey, "BIA", okey], [okey])
                    else:
                        vop(lambda s, b=breg, x=xf, o=ox:
                            s.tensor_tensor(o, b, x, ALU.add),
                            [bkey, xkey, okey], [okey])
                    dma("Y", bass.AP(dst_dram, oc * 128 * S + i * T,
                                     [[S, 128], [1, T]]), ox, [okey],
                        [okey, f"{dst_tag}@{i}"])
                return f
            return f_for

        if not SKIP123:
            load_weights([l0w["q"], l0w["k"], l0w["v"], l0w["o"]])
            build_negcs([(WA, 0), (WA, 8192), (WA, 16384)])
            sweep_stats(xfm)

        def evac_q(oc, breg, bkey):
            sop(lambda s, oc=oc, b=breg: s.activation(
                QR0[:, 512 * oc:512 * oc + 512], b, AF.Identity,
                bias=BIA[:, oc:oc + 1], scale=0.125),
                [bkey, "BIA", "QR0"], ["QR0"])

        def evac_k(i):
            def f(oc, breg, bkey):
                slots = [(4 * i + m) % NSL for m in range(4)]
                runs = [[0]]
                for m in range(1, 4):
                    if slots[m] == slots[m - 1] + 1:
                        runs[-1].append(m)
                    else:
                        runs.append([m])
                for run in runs:
                    m0, ln = run[0], len(run)
                    dst = bass.AP(KR, (slots[m0] * 8 + oc) * 128,
                                  [[NSL * 1024, 128], [1024, ln], [1, 128]])
                    src = bass.AP(PQ, (0 if bkey == "pa" else 512) + m0 * 128,
                                  [[1024, 128], [128, ln], [1, 128]])
                    sop(lambda s, d=dst, r=src, oc=oc: s.activation(
                        d, r, AF.Identity, bias=BIA[:, 8 + oc:9 + oc]),
                        [bkey, "BIA", "KR"], ["KR"])
            return f

        def evac_v(oc, breg, bkey):
            sop(lambda s, oc=oc, b=breg: s.activation(
                XBF[:, 512 * oc:512 * oc + 512], b, AF.Identity,
                bias=BIA[:, 16 + oc:17 + oc]), [bkey, "BIA", "XBF"], ["XBF"])

        def vt_build(i, src_buf, src_key, nsl=NSL):
            for m in range(4):
                sl = (4 * i + m) % nsl
                for dc in range(C):
                    tpose(PT[:, 128 * dc:128 * dc + 128],
                          src_buf[:, 512 * dc + 128 * m:
                                  512 * dc + 128 * m + 128],
                          [src_key], ["pt"] if dc == C - 1 else ["pt_"])
                dst = bass.AP(VT, sl * (H * 65),
                              [[NSL * H * 65, 128], [65, H], [1, 64]])
                vop(lambda s, d=dst: s.tensor_copy(
                    d, bass.AP(PT, 0, [[1024, 128], [64, 16], [1, 64]])),
                    ["pt", "pt_", "VT"], ["VT"])

        def attention(j):
            for ql in range(4):
                qb = 4 * j + ql
                variant = 0 if qb == 0 else (2 if qb == NKB - 1 else 1)
                if qb == 0:
                    for par in range(2):
                        vop(lambda s, par=par: s.memset(
                            SC[:, 512 * par:512 * par + 128], 0.0),
                            [f"sc{par}"], [f"sc{par}"])
                if qb == NKB - 1:
                    for par in range(2):
                        vop(lambda s, par=par: s.memset(
                            SC[:, 512 * par + 256:512 * par + 384], 0.0),
                            [f"sc{par}"], [f"sc{par}"])

                def scores(h):
                    par = h % 2
                    po = (h % 2) * 64
                    hc = h // 2
                    for jj, kb in enumerate((qb - 1, qb, qb + 1)):
                        if 0 <= kb < NKB:
                            sl = kb % NSL
                            lh = bass.AP(KR, (sl * 8 + hc) * 128
                                         + po * (NSL * 1024),
                                         [[NSL * 1024, 64], [1, 128]])
                            rh = bass.AP(QR0, hc * 512 + ql * 128 + po * 4096,
                                         [[4096, 64], [1, 128]])
                            mm(SC[:, 512 * par + 128 * jj:
                                  512 * par + 128 * jj + 128], lh, rh,
                               True, True, ["KR", "QR0"], [f"sc{par}"])

                def attend(h):
                    par = h % 2
                    avreg, avkey = ((AVQ[:, 0:512], "av0") if h % 2 == 0
                                    else (AVQ[:, 512:1024], "av1"))
                    sop(lambda s, par=par: s.activation(
                        WMT[:, :], SC[:, 512 * par:512 * par + 384],
                        AF.Exp, bias=CST[:, 0:1]),
                        [f"sc{par}", "WMT", "CST"], ["WMT"])
                    vop(lambda s, v=variant: s.tensor_tensor(
                        WMT2[:, :], WMT[:, :], MSK[:, 384 * v:384 * v + 384],
                        ALU.mult), ["WMT", "MSK", "WMT2"], ["WMT2"])
                    for jj, kb in enumerate((qb - 1, qb, qb + 1)):
                        sl = kb % NSL
                        rh = bass.AP(VT, sl * (H * 65) + h * 65,
                                     [[NSL * H * 65, 128], [1, 65]])
                        mm(avreg[:, 0:65], WMT2[:, 128 * jj:128 * jj + 128],
                           rh, jj == 0, jj == 2, ["WMT2", "VT"], [avkey])
                    vop(lambda s, h=h, a=avreg: s.tensor_copy(
                        RST[:, h:h + 1], a[:, 64:65]), [avkey, "RST"],
                        ["RST"])
                    vop(lambda s, h=h, a=avreg: s.tensor_copy(
                        AOtm[:, 64 * h:64 * h + 64], a[:, 0:64]),
                        [avkey, "AOtm"], ["AOtm"])

                for h in range(H + 1):
                    if h < H:
                        scores(h)
                    if h >= 1:
                        attend(h - 1)
                vop(lambda s: s.reciprocal(RSI[:, :], RST[:, :]),
                    ["RST", "RSI"], ["RSI"], small=["RST"])
                for h in range(H):
                    vop(lambda s, h=h: s.tensor_scalar(
                        AOtm[:, 64 * h:64 * h + 64],
                        AOtm[:, 64 * h:64 * h + 64], RSI[:, h:h + 1], None,
                        ALU.mult), ["AOtm", "RSI"], ["AOtm"], small=["RSI"])
                for dc in range(C):
                    tpose(PT[:, 128 * dc:128 * dc + 128],
                          AOtm[:, 128 * dc:128 * dc + 128], ["AOtm"],
                          ["pt"] if dc == C - 1 else ["pt_"])
                vop(lambda s, ql=ql: s.tensor_copy(
                    bass.AP(AO, ql * 128, [[4096, 128], [512, 8], [1, 128]]),
                    PT[:, :]), ["pt", "pt_", "AO"], ["AO"])

        evac_x1 = evac_resid(xfm, x1d, 3, "x1")
        for i in range(0 if SKIP123 else NT + 1):
            if i < NT:
                make_xs(i, xfm)
                proj_ln(i, 8192, 1, evac_k(i))
                proj_ln(i, 16384, 2, evac_v)
                vt_build(i, XBF, "XBF")
            if i >= 1:
                attention(i - 1)
                proj_plain(24576, AO, "AO", evac_x1(i - 1))
            if i < NT:
                proj_ln(i, 0, 0, evac_q)

        if PHASES >= 2 and not SKIP123:
            load_weights([l1w["q"], l1w["v"], l1w["g"], l1w["o"]])
            build_negcs([(WB, 0)])
            sweep_stats(x1d, "x1")
            evac_x2 = evac_resid(x1d, x2d, 4, "x2", "x1")

            def proj_wf(i, evac):
                for oc in range(C):
                    breg, bkey = (PA, "pa") if oc % 2 == 0 else (PB, "pb")
                    for dc in range(C):
                        mm(breg, WB[:, 1024 * dc + 128 * oc:
                                   1024 * dc + 128 * oc + 128],
                           XS[:, T * dc:T * dc + T],
                           dc == 0, False, ["WB", "XS"], [bkey])
                    mm(breg, NCS[0:1, 128 * oc:128 * oc + 128],
                       RXT[0:1, 0:T], False, True, ["NCS", "RXT"], [bkey])
                    evac(oc, breg, bkey)

            for i in range(NT):
                make_xs(i, x1d, "x1")
                proj_wf(i, evac_x2(i))

        if PHASES >= 3 and not SKIP123:
            build_negcs([(WA, 0), (WA, 8192), (WA, 16384)])
            sweep_stats(x2d, "x2")

            def evac_qr0(oc, breg, bkey):
                sop(lambda s, oc=oc, b=breg: s.activation(
                    QR0[:, 512 * oc:512 * oc + 512], b, AF.Identity),
                    [bkey, "QR0"], ["QR0"])

            def evac_gate(i):
                def f(oc, breg, bkey):
                    sop(lambda s, oc=oc, b=breg: s.activation(
                        AO[:, 512 * oc:512 * oc + 512], b, AF.Silu),
                        [bkey, "AO"], ["AO"])
                    dma("G", bass.AP(gscr, oc * 128 * S + i * T,
                                     [[S, 128], [1, T]]),
                        AO[:, 512 * oc:512 * oc + 512], ["AO"],
                        ["AO", f"g@{i}"])
                return f

            def evac_v3(oc, breg, bkey):
                sop(lambda s, oc=oc, b=breg: s.activation(
                    XBF[:, 512 * oc:512 * oc + 512], b, AF.Identity),
                    [bkey, "XBF"], ["XBF"])

            for i in range(NT):
                make_xs(i, x2d, "x2")
                proj_ln(i, 0, 0, evac_qr0)
                proj_ln(i, 16384, 2, evac_gate(i))
                proj_ln(i, 8192, 1, evac_v3)
                for tb in range(4 if P3RET else 0):
                    gtb = 4 * i + tb
                    STc = ST0 if gtb % 2 == 0 else ST1
                    STp = ST1 if gtb % 2 == 0 else ST0
                    ckey = f"ST{gtb % 2}"
                    pkey = f"ST{(gtb + 1) % 2}"
                    slv = tb % 2
                    for dc in range(C):
                        tpose(PT[:, 128 * dc:128 * dc + 128],
                              XBF[:, 512 * dc + 128 * tb:
                                  512 * dc + 128 * tb + 128], ["XBF"],
                              ["pt"] if dc == C - 1 else ["pt_"])
                    vop(lambda s, slv=slv: s.tensor_copy(
                        KR[:, slv * 1024:slv * 1024 + 1024], PT[:, :]),
                        ["pt", "pt_", "KR"], ["KR"])
                    for c4 in range(8):
                        mm(SC[:, 0:128],
                           KR[:, slv * 1024 + 128 * c4:
                              slv * 1024 + 128 * c4 + 128],
                           LMAT[:, 128 * (2 * c4):128 * (2 * c4) + 128],
                           True, True, ["KR", "LMAT"], ["sc0"])
                        mm(SC[:, 512:640],
                           KR[:, slv * 1024 + 128 * c4:
                              slv * 1024 + 128 * c4 + 128],
                           LMAT[:, 128 * (2 * c4 + 1):
                                128 * (2 * c4 + 1) + 128],
                           True, True, ["KR", "LMAT"], ["sc1"])
                        vop(lambda s, c=c4, sp=STp, sc_=STc:
                            s.scalar_tensor_tensor(
                                sc_[0:64, 128 * c:128 * c + 128],
                                PWR[0:64, 128 * c:128 * c + 128],
                                sp[0:64, 128 * c + 127:128 * c + 128],
                                SC[0:64, 0:128], ALU.mult, ALU.add),
                            ["PWR", pkey, "sc0", ckey], [ckey])
                        vop(lambda s, c=c4, sp=STp, sc_=STc:
                            s.scalar_tensor_tensor(
                                sc_[64:128, 128 * c:128 * c + 128],
                                PWR[64:128, 128 * c:128 * c + 128],
                                sp[64:128, 128 * c + 127:128 * c + 128],
                                SC[64:128, 512:640], ALU.mult, ALU.add),
                            ["PWR", pkey, "sc1", ckey], [ckey])
                        vop(lambda s, c=c4, sc_=STc, tb=tb:
                            s.tensor_tensor(
                                AO[:, 512 * c + 128 * tb:
                                   512 * c + 128 * tb + 128],
                                sc_[:, 128 * c:128 * c + 128],
                                QR0[:, 512 * c + 128 * tb:
                                    512 * c + 128 * tb + 128], ALU.mult),
                            [ckey, "QR0", "AO"], ["AO"])
                vop(lambda s: s.tensor_tensor(XS[:, :], AO[:, :], AO[:, :],
                                              ALU.mult), ["AO", "XS"], ["XS"])
                for dc in range(C):
                    mm(AVQ[0:16, 0:512], OH[:, 16 * i:16 * i + 16],
                       AO[:, T * dc:T * dc + T], i == 0 and dc == 0,
                       i == NT - 1 and dc == C - 1, ["AO", "OH"], ["av0"])
                for dc in range(C):
                    mm(AVQ[0:16, 512:1024], OH[:, 16 * i:16 * i + 16],
                       XS[:, T * dc:T * dc + T], i == 0 and dc == 0,
                       i == NT - 1 and dc == C - 1, ["XS", "OH"], ["av1"])
                dma("G", bass.AP(yscr, i * T,
                                 [[S, 128], [128 * S, 8], [1, T]]),
                    AO[:, :], ["AO"], [f"y@{i}"])

            rowmath("av0", AVQ[0:16, 0:512], "av1", AVQ[0:16, 512:1024])
            evac_x3 = evac_resid(x2d, x3d, None, "x3", "x2")
            for i in range(NT if P3SWC else 0):
                dma("G", AO[:, :],
                    bass.AP(yscr, i * T, [[S, 128], [128 * S, 8], [1, T]]),
                    ["AO", f"y@{i}"], ["AO"])
                dma("G", QR0[:, :],
                    bass.AP(gscr, i * T, [[S, 128], [128 * S, 8], [1, T]]),
                    ["QR0", f"g@{i}"], ["QR0"])
                extract_row_f(i, 0, 0)
                extract_row_f(i, T, T)
                bcast_row(SC[:, 0:T], "sc0", 0)
                bcast_row(SC[:, 512:512 + T], "sc1", T)
                for dc in range(C):
                    vop(lambda s, dc=dc: s.tensor_tensor(
                        XBF[:, T * dc:T * dc + T], AO[:, T * dc:T * dc + T],
                        SC[:, 0:T], ALU.add), ["AO", "sc0", "XBF"], ["XBF"])
                    vop(lambda s, dc=dc: s.scalar_tensor_tensor(
                        XS[:, T * dc:T * dc + T], XBF[:, T * dc:T * dc + T],
                        BIA[:, 40 + dc:41 + dc], SC[:, 512:512 + T],
                        ALU.mult, ALU.mult), ["XBF", "BIA", "sc1", "XS"],
                        ["XS"])
                    vop(lambda s, dc=dc: s.scalar_tensor_tensor(
                        XBF[:, T * dc:T * dc + T], XS[:, T * dc:T * dc + T],
                        BIA[:, 48 + dc:49 + dc],
                        QR0[:, T * dc:T * dc + T], ALU.add, ALU.mult),
                        ["XS", "BIA", "QR0", "XBF"], ["XBF"])
                proj_plain(24576, XBF, "XBF", evac_x3(i))

        if PHASES >= 4:
            for oc in range(8 if P4A >= 1 else 0):
                dma("Y", UF[:, 0:1024],
                    bass.AP(l1rhsT, oc * 128 * 1024, [[1024, 128], [1, 1024]]),
                    ["UF0"], ["UF0"])
                vop(lambda s: s.reduce_max(SMC[:, 0:1], UF[:, 0:1024],
                                           axis=AX.X,
                                           apply_absolute_value=True),
                    ["UF0", "SMC"], ["SMC"])
                vop(lambda s, oc=oc: s.tensor_scalar(
                    SMC2[:, oc:oc + 1], SMC[:, 0:1], 1e-12, 1.0 / 127.0,
                    ALU.max, ALU.mult), ["SMC", "SMC2"], ["SMC2"],
                    small=["SMC"])
                vop(lambda s, oc=oc: s.reciprocal(SMC[:, 1:2],
                                                  SMC2[:, oc:oc + 1]),
                    ["SMC2", "SMC"], ["SMC"], small=["SMC2"])
                vop(lambda s: s.tensor_scalar(
                    OX[:, 0:1024], UF[:, 0:1024], SMC[:, 1:2], 128.5,
                    ALU.mult, ALU.add), ["UF0", "SMC", "OX0", "OX1"],
                    ["OX0", "OX1"], small=["SMC"])
                for lb in range(8):
                    b = float(128 >> lb)
                    src, sk = (OX[:, 0:1024], ["OX0", "OX1"]) if lb == 0 \
                        else (UF[:, 0:1024], ["UF0"])
                    vop(lambda s, b=b, src=src: s.tensor_scalar(
                        ST1[:, :], src, b, -b, ALU.is_ge, ALU.mult),
                        sk + ["ST1"], ["ST1"])
                    vop(lambda s, src=src: s.tensor_tensor(
                        UF[:, 0:1024], src, ST1[:, :], ALU.add),
                        sk + ["ST1", "UF0"], ["UF0"])
                dst = XS if oc < 4 else XBF
                dk = "XS" if oc < 4 else "XBF"
                cb = (oc % 4) * 1024
                vop(lambda s, d=dst, cb=cb: s.scalar_tensor_tensor(
                    d[:, cb:cb + 1024], OX[:, 0:1024], 128.0, UF[:, 0:1024],
                    ALU.subtract, ALU.subtract),
                    ["OX0", "OX1", "UF0", dk], [dk])
            if P4A >= 2:
                tpose(PQ[0:8, 0:128], SMC2[:, 0:8], ["SMC2"], ["pa"],
                      f32=True)
                vop(lambda s: s.tensor_copy(ROWSF[0:8, 0:128],
                                            PQ[0:8, 0:128]),
                    ["pa", "ROWSF"], ["ROWSF"])
            for oc in range(8 if P4A >= 2 else 0):
                mm(PQ[0:1, 512:640], IDF[0:8, oc:oc + 1],
                   ROWSF[0:8, 0:128], True, True, ["IDB", "ROWSF"], ["pb"])
                vop(lambda s: s.tensor_copy(RXF[0:1, 0:128],
                                            PQ[0:1, 512:640]),
                    ["pb", "RXF"], ["RXF"])
                mm(PQ[:, 512:640], ON16F[0:1, :], RXF[0:1, 0:128],
                   True, True, ["ON16F", "RXF"], ["pb"])
                vop(lambda s, oc=oc: s.tensor_copy(
                    RSB[:, 128 * oc:128 * oc + 128], PQ[:, 512:640]),
                    ["pb", "RSB"], ["RSB"])
            for oc in range(8 if P4A >= 3 else 0):
                src_t = XS if oc < 4 else XBF
                sk = "XS" if oc < 4 else "XBF"
                cb = (oc % 4) * 1024
                for kc in range(8):
                    tpose(PT[:, 128 * kc:128 * kc + 128],
                          src_t[:, cb + 128 * kc:cb + 128 * kc + 128],
                          [sk], ["pt"] if kc == C - 1 else ["pt_"])
                vop(lambda s, oc=oc: s.tensor_copy(
                    bass.AP(WB, 128 * oc, [[8192, 128], [1024, 8], [1, 128]]),
                    PT[:, :]), ["pt", "pt_", "WB"], ["WB"])

            if P4B:
                sweep_stats(x3d, "x3")
            for pk, (roff, cbase) in enumerate([(0, 0), (T, 64)]
                                               if P4B else []):
                for q4 in range(4):
                    tpose(PQ[:, 0:16],
                          ROWSF[0:16, roff + 128 * q4:roff + 128 * q4 + 128],
                          ["ROWSF"], ["pa"], f32=True)
                    vop(lambda s, q4=q4, cb=cbase: s.tensor_copy(
                        bass.AP(COLS, cb + q4, [[128, 128], [4, 16]]),
                        PQ[:, 0:16]), ["pa", "COLS"], ["COLS"])

            def p4_parity(tb):
                par = tb % 2
                ufr, ufk = (UF[:, 0:1024], "UF0") if par == 0 \
                    else (PWR[:, :], "PWR")
                smc, smk = (SMC, "SMC") if par == 0 else (SMC2, "SMC2")
                return (par, ufr, ufk, smc, smk, f"XFH{par}",
                        XBF[:, 1024 * par:1024 * par + 1024], f"XBF{par}",
                        XS[:, 1024 * par:1024 * par + 1024], f"XS{par}")

            def p4_stageA(tb):
                par, ufr, ufk, smc, smk, xfk, xbfr, xbk, xsr, xsk = \
                    p4_parity(tb)
                xfr = XF[:, 1024 * par:1024 * par + 1024]
                dma("Y", xfr,
                    bass.AP(x3d, tb * 128,
                            [[S, 128], [128 * S, 8], [1, 128]]),
                    [xfk, f"x3@{tb // 4}"], [xfk])
                for dc in range(C):
                    tpose(PQ[:, 128 * dc:128 * dc + 128],
                          xfr[:, 128 * dc:128 * dc + 128], [xfk],
                          ["pa"] if dc < 4 else ["pb"], f32=True)
                sop(lambda s, tb=tb, ufr=ufr: s.activation(
                    ufr, PQ[:, :], AF.Identity,
                    bias=COLS[:, tb:tb + 1]),
                    ["pa", "pb", "COLS", ufk], [ufk])
                vop(lambda s, smc=smc, ufr=ufr: s.reduce_max(
                    smc[:, 0:1], ufr, axis=AX.X,
                    apply_absolute_value=True), [ufk, smk], [smk])
                vop(lambda s, tb=tb, smc=smc: s.tensor_scalar(
                    smc[:, 1:2], smc[:, 0:1], COLS[:, 64 + tb:65 + tb],
                    None, ALU.mult), [smk, "COLS"], [smk], small=[smk])
                vop(lambda s, smc=smc: s.tensor_scalar(
                    smc[:, 2:3], smc[:, 1:2], 1e-12, 1.0 / 127.0, ALU.max,
                    ALU.mult), [smk, f"LSC{par}"], [f"LSC{par}"],
                    small=[smk])
                vop(lambda s, smc=smc: s.reciprocal(smc[:, 3:4],
                                                    smc[:, 0:1]),
                    [smk, f"RCQ{par}"], [f"RCQ{par}"], small=[smk])
                vop(lambda s, smc=smc: s.tensor_scalar(
                    smc[:, 4:5], smc[:, 3:4], 127.0, None, ALU.mult),
                    [f"RCQ{par}", f"RCQ2{par}"], [f"RCQ2{par}"],
                    small=[f"RCQ{par}"])
                sop(lambda s, smc=smc, ufr=ufr: s.activation(
                    OX[:, 0:1024], ufr, AF.Identity,
                    bias=CST[:, 1:2], scale=smc[:, 4:5]),
                    [ufk, f"RCQ2{par}", "OX0", "OX1", "CST"],
                    ["OX0", "OX1"], small=[f"RCQ2{par}"])
                for lb in range(8):
                    b = float(128 >> lb)
                    src, sk = (OX[:, 0:1024], ["OX0", "OX1"]) if lb == 0 \
                        else (ST1[:, :], ["ST1"])
                    vop(lambda s, b=b, src=src: s.tensor_scalar(
                        ST0[:, :], src, b, -b, ALU.is_ge, ALU.mult),
                        sk + ["ST0"], ["ST0"])
                    vop(lambda s, src=src: s.tensor_tensor(
                        ST1[:, :], src, ST0[:, :], ALU.add),
                        sk + ["ST0", "ST1"], ["ST1"])
                vop(lambda s, xbfr=xbfr: s.scalar_tensor_tensor(
                    xbfr, OX[:, 0:1024], 128.0, ST1[:, :],
                    ALU.subtract, ALU.subtract),
                    ["OX0", "OX1", "ST1", xbk], [xbk])

            def p4_stageA2(tb):
                par, ufr, ufk, smc, smk, xfk, xbfr, xbk, xsr, xsk = \
                    p4_parity(tb)
                for kc in range(C):
                    tpose(PT[:, 128 * kc:128 * kc + 128],
                          xbfr[:, 128 * kc:128 * kc + 128], [xbk],
                          ["pt"] if kc == C - 1 else ["pt_"])
                sop(lambda s, xsr=xsr: s.activation(xsr, PT[:, :],
                                                    AF.Identity),
                    ["pt", "pt_", xsk], [xsk])

            def p4_stageB(tb):
                par, ufr, ufk, smc, smk, xfk, xbfr, xbk, xsr, xsk = \
                    p4_parity(tb)
                for half in range(2):
                    sck = f"sc{half}"
                    for kc in range(C):
                        mm(SC[:, 512 * half:512 * half + 512],
                           xsr[:, 128 * kc:128 * kc + 128],
                           WB[:, 1024 * kc + 512 * half:
                              1024 * kc + 512 * half + 512],
                           kc == 0, kc == C - 1, [xsk, "WB"], [sck])
                for half in range(2):
                    sck = f"sc{half}"
                    stg = XF[:, 1024 * par + 512 * half:
                             1024 * par + 512 * half + 512]
                    okey = f"OF{par}{half}"
                    vop(lambda s, half=half, smc=smc, stg=stg:
                        s.scalar_tensor_tensor(
                            stg, RSB[:, 512 * half:512 * half + 512],
                            smc[:, 2:3],
                            SC[:, 512 * half:512 * half + 512], ALU.mult,
                            ALU.mult),
                        ["RSB", f"LSC{par}", sck, okey, xfk], [okey],
                        small=[f"LSC{par}"])
                    vop(lambda s, half=half, tb=tb, ufr=ufr, stg=stg:
                        s.scalar_tensor_tensor(
                            stg, ufr[:, 512 * half:512 * half + 512],
                            COLS[:, tb:tb + 1], stg,
                            ALU.subtract, ALU.add),
                        [ufk, "COLS", okey], [okey])
                    dma("Y", bass.AP(ofm, tb * 128 * D + 512 * half,
                                     [[1024, 128], [1, 512]]),
                        stg, [okey, xfk], [okey])

            for tb in range(NKB + 1 if P4C else 0):
                if tb < NKB:
                    p4_stageA(tb)
                if tb >= 1:
                    p4_stageB(tb - 1)
                if tb < NKB:
                    p4_stageA2(tb)

        p.finish()

        with nc.Block() as block:
            @block.tensor
            def _(sec):
                for f in p.ops["T"]:
                    f(sec)

            @block.vector
            def _(sec):
                for f in p.ops["V"]:
                    f(sec)

            @block.scalar
            def _(sec):
                for f in p.ops["S"]:
                    f(sec)

            @block.sync
            def _(sec):
                for f in p.ops["Y"]:
                    f(sec)

            @block.gpsimd
            def _(sec):
                for f in p.ops["G"]:
                    f(sec)

    return nc


_CONSTS = None


def host_consts():
    global _CONSTS
    if _CONSTS is not None:
        return _CONSTS
    import ml_dtypes
    bf = ml_dtypes.bfloat16
    ident = np.eye(128, dtype=bf)
    oh = np.zeros((128, 256), dtype=bf)
    for i in range(16):
        oh[16 * 0:, 16 * i + i] = 0
    for i in range(16):
        oh[:, 16 * i + i] = 1
    ones16 = np.ones((16, 128), dtype=bf)
    ones16f = np.ones((16, 128), dtype=np.float32)
    k = np.arange(128)[:, None]
    q = np.arange(128)[None, :]
    mL = (k >= q).astype(bf)
    mR = (k <= q).astype(bf)
    mid = np.ones((128, 128), dtype=bf)
    zero = np.zeros((128, 128), dtype=bf)
    masks = np.concatenate([zero, mid, mR, mL, mid, mR, mL, mid, zero],
                           axis=1)
    s_ = np.arange(128)[:, None]
    t_ = np.arange(128)[None, :]
    diota = np.broadcast_to((t_ - s_).astype(np.float32), (128, 128))
    dnegb = np.where(t_ >= s_, 0.0, -1e9).astype(np.float32)
    dp1 = np.broadcast_to((t_ + 1).astype(np.float32), (128, 128))
    dtabs = np.ascontiguousarray(
        np.concatenate([diota, dnegb, dp1], axis=1).astype(np.float32))
    _CONSTS = {
        "identb": ident, "identf": np.eye(128, dtype=np.float32),
        "oh16": oh, "ones16": ones16, "ones16f": ones16f,
        "masks": np.ascontiguousarray(masks), "dtabs": dtabs,
    }
    return _CONSTS


def host_inputs(inputs, core):
    x = np.asarray(inputs["L0_attn_x"][core])
    cc = np.ascontiguousarray
    d = {
        "xfm": cc(x.T),
        "l0wq": cc(np.asarray(inputs["L0_attn_Wq"]).T),
        "l0wk": cc(np.asarray(inputs["L0_attn_Wk"]).T),
        "l0wv": cc(np.asarray(inputs["L0_attn_Wv"]).T),
        "l0wo": cc(np.asarray(inputs["L0_attn_Wo"]).T),
        "l0bq": np.asarray(inputs["L0_attn_bq"]),
        "l0bk": np.asarray(inputs["L0_attn_bk"]),
        "l0bv": np.asarray(inputs["L0_attn_bv"]),
        "l0bo": np.asarray(inputs["L0_attn_bo"]),
        "g1r": cc(np.asarray(inputs["L0_ffn_G1_R"])[0].transpose(1, 0, 2)
                  .reshape(4, 1024)),
        "g2r": cc(np.asarray(inputs["L0_ffn_G2_R"])[:, :, 0, :]
                  .reshape(4, 256)),
        "g1i": cc(np.asarray(inputs["L0_ffn_G1_I"])[0].transpose(1, 0, 2)
                  .reshape(4, 1024)),
        "g2i": cc(np.asarray(inputs["L0_ffn_G2_I"])[:, :, 0, :]
                  .reshape(4, 256)),
        "l0fb": cc(np.concatenate([np.asarray(inputs["L0_ffn_bias_real"]),
                                   np.asarray(inputs["L0_ffn_bias_imag"])])),
        "l1wq": cc(np.asarray(inputs["L1_attn_Wq"]).T),
        "l1wv": cc(np.asarray(inputs["L1_attn_Wv"]).T),
        "l1wg": cc(np.asarray(inputs["L1_attn_Wg"]).T),
        "l1wo": cc(np.asarray(inputs["L1_attn_Wo"]).T),
        "l1lng": np.asarray(inputs["L1_attn_ln_g"]),
        "l1lnb": np.asarray(inputs["L1_attn_ln_b"]),
        "l1decay": cc(np.asarray(inputs["L1_attn_decay"])[:, 1, 0]),
        "l1rhsT": cc(np.asarray(inputs["L1_ffn_rhs"]).T),
    }
    d.update(host_consts())
    return d


def kernel(**inputs):
    nc = build()
    in_maps = [host_inputs(inputs, core) for core in range(8)]
    res = run_bass_kernel_spmd(nc, in_maps, core_ids=list(range(8)))
    outs = []
    for core in range(8):
        r = res.results[core]
        if PHASES == 4:
            outs.append(np.asarray(r["ofm"]))
        else:
            key = {1: "x1d", 2: "x2d", 3: "x3d"}[PHASES]
            outs.append(np.asarray(r[key]).T)
    return np.stack(outs, axis=0)


# version 10
# speedup vs baseline: 1.1250x; 1.0007x over previous
import numpy as np
import concourse.bass as bass
import concourse.mybir as mybir
from concourse.bass_utils import run_bass_kernel_spmd
from contextlib import ExitStack

F32 = mybir.dt.float32
BF16 = mybir.dt.bfloat16
AF = mybir.ActivationFunctionType
ALU = mybir.AluOpType
AX = mybir.AxisListType

S = 8192
D = 1024
T = 512
NT = S // T
C = 8
H = 16
NKB = S // 128
NSL = 9
RCONST = 1.5 * 2.0 ** 23
EXPB = -3.0

PHASES = 4
P3RET = True
P3SWC = True
P4A = 3
P4B = True
P4C = True
SKIP123 = False


class Prog:
    ENG = ("T", "V", "S", "Y", "G")
    NPOOL = 12

    def __init__(self, nc, es):
        self.nc = nc
        self.ops = {e: [] for e in self.ENG}
        self.semkeys = ["T", "V", "S"] + \
            [f"Y{i}" for i in range(self.NPOOL)] + \
            [f"G{i}" for i in range(self.NPOOL)]
        self.sem = {k: es.enter_context(nc.semaphore(f"sem{k}"))
                    for k in self.semkeys}
        self.cnt = {k: 0 for k in self.semkeys}
        self.dma_idx = {"Y": 0, "G": 0}
        self.res = {}
        self.waited = {e: {k: 0 for k in self.semkeys} for e in self.ENG}
        self.icnt = {e: 0 for e in self.ENG}
        self.drained = {e: 0 for e in self.ENG}

    def _add_waits(self, eng, reads, writes, need):
        def add(k2, c2):
            if k2 == eng:
                return
            need[k2] = max(need.get(k2, 0), c2)

        for r in reads:
            st = self.res.get(r)
            if st and st["w"]:
                add(*st["w"])
        for w in writes:
            st = self.res.get(w)
            if st:
                if st["w"]:
                    add(*st["w"])
                for (k2, c2) in st["r"]:
                    add(k2, c2)

    def op(self, eng, fn, reads=(), writes=(), dma=False, small=()):
        reads = list(reads)
        writes = list(writes)
        need = {}
        self._add_waits(eng, reads, writes, need)
        if dma:
            semkey = f"{eng}{self.dma_idx[eng] % self.NPOOL}"
            self.dma_idx[eng] += 1
            if self.cnt[semkey] > 0:
                need[semkey] = max(need.get(semkey, 0), self.cnt[semkey])
            inc = 16
        else:
            semkey = eng
            inc = 1
        waits = []
        for k2, c2 in sorted(need.items()):
            if self.waited[eng][k2] < c2:
                self.waited[eng][k2] = c2
                waits.append((self.sem[k2], c2))
        drain = False
        for r in small:
            st = self.res.get(r)
            if st and st["w"] and st["w"][0] == eng:
                wc = st["w"][1]
                if self.icnt[eng] - wc < 12 and wc > self.drained[eng]:
                    drain = True
        if drain:
            self.drained[eng] = self.icnt[eng]
        self.cnt[semkey] += inc
        newc = self.cnt[semkey]
        if not dma:
            self.icnt[eng] = newc
        sem = self.sem[semkey]

        def emit(sec, waits=waits, fn=fn, drain=drain, sem=sem, inc=inc):
            for s_, v_ in waits:
                sec.wait_ge(s_, v_)
            if drain:
                sec.drain()
            fn(sec).then_inc(sem, inc)

        self.ops[eng].append(emit)
        for w in writes:
            self.res[w] = {"w": (semkey, newc), "r": []}
        for r in reads:
            st = self.res.setdefault(r, {"w": None, "r": []})
            st["r"].append((semkey, newc))
        return newc

    def finish(self):
        for k in self.semkeys:
            v = self.cnt[k]
            if v:
                def emit(sec, s_=self.sem[k], v_=v):
                    sec.wait_ge(s_, v_)
                self.ops["Y"].append(emit)


RACECHECK = False


def build():
    nc = bass.Bass("TRN2", target_bir_lowering=False,
                   detect_race_conditions=RACECHECK)

    def din(name, shape, dt=F32):
        return nc.dram_tensor(name, shape, dt, kind="ExternalInput")

    xfm = din("xfm", [D, S])
    l0w = {k: din(f"l0w{k}", [D, D]) for k in "qkvo"}
    l0b = {k: din(f"l0b{k}", [D]) for k in "qkvo"}
    g1r = din("g1r", [4, 1024]); g2r = din("g2r", [4, 256])
    g1i = din("g1i", [4, 1024]); g2i = din("g2i", [4, 256])
    l0fb = din("l0fb", [D])
    l1w = {k: din(f"l1w{k}", [D, D]) for k in "qvgo"}
    l1lng = din("l1lng", [D]); l1lnb = din("l1lnb", [D])
    l1decay = din("l1decay", [16])
    l1rhsT = din("l1rhsT", [D, D])
    identb = din("identb", [128, 128], BF16)
    identf = din("identf", [128, 128])
    oh16 = din("oh16", [128, 16 * 16], BF16)
    ones16 = din("ones16", [16, 128], BF16)
    ones16f = din("ones16f", [16, 128])
    masks = din("masks", [128, 3 * 384], BF16)
    dtabs = din("dtabs", [128, 384])

    ofm = nc.dram_tensor("ofm", [S, D], F32, kind="ExternalOutput")

    def scratch(name, k):
        kind = "ExternalOutput" if PHASES == k else "Internal"
        return nc.dram_tensor(name, [D, S], F32, kind=kind)

    x1d = scratch("x1d", 1)
    x2d = scratch("x2d", 2)
    x3d = scratch("x3d", 3)
    yscr = nc.dram_tensor("yscr", [D, S], BF16)
    gscr = nc.dram_tensor("gscr", [D, S], BF16)
    pscr = nc.dram_tensor("pscr", [2 * 1024 * 256], BF16)

    es = ExitStack()
    sbt = lambda n, sh, dt: es.enter_context(nc.sbuf_tensor(n, sh, dt))
    with es:
        WA = sbt("WA", [128, 4 * 8192], BF16)
        WB = sbt("WB", [128, 8192], BF16)
        KR = sbt("KR", [128, NSL * 1024], BF16)
        VT = sbt("VT", [128, NSL * (H * 65)], BF16)
        XF = sbt("XF", [128, 2048], F32)
        XBF = sbt("XBF", [128, 4096], BF16)
        XS = sbt("XS", [128, 4096], BF16)
        QR0 = sbt("QR0", [128, 4096], BF16)
        AO = sbt("AO", [128, 4096], BF16)
        AOtm = sbt("AOtm", [128, 1024], BF16)
        UF = sbt("UF", [128, 1024], F32)
        OX = sbt("OX", [128, 1024], F32)
        ST0 = sbt("ST0", [128, 1024], F32)
        ST1 = sbt("ST1", [128, 1024], F32)
        LMAT = sbt("LMAT", [128, H * 128], BF16)
        PWR = sbt("PWR", [128, 1024], F32)
        WMT = sbt("WMT", [128, 384], BF16)
        WMT2 = sbt("WMT2", [128, 384], BF16)
        MSK = sbt("MSK", [128, 3 * 384], BF16)
        IDB = sbt("IDB", [128, 128], BF16)
        IDF = sbt("IDF", [128, 128], F32)
        OH = sbt("OH", [128, 256], BF16)
        ON16 = sbt("ON16", [16, 128], BF16)
        ON16F = sbt("ON16F", [16, 128], F32)
        ONESC = sbt("ONESC", [128, 1], BF16)
        NCS = sbt("NCS", [1, 3 * 1024], BF16)
        ROWS = sbt("ROWS", [16, T], BF16)
        RXT = sbt("RXT", [1, 2 * T], BF16)
        RXF = sbt("RXF", [1, 128], F32)
        CST = sbt("CST", [128, 4], F32)
        ROWSF = sbt("ROWSF", [16, 2 * T], F32)
        COLS = sbt("COLS", [128, 128], F32)
        BIA = sbt("BIA", [128, 64], F32)
        RST = sbt("RST", [128, 16], F32)
        RSI = sbt("RSI", [128, 16], F32)
        SMC = sbt("SMC", [128, 8], F32)
        SMC2 = sbt("SMC2", [128, 8], F32)
        LNC = sbt("LNC", [128, 16], F32)
        PRC = sbt("PRC", [128, 8], F32)
        DEC = sbt("DEC", [16, 2], F32)
        LNR = sbt("LNR", [1, 16], F32)
        RSB = sbt("RSB", [128, 1024], F32)

        PQ = es.enter_context(nc.psum_tensor("PQ", [128, 1024], F32))
        SC = es.enter_context(nc.psum_tensor("SCP", [128, 1536], F32))
        AVQ = es.enter_context(nc.psum_tensor("AVQ", [128, 1024], F32))
        PT = es.enter_context(nc.psum_tensor("PTP", [128, 1024], BF16))

        p = Prog(nc, es)
        ident = IDB[:, :]

        def dma(q, dst, src, reads, writes, slow=False):
            p.op(q, lambda s, d=dst, r=src, sl=slow: s.dma_start(
                d, r, allow_slow_non_contiguous=sl),
                 reads=reads, writes=writes, dma=True)

        def mm(out, lh, rh, start, stop, reads, writes):
            p.op("T", lambda s, o=out, l=lh, r=rh, a=start, b=stop:
                 s.matmul(o, l, r, start=a, stop=b, skip_group_check=True),
                 reads=reads, writes=writes)

        def tpose(out, in_, reads, writes, f32=False):
            pc = in_.ap[0][1]
            idap = (IDF if f32 else IDB)[0:pc, 0:pc]
            p.op("T", lambda s, o=out, i=in_, d=idap: s.transpose(o, i, d),
                 reads=list(reads) + ["IDB"], writes=writes)

        vop = lambda fn, reads=(), writes=(), small=(): p.op("V", fn, reads, writes, small=small)
        sop = lambda fn, reads=(), writes=(), small=(): p.op("S", fn, reads, writes, small=small)

        PA = PQ[:, 0:512]
        PB = PQ[:, 512:1024]

        dma("Y", IDB[:, :], identb.ap(), [], ["IDB"])
        dma("Y", IDF[:, :], identf.ap(), [], ["IDB"])
        dma("Y", MSK[:, :], masks.ap(), [], ["MSK"])
        dma("Y", OH[:, :], oh16.ap(), [], ["OH"])
        dma("Y", ON16[:, :], ones16.ap(), [], ["ON16"])
        dma("Y", ON16F[:, :], ones16f.ap(), [], ["ON16F"])
        dma("Y", ST0[:, 0:384], dtabs.ap(), [], ["DTB"])
        for j, t in enumerate([l0b["q"], l0b["k"], l0b["v"], l0b["o"], l0fb,
                               l1lng, l1lnb]):
            dma("Y", BIA[:, 8 * j:8 * j + 8],
                bass.AP(t, 0, [[1, 128], [128, 8]]), ["BIA"], ["BIA"],
                slow=True)
        vop(lambda s: s.tensor_scalar(BIA[:, 0:8], BIA[:, 0:8], 0.125, None,
                                      ALU.mult), ["BIA"], ["BIA"])
        vop(lambda s: s.memset(ONESC[:, :], 1.0), [], ["ONESC"])
        vop(lambda s: s.memset(CST[:, 0:1], EXPB), ["CST"], ["CST"])
        vop(lambda s: s.memset(CST[:, 1:2], 128.5), ["CST"], ["CST"])
        vop(lambda s: s.memset(KR[:, :], 0.0), [], ["KR"])
        vop(lambda s: s.memset(
            bass.AP(VT, 0, [[NSL * H * 65, 128], [65, NSL * H], [1, 64]]),
            0.0), [], ["VT"])
        vop(lambda s: s.memset(
            bass.AP(VT, 64, [[NSL * H * 65, 128], [65, NSL * H]]), 1.0),
            ["VT"], ["VT"])

        dma("Y", DEC[:, 0:1], bass.AP(l1decay, 0, [[1, 16], [16, 1]]),
            ["DEC"], ["DEC"], slow=True)
        vop(lambda s: s.tensor_scalar(DEC[:, 0:1], DEC[:, 0:1], 1e-30, None,
                                      ALU.max), ["DEC"], ["DEC"])
        sop(lambda s: s.activation(DEC[:, 1:2], DEC[:, 0:1], AF.Ln),
            ["DEC"], ["DECL"])
        tpose(PQ[0:1, 0:16], DEC[:, 1:2], ["DECL"], ["pa"], f32=True)
        vop(lambda s: s.tensor_copy(LNR[:, :], PQ[0:1, 0:16]), ["pa"], ["LNR"])
        for h in range(H):
            mm(PQ[:, 0:1], ON16F[0:1, :], LNR[:, h:h + 1], True, True,
               ["ON16F", "LNR"], ["pa"])
            vop(lambda s, h=h: s.tensor_copy(LNC[:, h:h + 1], PQ[:, 0:1]),
                ["pa", "LNC"], ["LNC"])
        for h in range(H):
            vop(lambda s, h=h: s.scalar_tensor_tensor(
                OX[:, 0:128], ST0[:, 0:128], LNC[:, h:h + 1],
                ST0[:, 128:256], ALU.mult, ALU.add),
                ["DTB", "LNC", "OX0"], ["OX0"], small=["LNC"])
            sop(lambda s, h=h: s.activation(LMAT[:, 128 * h:128 * h + 128],
                                            OX[:, 0:128], AF.Exp),
                ["OX0"], ["LMAT"])
        for c4 in range(8):
            vop(lambda s, c=c4: s.tensor_copy(PRC[0:64, c:c + 1],
                                              LNC[0:64, 2 * c:2 * c + 1]),
                ["LNC", "PRC"], ["PRC"])
            vop(lambda s, c=c4: s.tensor_copy(PRC[64:128, c:c + 1],
                                              LNC[64:128, 2 * c + 1:2 * c + 2]),
                ["LNC", "PRC"], ["PRC"])
            sop(lambda s, c=c4: s.activation(PWR[:, 128 * c:128 * c + 128],
                                             ST0[:, 256:384], AF.Exp,
                                             scale=PRC[:, c:c + 1]),
                ["DTB", "PRC"], ["PWR"], small=["PRC"])
        vop(lambda s: s.memset(ST0[:, :], 0.0), [], ["ST0", "DTB"])
        vop(lambda s: s.memset(ST1[:, :], 0.0), [], ["ST1"])

        for part, (gg1, gg2) in enumerate([(g1r, g2r), (g1i, g2i)]):
            dma("Y", UF[0:4, 0:1024], gg1.ap(), ["UF0"], ["UF0"])
            dma("Y", OX[0:4, 0:256], gg2.ap(), ["OX0"], ["OX0"])
            for mb in range(8):
                mm(PQ[:, 0:256], UF[0:4, 128 * mb:128 * mb + 128],
                   OX[0:4, 0:256], True, True, ["UF0", "OX0"], ["pa"])
                vop(lambda s: s.tensor_copy(
                    XBF[:, 0:256],
                    bass.AP(PQ, 0, [[1024, 128], [1, 16], [16, 16]])),
                    ["pa", "KRS"], ["KRS"])
                dma("Y", bass.AP(pscr, (part * 1024 + 128 * mb) * 256,
                                 [[256, 128], [1, 256]]), XBF[:, 0:256],
                    ["KRS", "XBF"], [f"pscr{part}_{mb}"])
        pdeps = [f"pscr{i}_{m}" for i in range(2) for m in range(8)]

        def wf_reorder(dst_col_base):
            for q4 in range(4):
                for dl in range(8):
                    src_r = bass.AP(pscr, (8 * q4 + dl) * 256,
                                    [[16, 16], [8192, 32], [1, 16]])
                    src_i = bass.AP(pscr, 1024 * 256 + (8 * q4 + dl) * 256,
                                    [[16, 16], [8192, 32], [1, 16]])
                    rows = (16 * dl, 16 * dl + 16)
                    dma("Y", WB[rows[0]:rows[1], 1024 * q4:1024 * q4 + 512],
                        src_r, pdeps + ["WB"], ["WB"], slow=True)
                    dma("Y", WB[rows[0]:rows[1],
                               1024 * q4 + 512:1024 * q4 + 1024],
                        src_i, pdeps + ["WB"], ["WB"], slow=True)
                    dma("Y", WB[rows[0]:rows[1],
                               1024 * (4 + q4) + 512:1024 * (4 + q4) + 1024],
                        src_r, pdeps + ["WB"], ["WB"], slow=True)
                    dma("Y", WB[rows[0]:rows[1],
                               1024 * (4 + q4):1024 * (4 + q4) + 512],
                        src_i, pdeps + ["WB"], ["WB"], slow=True)
            for q4 in range(4):
                vop(lambda s, q=q4: s.tensor_scalar(
                    WB[:, 1024 * (4 + q):1024 * (4 + q) + 512],
                    WB[:, 1024 * (4 + q):1024 * (4 + q) + 512], -1.0, None,
                    ALU.mult), ["WB"], ["WB"])
        wf_reorder(0)

        def load_weights(tensors):
            for sl, t in enumerate(tensors):
                dma("G", WA[:, 8192 * sl:8192 * sl + 8192],
                    bass.AP(t, 0, [[1024, 128], [128 * 1024, 8], [1, 1024]]),
                    ["WA"], ["WA"])

        def build_negcs(slots):
            bk = {id(WA): "WA", id(WB): "WB"}
            for j, (buf, base) in enumerate(slots):
                for half in range(2):
                    for dc in range(C):
                        mm(PQ[0:1, 0:512], ONESC[:, :],
                           buf[:, base + 1024 * dc + 512 * half:
                               base + 1024 * dc + 512 * half + 512],
                           dc == 0, dc == C - 1, ["ONESC", bk[id(buf)]], ["pa"])
                    vop(lambda s, j=j, half=half: s.tensor_copy(
                        NCS[0:1, 1024 * j + 512 * half:
                            1024 * j + 512 * half + 512],
                        PQ[0:1, 0:512]), ["pa", "NCS"], ["NCS"])

        def sweep_stats(xsrc, tag=None):
            for i in range(NT):
                rd = ["XBF"] + ([f"{tag}@{i}"] if tag else [])
                dma("G", XBF[:, :],
                    bass.AP(xsrc, i * T, [[S, 128], [128 * S, 8], [1, T]]),
                    rd, ["XBF"])
                vop(lambda s: s.tensor_tensor(XS[:, :], XBF[:, :], XBF[:, :],
                                              ALU.mult), ["XBF", "XS"], ["XS"])
                for dc in range(C):
                    mm(PQ[0:16, 0:T], OH[:, 16 * i:16 * i + 16],
                       XBF[:, T * dc:T * dc + T], i == 0 and dc == 0,
                       i == NT - 1 and dc == C - 1, ["XBF", "OH"], ["pa"])
                for dc in range(C):
                    mm(PQ[0:16, 512:1024], OH[:, 16 * i:16 * i + 16],
                       XS[:, T * dc:T * dc + T], i == 0 and dc == 0,
                       i == NT - 1 and dc == C - 1, ["XS", "OH"], ["pb"])
            rowmath("pa", PQ[0:16, 0:T], "pb", PQ[0:16, 512:1024])

        def rowmath(ka, s_reg, kb, ss_reg):
            c1 = 1.0 / D
            vop(lambda s: s.tensor_scalar(ROWSF[:, 0:T], s_reg, -c1, None,
                                          ALU.mult), [ka, "ROWSF"], ["ROWSF"])
            vop(lambda s: s.scalar_tensor_tensor(
                OX[0:16, 0:512], ROWSF[:, 0:T], -1.0, ROWSF[:, 0:T],
                ALU.mult, ALU.mult), ["ROWSF", "OX0"], ["OX0"])
            vop(lambda s: s.tensor_scalar(OX[0:16, 0:512], OX[0:16, 0:512],
                                          1e-5, None, ALU.add), ["OX0"],
                ["OX0"])
            vop(lambda s: s.scalar_tensor_tensor(
                OX[0:16, 512:1024], ss_reg, c1, OX[0:16, 0:512],
                ALU.mult, ALU.add), [kb, "OX0", "OX1"], ["OX1"])
            vop(lambda s: s.reciprocal(OX[0:16, 0:512], OX[0:16, 512:1024]),
                ["OX1", "OX0"], ["OX0"])
            sop(lambda s: s.activation(ROWSF[:, T:2 * T], OX[0:16, 0:512],
                                       AF.Sqrt), ["OX0", "ROWSF"], ["ROWSF"])
            vop(lambda s: s.tensor_tensor(ROWS[:, 0:T], ROWSF[:, 0:T],
                                          ROWSF[:, T:2 * T], ALU.mult),
                ["ROWSF", "ROWS"], ["ROWS"])


        def extract_row(i, pack_off, dst_off):
            mm(SC[0:1, 0:T], IDB[0:16, i:i + 1],
               ROWS[0:16, pack_off:pack_off + T], True, True,
               ["IDB", "ROWS", "sc0"], ["sc0"])
            vop(lambda s, d=dst_off: s.tensor_copy(
                RXT[0:1, d:d + T], SC[0:1, 0:T]), ["sc0", "RXT"], ["RXT"])

        def extract_row_f(i, packf_off, dst_off):
            mm(SC[0:1, 0:T], IDF[0:16, i:i + 1],
               ROWSF[0:16, packf_off:packf_off + T], True, True,
               ["IDB", "ROWSF", "sc0"], ["sc0"])
            vop(lambda s, d=dst_off: s.tensor_copy(
                RXT[0:1, d:d + T], SC[0:1, 0:T]), ["sc0", "RXT"], ["RXT"])

        def bcast_row(dst_reg, dst_key, src_off):
            mm(dst_reg, ON16[0:1, :], RXT[0:1, src_off:src_off + T],
               True, True, ["ON16", "RXT", dst_key], [dst_key])

        def make_xs(i, xsrc, tag=None):
            rd = ["XBF"] + ([f"{tag}@{i}"] if tag else [])
            dma("G", XBF[:, :],
                bass.AP(xsrc, i * T, [[S, 128], [128 * S, 8], [1, T]]),
                rd, ["XBF"])
            extract_row_f(i, T, T)
            extract_row(i, 0, 0)
            bcast_row(SC[:, 0:T], "sc0", T)
            for dc in range(C):
                vop(lambda s, dc=dc: s.tensor_tensor(
                    XS[:, T * dc:T * dc + T], XBF[:, T * dc:T * dc + T],
                    SC[:, 0:T], ALU.mult), ["XBF", "sc0", "XS"], ["XS"])

        def proj_ln(i, wbase, ncs_j, evac):
            for oc in range(C):
                breg, bkey = (PA, "pa") if oc % 2 == 0 else (PB, "pb")
                for dc in range(C):
                    mm(breg, WA[:, wbase + 1024 * dc + 128 * oc:
                               wbase + 1024 * dc + 128 * oc + 128],
                       XS[:, T * dc:T * dc + T],
                       dc == 0, False, ["WA", "XS"], [bkey])
                mm(breg, NCS[0:1, 1024 * ncs_j + 128 * oc:
                             1024 * ncs_j + 128 * oc + 128],
                   RXT[0:1, 0:T], False, True, ["NCS", "RXT"], [bkey])
                evac(oc, breg, bkey)

        def proj_plain(wbase, rhs_buf, rhs_key, evac):
            for oc in range(C):
                breg, bkey = (PA, "pa") if oc % 2 == 0 else (PB, "pb")
                for dc in range(C):
                    mm(breg, WA[:, wbase + 1024 * dc + 128 * oc:
                               wbase + 1024 * dc + 128 * oc + 128],
                       rhs_buf[:, T * dc:T * dc + T],
                       dc == 0, dc == C - 1, ["WA", rhs_key], [bkey])
                evac(oc, breg, bkey)

        def evac_resid(xsrc_res, dst_dram, bias_slot, dst_tag, src_tag=None):
            def f_for(i):
                def f(oc, breg, bkey):
                    sl = oc % 2
                    xf = XF[:, 1024 * sl:1024 * sl + 512]
                    xkey = f"XFH{sl}"
                    rdx = [xkey] + ([f"{src_tag}@{i}"] if src_tag else [])
                    dma("Y", xf,
                        bass.AP(xsrc_res, oc * 128 * S + i * T,
                                [[S, 128], [1, T]]), rdx, [xkey])
                    ox = OX[:, 512 * sl:512 * sl + 512]
                    okey = f"OX{sl}"
                    if bias_slot is not None:
                        vop(lambda s, b=breg, x=xf, o=ox, oc=oc:
                            s.scalar_tensor_tensor(
                                o, b, BIA[:, 8 * bias_slot + oc:
                                          8 * bias_slot + oc + 1], x,
                                ALU.add, ALU.add),
                            [bkey, xkey, "BIA", okey], [okey])
                    else:
                        vop(lambda s, b=breg, x=xf, o=ox:
                            s.tensor_tensor(o, b, x, ALU.add),
                            [bkey, xkey, okey], [okey])
                    dma("Y", bass.AP(dst_dram, oc * 128 * S + i * T,
                                     [[S, 128], [1, T]]), ox, [okey],
                        [okey, f"{dst_tag}@{i}"])
                return f
            return f_for

        if not SKIP123:
            sweep_stats(xfm)
            load_weights([l0w["q"], l0w["k"], l0w["v"], l0w["o"]])
            build_negcs([(WA, 0), (WA, 8192), (WA, 16384)])

        def evac_q(oc, breg, bkey):
            sop(lambda s, oc=oc, b=breg: s.activation(
                QR0[:, 512 * oc:512 * oc + 512], b, AF.Identity,
                bias=BIA[:, oc:oc + 1], scale=0.125),
                [bkey, "BIA", "QR0"], ["QR0"])

        def evac_k(i):
            def f(oc, breg, bkey):
                slots = [(4 * i + m) % NSL for m in range(4)]
                runs = [[0]]
                for m in range(1, 4):
                    if slots[m] == slots[m - 1] + 1:
                        runs[-1].append(m)
                    else:
                        runs.append([m])
                for run in runs:
                    m0, ln = run[0], len(run)
                    dst = bass.AP(KR, (slots[m0] * 8 + oc) * 128,
                                  [[NSL * 1024, 128], [1024, ln], [1, 128]])
                    src = bass.AP(PQ, (0 if bkey == "pa" else 512) + m0 * 128,
                                  [[1024, 128], [128, ln], [1, 128]])
                    sop(lambda s, d=dst, r=src, oc=oc: s.activation(
                        d, r, AF.Identity, bias=BIA[:, 8 + oc:9 + oc]),
                        [bkey, "BIA", "KR"], ["KR"])
            return f

        def evac_v(oc, breg, bkey):
            sop(lambda s, oc=oc, b=breg: s.activation(
                XBF[:, 512 * oc:512 * oc + 512], b, AF.Identity,
                bias=BIA[:, 16 + oc:17 + oc]), [bkey, "BIA", "XBF"], ["XBF"])

        def vt_build(i, src_buf, src_key, nsl=NSL):
            for m in range(4):
                sl = (4 * i + m) % nsl
                for dc in range(C):
                    tpose(PT[:, 128 * dc:128 * dc + 128],
                          src_buf[:, 512 * dc + 128 * m:
                                  512 * dc + 128 * m + 128],
                          [src_key], ["pt"] if dc == C - 1 else ["pt_"])
                dst = bass.AP(VT, sl * (H * 65),
                              [[NSL * H * 65, 128], [65, H], [1, 64]])
                vop(lambda s, d=dst: s.tensor_copy(
                    d, bass.AP(PT, 0, [[1024, 128], [64, 16], [1, 64]])),
                    ["pt", "pt_", "VT"], ["VT"])

        def attention(j):
            for ql in range(4):
                qb = 4 * j + ql
                variant = 0 if qb == 0 else (2 if qb == NKB - 1 else 1)
                if qb == 0:
                    for par in range(2):
                        vop(lambda s, par=par: s.memset(
                            SC[:, 512 * par:512 * par + 128], 0.0),
                            [f"sc{par}"], [f"sc{par}"])
                if qb == NKB - 1:
                    for par in range(2):
                        vop(lambda s, par=par: s.memset(
                            SC[:, 512 * par + 256:512 * par + 384], 0.0),
                            [f"sc{par}"], [f"sc{par}"])

                def scores(h):
                    par = h % 2
                    po = (h % 2) * 64
                    hc = h // 2
                    for jj, kb in enumerate((qb - 1, qb, qb + 1)):
                        if 0 <= kb < NKB:
                            sl = kb % NSL
                            lh = bass.AP(KR, (sl * 8 + hc) * 128
                                         + po * (NSL * 1024),
                                         [[NSL * 1024, 64], [1, 128]])
                            rh = bass.AP(QR0, hc * 512 + ql * 128 + po * 4096,
                                         [[4096, 64], [1, 128]])
                            mm(SC[:, 512 * par + 128 * jj:
                                  512 * par + 128 * jj + 128], lh, rh,
                               True, True, ["KR", "QR0"], [f"sc{par}"])

                def attend(h):
                    par = h % 2
                    avreg, avkey = ((AVQ[:, 0:512], "av0") if h % 2 == 0
                                    else (AVQ[:, 512:1024], "av1"))
                    sop(lambda s, par=par: s.activation(
                        WMT[:, :], SC[:, 512 * par:512 * par + 384],
                        AF.Exp, bias=CST[:, 0:1]),
                        [f"sc{par}", "WMT", "CST"], ["WMT"])
                    vop(lambda s, v=variant: s.tensor_tensor(
                        WMT2[:, :], WMT[:, :], MSK[:, 384 * v:384 * v + 384],
                        ALU.mult), ["WMT", "MSK", "WMT2"], ["WMT2"])
                    for jj, kb in enumerate((qb - 1, qb, qb + 1)):
                        sl = kb % NSL
                        rh = bass.AP(VT, sl * (H * 65) + h * 65,
                                     [[NSL * H * 65, 128], [1, 65]])
                        mm(avreg[:, 0:65], WMT2[:, 128 * jj:128 * jj + 128],
                           rh, jj == 0, jj == 2, ["WMT2", "VT"], [avkey])
                    vop(lambda s, h=h, a=avreg: s.tensor_copy(
                        RST[:, h:h + 1], a[:, 64:65]), [avkey, "RST"],
                        ["RST"])
                    vop(lambda s, h=h, a=avreg: s.tensor_copy(
                        AOtm[:, 64 * h:64 * h + 64], a[:, 0:64]),
                        [avkey, "AOtm"], ["AOtm"])

                for h in range(H + 1):
                    if h < H:
                        scores(h)
                    if h >= 1:
                        attend(h - 1)
                vop(lambda s: s.reciprocal(RSI[:, :], RST[:, :]),
                    ["RST", "RSI"], ["RSI"], small=["RST"])
                for h in range(H):
                    vop(lambda s, h=h: s.tensor_scalar(
                        AOtm[:, 64 * h:64 * h + 64],
                        AOtm[:, 64 * h:64 * h + 64], RSI[:, h:h + 1], None,
                        ALU.mult), ["AOtm", "RSI"], ["AOtm"], small=["RSI"])
                for dc in range(C):
                    tpose(PT[:, 128 * dc:128 * dc + 128],
                          AOtm[:, 128 * dc:128 * dc + 128], ["AOtm"],
                          ["pt"] if dc == C - 1 else ["pt_"])
                vop(lambda s, ql=ql: s.tensor_copy(
                    bass.AP(AO, ql * 128, [[4096, 128], [512, 8], [1, 128]]),
                    PT[:, :]), ["pt", "pt_", "AO"], ["AO"])

        evac_x1 = evac_resid(xfm, x1d, 3, "x1")
        for i in range(0 if SKIP123 else NT + 1):
            if i < NT:
                make_xs(i, xfm)
                proj_ln(i, 8192, 1, evac_k(i))
                proj_ln(i, 16384, 2, evac_v)
                vt_build(i, XBF, "XBF")
            if i >= 1:
                attention(i - 1)
                proj_plain(24576, AO, "AO", evac_x1(i - 1))
            if i < NT:
                proj_ln(i, 0, 0, evac_q)

        if PHASES >= 2 and not SKIP123:
            load_weights([l1w["q"], l1w["v"], l1w["g"], l1w["o"]])
            build_negcs([(WB, 0)])
            sweep_stats(x1d, "x1")
            evac_x2 = evac_resid(x1d, x2d, 4, "x2", "x1")

            def proj_wf(i, evac):
                for oc in range(C):
                    breg, bkey = (PA, "pa") if oc % 2 == 0 else (PB, "pb")
                    for dc in range(C):
                        mm(breg, WB[:, 1024 * dc + 128 * oc:
                                   1024 * dc + 128 * oc + 128],
                           XS[:, T * dc:T * dc + T],
                           dc == 0, False, ["WB", "XS"], [bkey])
                    mm(breg, NCS[0:1, 128 * oc:128 * oc + 128],
                       RXT[0:1, 0:T], False, True, ["NCS", "RXT"], [bkey])
                    evac(oc, breg, bkey)

            for i in range(NT):
                make_xs(i, x1d, "x1")
                proj_wf(i, evac_x2(i))

        if PHASES >= 3 and not SKIP123:
            build_negcs([(WA, 0), (WA, 8192), (WA, 16384)])
            sweep_stats(x2d, "x2")

            def evac_qr0(oc, breg, bkey):
                sop(lambda s, oc=oc, b=breg: s.activation(
                    QR0[:, 512 * oc:512 * oc + 512], b, AF.Identity),
                    [bkey, "QR0"], ["QR0"])

            def evac_gate(i):
                def f(oc, breg, bkey):
                    sop(lambda s, oc=oc, b=breg: s.activation(
                        AO[:, 512 * oc:512 * oc + 512], b, AF.Silu),
                        [bkey, "AO"], ["AO"])
                    dma("G", bass.AP(gscr, oc * 128 * S + i * T,
                                     [[S, 128], [1, T]]),
                        AO[:, 512 * oc:512 * oc + 512], ["AO"],
                        ["AO", f"g@{i}"])
                return f

            def evac_v3(oc, breg, bkey):
                sop(lambda s, oc=oc, b=breg: s.activation(
                    XBF[:, 512 * oc:512 * oc + 512], b, AF.Identity),
                    [bkey, "XBF"], ["XBF"])

            for i in range(NT):
                make_xs(i, x2d, "x2")
                proj_ln(i, 0, 0, evac_qr0)
                proj_ln(i, 16384, 2, evac_gate(i))
                proj_ln(i, 8192, 1, evac_v3)
                for tb in range(4 if P3RET else 0):
                    gtb = 4 * i + tb
                    STc = ST0 if gtb % 2 == 0 else ST1
                    STp = ST1 if gtb % 2 == 0 else ST0
                    ckey = f"ST{gtb % 2}"
                    pkey = f"ST{(gtb + 1) % 2}"
                    slv = tb % 2
                    for dc in range(C):
                        tpose(PT[:, 128 * dc:128 * dc + 128],
                              XBF[:, 512 * dc + 128 * tb:
                                  512 * dc + 128 * tb + 128], ["XBF"],
                              ["pt"] if dc == C - 1 else ["pt_"])
                    vop(lambda s, slv=slv: s.tensor_copy(
                        KR[:, slv * 1024:slv * 1024 + 1024], PT[:, :]),
                        ["pt", "pt_", "KR"], ["KR"])
                    for c4 in range(8):
                        mm(SC[:, 0:128],
                           KR[:, slv * 1024 + 128 * c4:
                              slv * 1024 + 128 * c4 + 128],
                           LMAT[:, 128 * (2 * c4):128 * (2 * c4) + 128],
                           True, True, ["KR", "LMAT"], ["sc0"])
                        mm(SC[:, 512:640],
                           KR[:, slv * 1024 + 128 * c4:
                              slv * 1024 + 128 * c4 + 128],
                           LMAT[:, 128 * (2 * c4 + 1):
                                128 * (2 * c4 + 1) + 128],
                           True, True, ["KR", "LMAT"], ["sc1"])
                        vop(lambda s, c=c4, sp=STp, sc_=STc:
                            s.scalar_tensor_tensor(
                                sc_[0:64, 128 * c:128 * c + 128],
                                PWR[0:64, 128 * c:128 * c + 128],
                                sp[0:64, 128 * c + 127:128 * c + 128],
                                SC[0:64, 0:128], ALU.mult, ALU.add),
                            ["PWR", pkey, "sc0", ckey], [ckey])
                        vop(lambda s, c=c4, sp=STp, sc_=STc:
                            s.scalar_tensor_tensor(
                                sc_[64:128, 128 * c:128 * c + 128],
                                PWR[64:128, 128 * c:128 * c + 128],
                                sp[64:128, 128 * c + 127:128 * c + 128],
                                SC[64:128, 512:640], ALU.mult, ALU.add),
                            ["PWR", pkey, "sc1", ckey], [ckey])
                        vop(lambda s, c=c4, sc_=STc, tb=tb:
                            s.tensor_tensor(
                                AO[:, 512 * c + 128 * tb:
                                   512 * c + 128 * tb + 128],
                                sc_[:, 128 * c:128 * c + 128],
                                QR0[:, 512 * c + 128 * tb:
                                    512 * c + 128 * tb + 128], ALU.mult),
                            [ckey, "QR0", "AO"], ["AO"])
                vop(lambda s: s.tensor_tensor(XS[:, :], AO[:, :], AO[:, :],
                                              ALU.mult), ["AO", "XS"], ["XS"])
                for dc in range(C):
                    mm(AVQ[0:16, 0:512], OH[:, 16 * i:16 * i + 16],
                       AO[:, T * dc:T * dc + T], i == 0 and dc == 0,
                       i == NT - 1 and dc == C - 1, ["AO", "OH"], ["av0"])
                for dc in range(C):
                    mm(AVQ[0:16, 512:1024], OH[:, 16 * i:16 * i + 16],
                       XS[:, T * dc:T * dc + T], i == 0 and dc == 0,
                       i == NT - 1 and dc == C - 1, ["XS", "OH"], ["av1"])
                dma("G", bass.AP(yscr, i * T,
                                 [[S, 128], [128 * S, 8], [1, T]]),
                    AO[:, :], ["AO"], [f"y@{i}"])

            rowmath("av0", AVQ[0:16, 0:512], "av1", AVQ[0:16, 512:1024])
            evac_x3 = evac_resid(x2d, x3d, None, "x3", "x2")
            for i in range(NT if P3SWC else 0):
                dma("G", AO[:, :],
                    bass.AP(yscr, i * T, [[S, 128], [128 * S, 8], [1, T]]),
                    ["AO", f"y@{i}"], ["AO"])
                dma("G", QR0[:, :],
                    bass.AP(gscr, i * T, [[S, 128], [128 * S, 8], [1, T]]),
                    ["QR0", f"g@{i}"], ["QR0"])
                extract_row_f(i, 0, 0)
                extract_row_f(i, T, T)
                bcast_row(SC[:, 0:T], "sc0", 0)
                bcast_row(SC[:, 512:512 + T], "sc1", T)
                for dc in range(C):
                    vop(lambda s, dc=dc: s.tensor_tensor(
                        XBF[:, T * dc:T * dc + T], AO[:, T * dc:T * dc + T],
                        SC[:, 0:T], ALU.add), ["AO", "sc0", "XBF"], ["XBF"])
                    vop(lambda s, dc=dc: s.scalar_tensor_tensor(
                        XS[:, T * dc:T * dc + T], XBF[:, T * dc:T * dc + T],
                        BIA[:, 40 + dc:41 + dc], SC[:, 512:512 + T],
                        ALU.mult, ALU.mult), ["XBF", "BIA", "sc1", "XS"],
                        ["XS"])
                    vop(lambda s, dc=dc: s.scalar_tensor_tensor(
                        XBF[:, T * dc:T * dc + T], XS[:, T * dc:T * dc + T],
                        BIA[:, 48 + dc:49 + dc],
                        QR0[:, T * dc:T * dc + T], ALU.add, ALU.mult),
                        ["XS", "BIA", "QR0", "XBF"], ["XBF"])
                proj_plain(24576, XBF, "XBF", evac_x3(i))

        if PHASES >= 4:
            for oc in range(8 if P4A >= 1 else 0):
                dma("Y", UF[:, 0:1024],
                    bass.AP(l1rhsT, oc * 128 * 1024, [[1024, 128], [1, 1024]]),
                    ["UF0"], ["UF0"])
                vop(lambda s: s.reduce_max(SMC[:, 0:1], UF[:, 0:1024],
                                           axis=AX.X,
                                           apply_absolute_value=True),
                    ["UF0", "SMC"], ["SMC"])
                vop(lambda s, oc=oc: s.tensor_scalar(
                    SMC2[:, oc:oc + 1], SMC[:, 0:1], 1e-12, 1.0 / 127.0,
                    ALU.max, ALU.mult), ["SMC", "SMC2"], ["SMC2"],
                    small=["SMC"])
                vop(lambda s, oc=oc: s.reciprocal(SMC[:, 1:2],
                                                  SMC2[:, oc:oc + 1]),
                    ["SMC2", "SMC"], ["SMC"], small=["SMC2"])
                vop(lambda s: s.tensor_scalar(
                    OX[:, 0:1024], UF[:, 0:1024], SMC[:, 1:2], 128.5,
                    ALU.mult, ALU.add), ["UF0", "SMC", "OX0", "OX1"],
                    ["OX0", "OX1"], small=["SMC"])
                for lb in range(8):
                    b = float(128 >> lb)
                    src, sk = (OX[:, 0:1024], ["OX0", "OX1"]) if lb == 0 \
                        else (UF[:, 0:1024], ["UF0"])
                    vop(lambda s, b=b, src=src: s.tensor_scalar(
                        ST1[:, :], src, b, -b, ALU.is_ge, ALU.mult),
                        sk + ["ST1"], ["ST1"])
                    vop(lambda s, src=src: s.tensor_tensor(
                        UF[:, 0:1024], src, ST1[:, :], ALU.add),
                        sk + ["ST1", "UF0"], ["UF0"])
                dst = XS if oc < 4 else XBF
                dk = "XS" if oc < 4 else "XBF"
                cb = (oc % 4) * 1024
                vop(lambda s, d=dst, cb=cb: s.scalar_tensor_tensor(
                    d[:, cb:cb + 1024], OX[:, 0:1024], 128.0, UF[:, 0:1024],
                    ALU.subtract, ALU.subtract),
                    ["OX0", "OX1", "UF0", dk], [dk])
            if P4A >= 2:
                tpose(PQ[0:8, 0:128], SMC2[:, 0:8], ["SMC2"], ["pa"],
                      f32=True)
                vop(lambda s: s.tensor_copy(ROWSF[0:8, 0:128],
                                            PQ[0:8, 0:128]),
                    ["pa", "ROWSF"], ["ROWSF"])
            for oc in range(8 if P4A >= 2 else 0):
                mm(PQ[0:1, 512:640], IDF[0:8, oc:oc + 1],
                   ROWSF[0:8, 0:128], True, True, ["IDB", "ROWSF"], ["pb"])
                vop(lambda s: s.tensor_copy(RXF[0:1, 0:128],
                                            PQ[0:1, 512:640]),
                    ["pb", "RXF"], ["RXF"])
                mm(PQ[:, 512:640], ON16F[0:1, :], RXF[0:1, 0:128],
                   True, True, ["ON16F", "RXF"], ["pb"])
                vop(lambda s, oc=oc: s.tensor_copy(
                    RSB[:, 128 * oc:128 * oc + 128], PQ[:, 512:640]),
                    ["pb", "RSB"], ["RSB"])
            for oc in range(8 if P4A >= 3 else 0):
                src_t = XS if oc < 4 else XBF
                sk = "XS" if oc < 4 else "XBF"
                cb = (oc % 4) * 1024
                for kc in range(8):
                    tpose(PT[:, 128 * kc:128 * kc + 128],
                          src_t[:, cb + 128 * kc:cb + 128 * kc + 128],
                          [sk], ["pt"] if kc == C - 1 else ["pt_"])
                vop(lambda s, oc=oc: s.tensor_copy(
                    bass.AP(WB, 128 * oc, [[8192, 128], [1024, 8], [1, 128]]),
                    PT[:, :]), ["pt", "pt_", "WB"], ["WB"])

            if P4B:
                sweep_stats(x3d, "x3")
            for pk, (roff, cbase) in enumerate([(0, 0), (T, 64)]
                                               if P4B else []):
                for q4 in range(4):
                    tpose(PQ[:, 0:16],
                          ROWSF[0:16, roff + 128 * q4:roff + 128 * q4 + 128],
                          ["ROWSF"], ["pa"], f32=True)
                    vop(lambda s, q4=q4, cb=cbase: s.tensor_copy(
                        bass.AP(COLS, cb + q4, [[128, 128], [4, 16]]),
                        PQ[:, 0:16]), ["pa", "COLS"], ["COLS"])

            def p4_parity(tb):
                par = tb % 2
                ufr, ufk = (UF[:, 0:1024], "UF0") if par == 0 \
                    else (PWR[:, :], "PWR")
                smc, smk = (SMC, "SMC") if par == 0 else (SMC2, "SMC2")
                return (par, ufr, ufk, smc, smk, f"XFH{par}",
                        XBF[:, 1024 * par:1024 * par + 1024], f"XBF{par}",
                        XS[:, 1024 * par:1024 * par + 1024], f"XS{par}")

            def p4_stageA(tb):
                par, ufr, ufk, smc, smk, xfk, xbfr, xbk, xsr, xsk = \
                    p4_parity(tb)
                xfr = XF[:, 1024 * par:1024 * par + 1024]
                dma("Y", xfr,
                    bass.AP(x3d, tb * 128,
                            [[S, 128], [128 * S, 8], [1, 128]]),
                    [xfk, f"x3@{tb // 4}"], [xfk])
                for dc in range(C):
                    tpose(PQ[:, 128 * dc:128 * dc + 128],
                          xfr[:, 128 * dc:128 * dc + 128], [xfk],
                          ["pa"] if dc < 4 else ["pb"], f32=True)
                sop(lambda s, tb=tb, ufr=ufr: s.activation(
                    ufr, PQ[:, :], AF.Identity,
                    bias=COLS[:, tb:tb + 1]),
                    ["pa", "pb", "COLS", ufk], [ufk])
                vop(lambda s, smc=smc, ufr=ufr: s.reduce_max(
                    smc[:, 0:1], ufr, axis=AX.X,
                    apply_absolute_value=True), [ufk, smk], [smk])
                vop(lambda s, tb=tb, smc=smc: s.tensor_scalar(
                    smc[:, 1:2], smc[:, 0:1], COLS[:, 64 + tb:65 + tb],
                    None, ALU.mult), [smk, "COLS"], [smk], small=[smk])
                vop(lambda s, smc=smc: s.tensor_scalar(
                    smc[:, 2:3], smc[:, 1:2], 1e-12, 1.0 / 127.0, ALU.max,
                    ALU.mult), [smk, f"LSC{par}"], [f"LSC{par}"],
                    small=[smk])
                vop(lambda s, smc=smc: s.reciprocal(smc[:, 3:4],
                                                    smc[:, 0:1]),
                    [smk, f"RCQ{par}"], [f"RCQ{par}"], small=[smk])
                vop(lambda s, smc=smc: s.tensor_scalar(
                    smc[:, 4:5], smc[:, 3:4], 127.0, None, ALU.mult),
                    [f"RCQ{par}", f"RCQ2{par}"], [f"RCQ2{par}"],
                    small=[f"RCQ{par}"])
                sop(lambda s, smc=smc, ufr=ufr: s.activation(
                    OX[:, 0:1024], ufr, AF.Identity,
                    bias=CST[:, 1:2], scale=smc[:, 4:5]),
                    [ufk, f"RCQ2{par}", "OX0", "OX1", "CST"],
                    ["OX0", "OX1"], small=[f"RCQ2{par}"])
                for lb in range(8):
                    b = float(128 >> lb)
                    src, sk = (OX[:, 0:1024], ["OX0", "OX1"]) if lb == 0 \
                        else (ST1[:, :], ["ST1"])
                    vop(lambda s, b=b, src=src: s.tensor_scalar(
                        ST0[:, :], src, b, -b, ALU.is_ge, ALU.mult),
                        sk + ["ST0"], ["ST0"])
                    vop(lambda s, src=src: s.tensor_tensor(
                        ST1[:, :], src, ST0[:, :], ALU.add),
                        sk + ["ST0", "ST1"], ["ST1"])
                vop(lambda s, xbfr=xbfr: s.scalar_tensor_tensor(
                    xbfr, OX[:, 0:1024], 128.0, ST1[:, :],
                    ALU.subtract, ALU.subtract),
                    ["OX0", "OX1", "ST1", xbk], [xbk])

            def p4_stageA2(tb):
                par, ufr, ufk, smc, smk, xfk, xbfr, xbk, xsr, xsk = \
                    p4_parity(tb)
                for kc in range(C):
                    tpose(PT[:, 128 * kc:128 * kc + 128],
                          xbfr[:, 128 * kc:128 * kc + 128], [xbk],
                          ["pt"] if kc == C - 1 else ["pt_"])
                sop(lambda s, xsr=xsr: s.activation(xsr, PT[:, :],
                                                    AF.Identity),
                    ["pt", "pt_", xsk], [xsk])

            def p4_stageB(tb):
                par, ufr, ufk, smc, smk, xfk, xbfr, xbk, xsr, xsk = \
                    p4_parity(tb)
                for half in range(2):
                    sck = f"sc{half}"
                    for kc in range(C):
                        mm(SC[:, 512 * half:512 * half + 512],
                           xsr[:, 128 * kc:128 * kc + 128],
                           WB[:, 1024 * kc + 512 * half:
                              1024 * kc + 512 * half + 512],
                           kc == 0, kc == C - 1, [xsk, "WB"], [sck])
                for half in range(2):
                    sck = f"sc{half}"
                    stg = XF[:, 1024 * par + 512 * half:
                             1024 * par + 512 * half + 512]
                    okey = f"OF{par}{half}"
                    vop(lambda s, half=half, smc=smc, stg=stg:
                        s.scalar_tensor_tensor(
                            stg, RSB[:, 512 * half:512 * half + 512],
                            smc[:, 2:3],
                            SC[:, 512 * half:512 * half + 512], ALU.mult,
                            ALU.mult),
                        ["RSB", f"LSC{par}", sck, okey, xfk], [okey],
                        small=[f"LSC{par}"])
                    vop(lambda s, half=half, tb=tb, ufr=ufr, stg=stg:
                        s.scalar_tensor_tensor(
                            stg, ufr[:, 512 * half:512 * half + 512],
                            COLS[:, tb:tb + 1], stg,
                            ALU.subtract, ALU.add),
                        [ufk, "COLS", okey], [okey])
                    dma("Y", bass.AP(ofm, tb * 128 * D + 512 * half,
                                     [[1024, 128], [1, 512]]),
                        stg, [okey, xfk], [okey])

            for tb in range(NKB + 1 if P4C else 0):
                if tb < NKB:
                    p4_stageA(tb)
                if tb >= 1:
                    p4_stageB(tb - 1)
                if tb < NKB:
                    p4_stageA2(tb)

        p.finish()

        with nc.Block() as block:
            @block.tensor
            def _(sec):
                for f in p.ops["T"]:
                    f(sec)

            @block.vector
            def _(sec):
                for f in p.ops["V"]:
                    f(sec)

            @block.scalar
            def _(sec):
                for f in p.ops["S"]:
                    f(sec)

            @block.sync
            def _(sec):
                for f in p.ops["Y"]:
                    f(sec)

            @block.gpsimd
            def _(sec):
                for f in p.ops["G"]:
                    f(sec)

    return nc


_CONSTS = None


def host_consts():
    global _CONSTS
    if _CONSTS is not None:
        return _CONSTS
    import ml_dtypes
    bf = ml_dtypes.bfloat16
    ident = np.eye(128, dtype=bf)
    oh = np.zeros((128, 256), dtype=bf)
    for i in range(16):
        oh[16 * 0:, 16 * i + i] = 0
    for i in range(16):
        oh[:, 16 * i + i] = 1
    ones16 = np.ones((16, 128), dtype=bf)
    ones16f = np.ones((16, 128), dtype=np.float32)
    k = np.arange(128)[:, None]
    q = np.arange(128)[None, :]
    mL = (k >= q).astype(bf)
    mR = (k <= q).astype(bf)
    mid = np.ones((128, 128), dtype=bf)
    zero = np.zeros((128, 128), dtype=bf)
    masks = np.concatenate([zero, mid, mR, mL, mid, mR, mL, mid, zero],
                           axis=1)
    s_ = np.arange(128)[:, None]
    t_ = np.arange(128)[None, :]
    diota = np.broadcast_to((t_ - s_).astype(np.float32), (128, 128))
    dnegb = np.where(t_ >= s_, 0.0, -1e9).astype(np.float32)
    dp1 = np.broadcast_to((t_ + 1).astype(np.float32), (128, 128))
    dtabs = np.ascontiguousarray(
        np.concatenate([diota, dnegb, dp1], axis=1).astype(np.float32))
    _CONSTS = {
        "identb": ident, "identf": np.eye(128, dtype=np.float32),
        "oh16": oh, "ones16": ones16, "ones16f": ones16f,
        "masks": np.ascontiguousarray(masks), "dtabs": dtabs,
    }
    return _CONSTS


def host_inputs(inputs, core):
    x = np.asarray(inputs["L0_attn_x"][core])
    cc = np.ascontiguousarray
    d = {
        "xfm": cc(x.T),
        "l0wq": cc(np.asarray(inputs["L0_attn_Wq"]).T),
        "l0wk": cc(np.asarray(inputs["L0_attn_Wk"]).T),
        "l0wv": cc(np.asarray(inputs["L0_attn_Wv"]).T),
        "l0wo": cc(np.asarray(inputs["L0_attn_Wo"]).T),
        "l0bq": np.asarray(inputs["L0_attn_bq"]),
        "l0bk": np.asarray(inputs["L0_attn_bk"]),
        "l0bv": np.asarray(inputs["L0_attn_bv"]),
        "l0bo": np.asarray(inputs["L0_attn_bo"]),
        "g1r": cc(np.asarray(inputs["L0_ffn_G1_R"])[0].transpose(1, 0, 2)
                  .reshape(4, 1024)),
        "g2r": cc(np.asarray(inputs["L0_ffn_G2_R"])[:, :, 0, :]
                  .reshape(4, 256)),
        "g1i": cc(np.asarray(inputs["L0_ffn_G1_I"])[0].transpose(1, 0, 2)
                  .reshape(4, 1024)),
        "g2i": cc(np.asarray(inputs["L0_ffn_G2_I"])[:, :, 0, :]
                  .reshape(4, 256)),
        "l0fb": cc(np.concatenate([np.asarray(inputs["L0_ffn_bias_real"]),
                                   np.asarray(inputs["L0_ffn_bias_imag"])])),
        "l1wq": cc(np.asarray(inputs["L1_attn_Wq"]).T),
        "l1wv": cc(np.asarray(inputs["L1_attn_Wv"]).T),
        "l1wg": cc(np.asarray(inputs["L1_attn_Wg"]).T),
        "l1wo": cc(np.asarray(inputs["L1_attn_Wo"]).T),
        "l1lng": np.asarray(inputs["L1_attn_ln_g"]),
        "l1lnb": np.asarray(inputs["L1_attn_ln_b"]),
        "l1decay": cc(np.asarray(inputs["L1_attn_decay"])[:, 1, 0]),
        "l1rhsT": cc(np.asarray(inputs["L1_ffn_rhs"]).T),
    }
    d.update(host_consts())
    return d


def kernel(**inputs):
    nc = build()
    in_maps = [host_inputs(inputs, core) for core in range(8)]
    res = run_bass_kernel_spmd(nc, in_maps, core_ids=list(range(8)))
    outs = []
    for core in range(8):
        r = res.results[core]
        if PHASES == 4:
            outs.append(np.asarray(r["ofm"]))
        else:
            key = {1: "x1d", 2: "x2d", 3: "x3d"}[PHASES]
            outs.append(np.asarray(r[key]).T)
    return np.stack(outs, axis=0)
